# version 14
# baseline (speedup 1.0000x reference)
import numpy as np
from contextlib import ExitStack

import concourse.bass as bass
import concourse.mybir as mybir
from concourse.bass_utils import run_bass_kernel_spmd
from concourse.alu_op_type import AluOpType as ALU

F32 = mybir.dt.float32
BF16 = mybir.dt.bfloat16
I32 = mybir.dt.int32
AF = mybir.ActivationFunctionType
AX = mybir.AxisListType

DM = 1024
NH = 16
HD = 64
NT = 64
NU = 16
NSLOT = 16
TOK = 2048
NE = 8
HID = 4096
NTILE = 24
NG = 4
GH = 4
POLY = [-0.037, 0.481, -2.3, 4.917, -1.791]
NEG = -1.0e30


class Emitter:

    def __init__(self, nc):
        self.nc = nc

    def _reset(self):
        self.cnt = dict(sync=0, gpsimd=0, tensor=0, vector=0, scalar=0)
        self.waits = []
        self.snaps = {}
        self.wi = 0

    CHAINED = ("vector", "scalar", "gpsimd")

    def op(self, eng, fn, tick=False):
        chained = eng in self.CHAINED
        if self.mode == "count":
            if tick or chained:
                self.cnt[eng] += 1
            return
        if self.mode == eng:
            if chained:
                n = self.ecnt.get(eng, 0)
                if n > 0:
                    self.eng.wait_ge(self.sems[eng], n)
                self.ecnt[eng] = n + 1
            inst = fn(self.eng)
            if tick or chained:
                inst.then_inc(self.sems[eng], 1)

    def dma(self, queue, stream, fn):
        key = "d:" + stream
        if self.mode == "count":
            self.cnt[key] = self.cnt.get(key, 0) + 16
            return
        if self.mode == queue:
            fn(self.eng).then_inc(self.sems[key], 16)

    def wait(self, waiter, src):
        if self.mode == "count":
            self.waits.append((src, self.cnt.get(src, 0)))
            return
        src_, val = self.waits[self.wi]
        self.wi += 1
        if self.mode == waiter and val > 0:
            self.eng.wait_ge(self.sems[src_], val)

    def snap(self, key, src):
        if self.mode == "count":
            self.snaps.setdefault(key, []).append(self.cnt.get(src, 0))

    def wait_snap(self, waiter, src, key, idx):
        if idx < 0:
            return
        if self.mode == "count":
            self.waits.append((src, ("s", key, idx)))
            return
        src_, val = self.waits[self.wi]
        self.wi += 1
        if isinstance(val, tuple):
            val = self.snaps[val[1]][val[2]]
        if self.mode == waiter and val > 0:
            self.eng.wait_ge(self.sems[src_], val)

    def run(self, name, phase_fn):
        nc = self.nc
        self.mode = "count"
        self._reset()
        phase_fn(self)
        self.waits = [(s, self.snaps[v[1]][v[2]] if isinstance(v, tuple) else v)
                      for (s, v) in self.waits]
        with ExitStack() as es:
            keys = ["sync", "gpsimd", "tensor", "vector", "scalar"]
            keys += [k for k in self.cnt if k.startswith("d:")]
            sems = {k: es.enter_context(nc.semaphore(f"{name}_{k}".replace(":", "_")))
                    for k in keys}
            block = es.enter_context(nc.Block(name))

            def mk(eng_name):
                def body(eng_obj):
                    self.mode = eng_name
                    self.eng = eng_obj
                    self.sems = sems
                    self.wi = 0
                    self.ecnt = {}
                    phase_fn(self)
                return body

            block.sync(mk("sync"))
            block.gpsimd(mk("gpsimd"))
            block.tensor(mk("tensor"))
            block.vector(mk("vector"))
            block.scalar(mk("scalar"))


def build_nc():
    nc = bass.Bass("TRN2", target_bir_lowering=False)

    xe = nc.dram_tensor("xe", [8192, DM], F32, kind="ExternalInput").ap()
    wq = nc.dram_tensor("wq", [DM, DM], F32, kind="ExternalInput").ap()
    wk = nc.dram_tensor("wk", [DM, DM], F32, kind="ExternalInput").ap()
    wv = nc.dram_tensor("wv", [DM, DM], F32, kind="ExternalInput").ap()
    wo = nc.dram_tensor("wo", [DM, DM], F32, kind="ExternalInput").ap()
    bo = nc.dram_tensor("bo", [DM], F32, kind="ExternalInput").ap()
    rw = nc.dram_tensor("rw", [DM, NE], F32, kind="ExternalInput").ap()
    rb = nc.dram_tensor("rb", [NE], F32, kind="ExternalInput").ap()
    w1 = nc.dram_tensor("w1", [NE * 4 * 128, 8 * 2048], F32, kind="ExternalInput").ap()
    b1 = nc.dram_tensor("b1", [NE, 2 * HID], F32, kind="ExternalInput").ap()
    w2 = nc.dram_tensor("w2", [NE * 4 * 128, 8 * 1024], F32, kind="ExternalInput").ap()
    b2 = nc.dram_tensor("b2", [NE, DM], F32, kind="ExternalInput").ap()
    m1 = nc.dram_tensor("m1", [128, 512], F32, kind="ExternalInput").ap()
    m2 = nc.dram_tensor("m2", [128, 512], F32, kind="ExternalInput").ap()
    y = nc.dram_tensor("y", [TOK, DM], F32, kind="ExternalOutput").ap()

    xnf = nc.dram_tensor("xnf", [8, 128, 8192], BF16, kind="Internal").ap()
    kfm = nc.dram_tensor("kfm", [NU, 65, NH * 512], BF16, kind="Internal").ap()
    ktk = nc.dram_tensor("ktk", [NU, 128, 4, NH * 65], BF16, kind="Internal").ap()
    vtk = nc.dram_tensor("vtk", [NU, 128, 4, NH * 65], BF16, kind="Internal").ap()
    xtk = nc.dram_tensor("xtk", [TOK, DM], BF16, kind="Internal").ap()
    gmap = nc.dram_tensor("gmap", [NTILE * 128, 1], I32, kind="Internal").ap()
    ytok = nc.dram_tensor("ytok", [TOK, DM], BF16, kind="Internal").ap()
    w1b = nc.dram_tensor("w1b", [NE * 4 * 128, 8 * 2048], BF16, kind="Internal").ap()
    w2b = nc.dram_tensor("w2b", [NE * 4 * 128, 8 * 1024], BF16, kind="Internal").ap()
    b1flat = b1.rearrange("e (r f) -> (e r) f", f=128)

    em = Emitter(nc)
    outer = ExitStack()
    sb = lambda *a: outer.enter_context(nc.sbuf_tensor(*a))

    m2s = sb("m2s", [128, 512], F32)
    idb = sb("idb", [128, 128], BF16)
    m01 = sb("m01", [128, 512], BF16)
    idf = sb("idf", [128, 128], F32)
    ones_r_b = sb("ones_r_b", [1, 128], BF16)
    ones_r_f = sb("ones_r_f", [1, 128], F32)
    ones_c_f = sb("ones_c_f", [64, 1], F32)
    ones_c_b = sb("ones_c_b", [64, 1], BF16)
    rwS = sb("rwS", [128, 8, NE], F32)
    rbS = sb("rbS", [1, NE], F32)
    maskS = sb("maskS", [128, NSLOT * NE], F32)
    Lstrict = sb("Lstrict", [128, 128], F32)
    Acomb = sb("Acomb", [128, 128], F32)
    Bmat8 = sb("Bmat8", [8, 128], F32)
    Sie = sb("Sie", [128, 8], F32)
    thr8f = sb("thr8f", [8, 16], F32)
    t128r = sb("t128r", [8, NTILE], F32)
    ones_c128 = sb("ones_c128", [128, 1], F32)
    rowofff = sb("rowofff", [128, 1], F32)
    rowoffq = sb("rowoffq", [128, 4], F32)
    tokid = sb("tokid", [128, NSLOT], I32)
    widxQ = sb("widxQ", [128, 4, NTILE], I32)
    widxb1I = sb("widxb1I", [64, NTILE], I32)
    stackA = ExitStack()
    sbA = lambda *a: stackA.enter_context(nc.sbuf_tensor(*a))
    stackW = ExitStack()
    wot = stackW.enter_context(nc.sbuf_tensor("wot", [128, 8, DM], BF16))
    bot = stackW.enter_context(nc.sbuf_tensor("bot", [128, 8], F32))
    wqt = sbA("wqt", [128, 8, DM], BF16)
    wkt = sbA("wkt", [128, 8, DM], BF16)
    wvt = sbA("wvt", [128, 8, DM], BF16)
    xnown = sbA("xnown", [128, 8, TOK], BF16)

    with outer:
        with ExitStack() as es:
            psb = lambda *a: es.enter_context(nc.psum_tensor(*a))
            lsb = lambda *a: es.enter_context(nc.sbuf_tensor(*a))
            iti = lsb("iti", [128, 128], I32)
            wrow = [lsb(f"wrow{i}", [128, DM], F32) for i in range(2)]
            bo8 = lsb("bo8", [8, 128], F32)
            it8 = lsb("it8", [8, 128], I32)
            ithr = lsb("ithr", [8, 16], I32)
            it24 = lsb("it24", [8, NTILE], I32)
            irow = lsb("irow", [128, 1], I32)
            irq = lsb("irq", [128, 4], I32)
            s16b = lsb("s16b", [8, 128], F32)
            b16sb = lsb("b16sb", [128, 128], F32)
            trp = [psb(f"trp{i}", [128, 128], F32) for i in range(2)]

            def prep(em):
                em.dma("sync", "cst", lambda e: e.dma_start(m2s[:], m2))
                em.dma("sync", "cst", lambda e: e.dma_start(rbS[:], rb[None, :]))
                for c in range(8):
                    em.dma("sync", "cst", lambda e, c=c: e.dma_start(
                        rwS[:, c, :], rw[c * 128:(c + 1) * 128, :]))
                em.dma("sync", "cst", lambda e: e.dma_start(
                    bo8[:], bo.rearrange("(c p) -> c p", p=128)))
                em.op("gpsimd", lambda e: e.iota(
                    iti[:], pattern=[[1, 128]], base=0, channel_multiplier=-1),
                    tick=True)
                em.op("gpsimd", lambda e: e.memset(ones_r_b[:], 1.0), tick=True)
                em.op("gpsimd", lambda e: e.memset(ones_r_f[:], 1.0), tick=True)
                em.op("gpsimd", lambda e: e.memset(ones_c_f[:], 1.0), tick=True)
                em.op("gpsimd", lambda e: e.memset(ones_c_b[:], 1.0), tick=True)
                em.wait("vector", "gpsimd")
                em.op("vector", lambda e: e.tensor_scalar(
                    idb[:], iti[:], 0, None, ALU.is_equal, ALU.bypass), tick=True)
                em.wait("vector", "d:cst")
                em.op("vector", lambda e: e.tensor_scalar(
                    m01[:], m2s[:], 0, None, ALU.is_equal, ALU.bypass), tick=True)
                em.op("vector", lambda e: e.tensor_scalar(
                    idf[:], iti[:], 0, None, ALU.is_equal, ALU.bypass), tick=True)

                wlist = [(wq, wqt), (wk, wkt), (wv, wvt), (wo, wot)]
                it = 0
                for wsrc, wdst in wlist:
                    for r in range(8):
                        buf = wrow[it % 2]
                        em.wait_snap("sync", "tensor", "wrow_free", it - 2)
                        em.dma("sync", f"wrow{it % 2}", lambda e, buf=buf, wsrc=wsrc, r=r: e.dma_start(
                            buf[:], wsrc[r * 128:(r + 1) * 128, :]))
                        em.wait("tensor", f"d:wrow{it % 2}")
                        em.wait("tensor", "vector")
                        for c in range(8):
                            pt = trp[c % 2]
                            em.wait_snap("tensor", "scalar", "wtr_evac",
                                         it * 8 + c - 2)
                            em.op("tensor", lambda e, pt=pt, buf=buf, c=c: e.transpose(
                                pt[:], buf[:, c * 128:(c + 1) * 128], idf[:]),
                                tick=True)
                            em.wait("scalar", "tensor")
                            em.op("scalar", lambda e, pt=pt, wdst=wdst, c=c, r=r: e.copy(
                                wdst[:, c, r * 128:(r + 1) * 128], pt[:]), tick=True)
                            em.snap("wtr_evac", "scalar")
                        em.snap("wrow_free", "tensor")
                        it += 1
                em.wait("tensor", "d:cst")
                em.wait_snap("tensor", "scalar", "wtr_evac", 8 * 32 - 2)
                em.op("tensor", lambda e: e.transpose(
                    trp[0][0:128, 0:8], bo8[:], idf[0:8, 0:8]), tick=True)
                em.wait("scalar", "tensor")
                em.op("scalar", lambda e: e.copy(bot[:], trp[0][0:128, 0:8]),
                      tick=True)
                em.op("gpsimd", lambda e: e.iota(
                    it8[:].rearrange("p (b j) -> p b j", b=8),
                    pattern=[[1, 8], [0, 16]], base=0, channel_multiplier=-1),
                    tick=True)
                em.op("gpsimd", lambda e: e.iota(
                    ithr[:], pattern=[[128, 16]], base=0, channel_multiplier=0),
                    tick=True)
                em.op("gpsimd", lambda e: e.iota(
                    it24[:], pattern=[[128, NTILE]], base=0,
                    channel_multiplier=0), tick=True)
                em.op("gpsimd", lambda e: e.iota(
                    irow[:], pattern=[[0, 1]], base=0, channel_multiplier=1),
                    tick=True)
                em.op("gpsimd", lambda e: e.iota(
                    irq[:], pattern=[[128, 4]], base=0, channel_multiplier=1),
                    tick=True)
                em.op("gpsimd", lambda e: e.iota(
                    tokid[:], pattern=[[128, NSLOT]], base=0,
                    channel_multiplier=1), tick=True)
                em.op("gpsimd", lambda e: e.memset(ones_c128[:], 1.0), tick=True)
                em.wait("vector", "gpsimd")
                em.op("vector", lambda e: e.tensor_scalar(
                    Lstrict[:], iti[:], 0, None, ALU.is_gt, ALU.bypass), tick=True)
                em.op("vector", lambda e: e.tensor_scalar(
                    s16b[:], it8[:], 0, None, ALU.is_equal, ALU.bypass), tick=True)
                em.op("vector", lambda e: e.tensor_scalar(
                    Bmat8[:], it8[:], 0, None, ALU.is_gt, ALU.bypass), tick=True)
                em.op("vector", lambda e: e.tensor_scalar(
                    thr8f[:], ithr[:], 1.0, None, ALU.mult, ALU.bypass), tick=True)
                em.op("vector", lambda e: e.tensor_scalar(
                    t128r[:], it24[:], 1.0, None, ALU.mult, ALU.bypass), tick=True)
                em.op("vector", lambda e: e.tensor_scalar(
                    rowofff[:], irow[:], 1.0, None, ALU.mult, ALU.bypass), tick=True)
                em.op("vector", lambda e: e.tensor_scalar(
                    rowoffq[:], irq[:], 1.0, None, ALU.mult, ALU.bypass),
                    tick=True)
                em.wait("tensor", "vector")
                em.wait("tensor", "scalar")
                em.op("tensor", lambda e: e.transpose(
                    trp[0][0:128, 0:8], s16b[:], idf[0:8, 0:8]), tick=True)
                em.op("tensor", lambda e: e.matmul(
                    trp[1][:], s16b[:], s16b[:], start=True, stop=True), tick=True)
                em.wait("scalar", "tensor")
                em.op("scalar", lambda e: e.copy(Sie[:], trp[0][0:128, 0:8]),
                      tick=True)
                em.op("scalar", lambda e: e.copy(b16sb[:], trp[1][:]), tick=True)
                em.wait("vector", "scalar")
                em.op("vector", lambda e: e.tensor_tensor(
                    Acomb[:], b16sb[:], Lstrict[:], ALU.mult), tick=True)
                em.wait("sync", "d:cst")

            em.run("prep", prep)

        with ExitStack() as es:
            psb = lambda *a: es.enter_context(nc.psum_tensor(*a))
            lsb = lambda *a: es.enter_context(nc.sbuf_tensor(*a))
            xT = [lsb(f"xT{i}", [128, DM], F32) for i in range(2)]
            ssum = lsb("ssum", [128, 1], F32)
            nmean = lsb("nmean", [128, 1], F32)
            ssq = lsb("ssq", [128, 1], F32)
            vstd = lsb("vstd", [128, 1], F32)
            rstd = lsb("rstd", [128, 1], F32)
            sqj = lsb("sqj", [128, DM], F32)
            xnb = [lsb(f"xnb{i}", [128, DM], BF16) for i in range(2)]
            xn32 = lsb("xn32", [128, DM], F32)
            xf32 = lsb("xf32", [128, 8, 128], F32)
            xfst = [lsb(f"xfst{i}", [128, 8, 512], BF16) for i in range(2)]
            rmax8 = lsb("rmax8", [128, 1], F32)
            rraw = lsb("rraw", [128, 1], F32)
            ravail = lsb("ravail", [128, 1], F32)
            rtaken = lsb("rtaken", [128, 1], F32)
            trb = psb("trb", [128, DM], BF16)
            trf = psb("trf", [128, DM], F32)
            lp = psb("lp", [128, NE], F32)

            def p0(em):
                for p in range(NT):
                    u, r = p // 4, p % 4
                    own = (r == 0)
                    slot = u
                    buf = xT[p % 2]
                    em.wait_snap("sync", "vector", "xT_free", p - 2)
                    em.dma("sync", f"x{p % 2}", lambda e, buf=buf, p=p: e.dma_start(
                        buf[:], xe[p * 128:(p + 1) * 128, :]))
                    em.wait("vector", f"d:x{p % 2}")
                    em.wait_snap("vector", "scalar", "sqrt_done", p - 1)
                    em.op("vector", lambda e, buf=buf: e.reduce_sum(
                        ssum[:], buf[:], axis=AX.X))
                    em.op("vector", lambda e: e.tensor_scalar(
                        nmean[:], ssum[:], -1.0 / DM, None, ALU.mult, ALU.bypass),
                        tick=True)
                    em.wait("scalar", "vector")
                    em.op("scalar", lambda e, buf=buf: e.activation(
                        sqj[:], buf[:], AF.Square, bias=nmean[:], scale=1.0),
                        tick=True)
                    em.wait("vector", "scalar")
                    em.op("vector", lambda e: e.reduce_sum(
                        ssq[:], sqj[:], axis=AX.X))
                    em.op("vector", lambda e: e.tensor_scalar(
                        vstd[:], ssq[:], 1.0 / DM, 1.0e-5, ALU.mult, ALU.add),
                        tick=True)
                    em.wait("scalar", "vector")
                    em.op("scalar", lambda e: e.activation(
                        vstd[:], vstd[:], AF.Sqrt), tick=True)
                    em.snap("sqrt_done", "scalar")
                    em.wait("vector", "scalar")
                    em.op("vector", lambda e: e.reciprocal(rstd[:], vstd[:]))
                    xb = xnb[p % 2]
                    em.wait_snap("vector", "tensor", "xnb_free", p - 2)
                    em.op("vector", lambda e, buf=buf, xb=xb: e.tensor_scalar(
                        xb[:], buf[:], nmean[:], rstd[:], ALU.add, ALU.mult),
                        tick=True)
                    if own:
                        em.wait_snap("vector", "tensor", "trf_done", slot - 1)
                        em.wait_snap("vector", "d:xtk", "xtk_sent", slot - 1)
                        em.op("vector", lambda e, buf=buf: e.tensor_scalar(
                            xn32[:], buf[:], nmean[:], rstd[:], ALU.add, ALU.mult),
                            tick=True)
                        em.wait("gpsimd", "vector")
                        em.dma("gpsimd", "xtk", lambda e, slot=slot: e.dma_start(
                            xtk[slot * 128:(slot + 1) * 128, :], xn32[:]))
                        em.snap("xtk_sent", "d:xtk")
                    em.snap("xT_free", "vector")
                    em.wait("tensor", "vector")
                    em.wait_snap("tensor", "scalar", "trb_evac", p - 1)
                    for c in range(8):
                        em.op("tensor", lambda e, xb=xb, c=c: e.transpose(
                            trb[:, c * 128:(c + 1) * 128],
                            xb[:, c * 128:(c + 1) * 128], idb[:]), tick=(c == 7))
                    em.snap("xnb_free", "tensor")
                    st = xfst[u % 2]
                    em.wait("scalar", "tensor")
                    if r == 0:
                        em.wait_snap("scalar", f"d:xf{u % 2}", "xfst_sent", u - 2)
                    em.op("scalar", lambda e, st=st, r=r: e.copy(
                        st[:, :, r * 128:(r + 1) * 128],
                        trb[:].rearrange("p (c t) -> p c t", c=8)), tick=True)
                    if own:
                        em.op("scalar", lambda e, slot=slot: e.copy(
                            xnown[:, :, slot * 128:(slot + 1) * 128],
                            trb[:].rearrange("p (c t) -> p c t", c=8)), tick=True)
                    em.snap("trb_evac", "scalar")
                    if own:
                        em.wait("tensor", "vector")
                        em.wait_snap("tensor", "scalar", "trf_evac", slot - 1)
                        for c in range(8):
                            em.op("tensor", lambda e, c=c: e.transpose(
                                trf[:, c * 128:(c + 1) * 128],
                                xn32[:, c * 128:(c + 1) * 128], idf[:]),
                                tick=(c == 7))
                        em.snap("trf_done", "tensor")
                        em.wait("scalar", "tensor")
                        em.op("scalar", lambda e: e.copy(
                            xf32[:], trf[:].rearrange("p (c t) -> p c t", c=8)),
                            tick=True)
                        em.snap("trf_evac", "scalar")
                        em.wait("tensor", "scalar")
                        em.wait_snap("tensor", "vector", "lp_done", slot - 1)
                        for c in range(8):
                            em.op("tensor", lambda e, c=c: e.matmul(
                                lp[:], xf32[:, c, :], rwS[:, c, :],
                                start=(c == 0), stop=False))
                        em.op("tensor", lambda e: e.matmul(
                            lp[:], ones_r_f[:], rbS[:], start=False, stop=True),
                            tick=True)
                        em.wait("vector", "tensor")
                        em.op("vector", lambda e: e.reduce_max(
                            rmax8[:], lp[:], axis=AX.X))
                        em.op("vector", lambda e: e.memset(rtaken[:], 0.0))
                        for ei in range(NE):
                            mcol = maskS[:, ei * NSLOT + slot:ei * NSLOT + slot + 1]
                            em.op("vector", lambda e, ei=ei: e.tensor_tensor(
                                rraw[:], lp[:, ei:ei + 1], rmax8[:], ALU.is_ge))
                            em.op("vector", lambda e: e.tensor_scalar(
                                ravail[:], rtaken[:], -1.0, 1.0, ALU.mult, ALU.add))
                            em.op("vector", lambda e, mcol=mcol: e.tensor_tensor(
                                mcol, rraw[:], ravail[:], ALU.mult))
                            em.op("vector", lambda e, mcol=mcol: e.tensor_tensor(
                                rtaken[:], rtaken[:], mcol, ALU.max),
                                tick=(ei == NE - 1))
                        em.snap("lp_done", "vector")
                    if r == 3:
                        em.wait("sync", "scalar")
                        em.dma("sync", f"xf{u % 2}", lambda e, st=st, u=u: e.dma_start(
                            xnf[:, :, u * 512:(u + 1) * 512]
                            .rearrange("c p t -> p c t"), st[:]))
                        em.snap("xfst_sent", f"d:xf{u % 2}")
                em.wait("sync", "d:xf0")
                em.wait("sync", "d:xf1")
                em.wait("gpsimd", "d:xtk")

            em.run("p0", p0)

        with ExitStack() as es:
            psb = lambda *a: es.enter_context(nc.psum_tensor(*a))
            lsb = lambda *a: es.enter_context(nc.sbuf_tensor(*a))
            rpA = psb("rpA", [128, 128], F32)
            rpB = psb("rpB", [128, 128], F32)
            cnts = lsb("cnts", [128, 1], F32)
            c8s = lsb("c8s", [8, 1], F32)
            nt16 = lsb("nt16", [8, 16], F32)
            ntile8 = lsb("ntile8", [8, 1], F32)
            capG = lsb("capG", [8, 1], F32)
            baseEs = lsb("baseEs", [8, 1], F32)
            colbases = lsb("colbases", [128, 1], F32)
            colbrow = lsb("colbrow", [1, 128], F32)
            posF = lsb("posF", [128, 128], F32)
            posm = lsb("posm", [128, 128], F32)
            postok = lsb("postok", [128, NSLOT], F32)
            sidxI = lsb("sidxI", [128, NSLOT], I32)
            m8t = lsb("m8t", [8, NTILE], F32)
            etraw = lsb("etraw", [1, NTILE], F32)
            etrow = lsb("etrow", [1, NTILE], F32)
            skiprow = lsb("skiprow", [1, NTILE], F32)
            srowBig = lsb("srowBig", [1, NTILE], F32)
            ebs0 = lsb("ebs0", [128, NTILE], F32)
            skipbc = lsb("skipbc", [128, NTILE], F32)
            ebs = lsb("ebs", [128, NTILE], F32)
            sentI = lsb("sentI", [NTILE, 128], I32)

            def route(em):
                em.op("tensor", lambda e: e.matmul(
                    rpB[:, 0:1], maskS[:], ones_c128[:], start=True, stop=True),
                    tick=True)
                em.wait("scalar", "tensor")
                em.op("scalar", lambda e: e.copy(cnts[:], rpB[:, 0:1]), tick=True)
                em.wait("tensor", "scalar")
                em.op("tensor", lambda e: e.matmul(
                    rpB[0:8, 1:2], Sie[:], cnts[:], start=True, stop=True),
                    tick=True)
                em.wait("scalar", "tensor")
                em.op("scalar", lambda e: e.copy(c8s[:], rpB[0:8, 1:2]), tick=True)
                em.wait("vector", "scalar")
                em.op("vector", lambda e: e.tensor_scalar(
                    nt16[:], thr8f[:], c8s[:], None, ALU.is_lt, ALU.bypass),
                    tick=True)
                em.op("vector", lambda e: e.reduce_sum(
                    ntile8[:], nt16[:], axis=AX.X))
                em.op("vector", lambda e: e.tensor_scalar(
                    capG[:], ntile8[:], 128.0, None, ALU.mult, ALU.bypass),
                    tick=True)
                em.wait("tensor", "vector")
                em.op("tensor", lambda e: e.matmul(
                    rpB[0:8, 2:3], Lstrict[0:8, 0:8], capG[:], start=True,
                    stop=True), tick=True)
                em.wait("scalar", "tensor")
                em.op("scalar", lambda e: e.copy(baseEs[:], rpB[0:8, 2:3]),
                      tick=True)
                em.wait("tensor", "scalar")
                em.op("tensor", lambda e: e.matmul(
                    rpB[:, 3:4], Acomb[:], cnts[:], start=True, stop=False))
                em.op("tensor", lambda e: e.matmul(
                    rpB[:, 3:4], Bmat8[:], capG[:], start=False, stop=True),
                    tick=True)
                em.wait("scalar", "tensor")
                em.op("scalar", lambda e: e.copy(colbases[:], rpB[:, 3:4]),
                      tick=True)
                em.wait("tensor", "scalar")
                em.op("tensor", lambda e: e.transpose(
                    rpA[0:1, :], colbases[:], idf[:]), tick=True)
                em.wait("scalar", "tensor")
                em.op("scalar", lambda e: e.copy(colbrow[:], rpA[0:1, :]),
                      tick=True)
                em.wait("tensor", "scalar")
                em.op("tensor", lambda e: e.matmul(
                    rpA[:], Lstrict[:], maskS[:], start=True, stop=False))
                em.op("tensor", lambda e: e.matmul(
                    rpA[:], ones_r_f[:], colbrow[:], start=False, stop=True),
                    tick=True)
                em.wait("scalar", "tensor")
                em.op("scalar", lambda e: e.copy(posF[:], rpA[:]), tick=True)
                em.wait("vector", "scalar")
                em.op("vector", lambda e: e.tensor_tensor(
                    posm[:], posF[:], maskS[:], ALU.mult))
                em.op("vector", lambda e: e.reduce_sum(
                    postok[:], posm[:].rearrange("p (e i) -> p i e", i=NSLOT),
                    axis=AX.X))
                em.op("vector", lambda e: e.tensor_scalar(
                    sidxI[:], postok[:], 1.0, None, ALU.mult, ALU.bypass),
                    tick=True)
                em.op("gpsimd", lambda e: e.iota(
                    sentI[:], pattern=[[1, 128]], base=1000000,
                    channel_multiplier=128), tick=True)
                em.dma("gpsimd", "sent", lambda e: e.dma_start(
                    gmap.rearrange("(t p) o -> t (p o)", p=128), sentI[:]))
                em.wait("gpsimd", "d:sent")
                em.wait("gpsimd", "vector")
                for i in range(NSLOT):
                    em.dma("gpsimd", "gsc", lambda e, i=i: e.indirect_dma_start(
                        gmap, bass.IndirectOffsetOnAxis(sidxI[:, i:i + 1], 0),
                        tokid[:, i:i + 1], None))
                em.op("vector", lambda e: e.tensor_scalar(
                    m8t[:], t128r[:], baseEs[:], None, ALU.is_ge, ALU.bypass),
                    tick=True)
                em.wait("tensor", "vector")
                em.op("tensor", lambda e: e.matmul(
                    rpB[0:1, 8:8 + NTILE], ones_c128[0:8, :], m8t[:],
                    start=True, stop=True), tick=True)
                em.wait("scalar", "tensor")
                em.op("scalar", lambda e: e.copy(etraw[:], rpB[0:1, 8:8 + NTILE]),
                      tick=True)
                em.wait("vector", "scalar")
                em.op("vector", lambda e: e.tensor_scalar(
                    etrow[:], etraw[:], -1.0, None, ALU.add, ALU.bypass))
                em.op("vector", lambda e: e.memset(skiprow[:], 0.0))
                em.op("vector", lambda e: e.tensor_tensor(
                    skiprow[:, 2:NTILE], etrow[:, 2:NTILE], etrow[:, 0:NTILE - 2],
                    ALU.is_equal))
                em.op("vector", lambda e: e.tensor_scalar(
                    srowBig[:], skiprow[:], 100000.0, None, ALU.mult, ALU.bypass),
                    tick=True)
                em.wait("tensor", "vector")
                em.op("tensor", lambda e: e.matmul(
                    rpB[:, 32:32 + NTILE], ones_r_f[:], etrow[:], start=True,
                    stop=True))
                em.op("tensor", lambda e: e.matmul(
                    rpB[:, 64:64 + NTILE], ones_r_f[:], srowBig[:], start=True,
                    stop=True), tick=True)
                em.wait("scalar", "tensor")
                em.op("scalar", lambda e: e.copy(ebs0[:], rpB[:, 32:32 + NTILE]))
                em.op("scalar", lambda e: e.copy(skipbc[:], rpB[:, 64:64 + NTILE]),
                      tick=True)
                em.wait("vector", "scalar")
                em.op("vector", lambda e: e.tensor_tensor(
                    ebs[:], ebs0[:], skipbc[:], ALU.add), tick=True)
                for k in range(4):
                    em.op("vector", lambda e, k=k: e.tensor_scalar(
                        widxQ[:, k, :], ebs[:], 512.0, rowoffq[:, k:k + 1],
                        ALU.mult, ALU.add), tick=(k == 3))
                em.op("vector", lambda e: e.tensor_scalar(
                    widxb1I[:], ebs0[0:64, :], 64.0, rowofff[0:64, :],
                    ALU.mult, ALU.add), tick=True)
                em.wait("gpsimd", "d:gsc")

            em.run("route", route)

        with ExitStack() as es:
            psb = lambda *a: es.enter_context(nc.psum_tensor(*a))
            lsb = lambda *a: es.enter_context(nc.sbuf_tensor(*a))
            xkb = [lsb(f"xkb{i}", [128, 8, 512], BF16) for i in range(2)]
            kvin = [lsb(f"kvin{i}", [128, 1024], F32) for i in range(2)]
            kvout = [lsb(f"kvout{i}", [128, 1024], BF16) for i in range(2)]
            kstg = [lsb(f"kstg{i}", [65, NH * 512], BF16) for i in range(2)]
            ktst = [lsb(f"ktst{i}", [128, 4, NH * 65], BF16) for i in range(2)]
            vtst = [lsb(f"vtst{i}", [128, 4, NH * 65], BF16) for i in range(2)]
            kfp = [psb(f"kfp{i}", [128, 512], F32) for i in range(2)]
            ktp = psb("ktp", [128, DM], F32)
            vtp = psb("vtp", [128, DM], F32)

            kv_chunks = []
            for r in range(32):
                for cc in range(8):
                    kv_chunks.append((w2, w2b, r * 128, cc * 1024))

            def kv(em):
                kst = dict(ck=0, pend=[])

                def kconv(n):
                    for kp in kst["pend"]:
                        _, dstP, rp, cp = kv_chunks[kp]
                        em.wait("vector", f"d:kvi{kp % 2}")
                        em.wait_snap("vector", f"d:kvo{kp % 2}",
                                     "kvo_sent", kp - 2)
                        em.op("vector", lambda e, kp=kp: e.tensor_copy(
                            kvout[kp % 2][:], kvin[kp % 2][:]), tick=True)
                        em.snap("kvc", "vector")
                        em.wait("gpsimd", "vector")
                        em.dma("gpsimd", f"kvo{kp % 2}",
                               lambda e, kp=kp, dstP=dstP, rp=rp, cp=cp:
                               e.dma_start(
                                   dstP[rp:rp + 128, cp:cp + 1024],
                                   kvout[kp % 2][:]))
                        em.snap("kvo_sent", f"d:kvo{kp % 2}")
                    kst["pend"] = []
                    for _ in range(n):
                        k = kst["ck"]
                        if k >= len(kv_chunks):
                            return
                        kst["ck"] += 1
                        srcT, dstT, r0, c0 = kv_chunks[k]
                        em.wait_snap("gpsimd", "vector", "kvc", k - 2)
                        em.dma("gpsimd", f"kvi{k % 2}",
                               lambda e, srcT=srcT, r0=r0, c0=c0, k=k:
                               e.dma_start(kvin[k % 2][:],
                                           srcT[r0:r0 + 128, c0:c0 + 1024]))
                        kst["pend"].append(k)
                for i in range(2):
                    em.op("gpsimd", lambda e, i=i: e.memset(
                        kstg[i][64:65, :], 1.0), tick=True)
                    em.op("gpsimd", lambda e, i=i: e.memset(
                        ktst[i][:].rearrange("p t (h o) -> p t h o", o=65)
                        [:, :, :, 64:65], 1.0), tick=True)
                    em.op("gpsimd", lambda e, i=i: e.memset(
                        vtst[i][:].rearrange("p t (h o) -> p t h o", o=65)
                        [:, :, :, 64:65], 1.0), tick=True)
                for u in range(NU):
                    xb = xkb[u % 2]
                    em.wait_snap("sync", "tensor", "xkb_free", u - 2)
                    em.dma("sync", f"xkb{u % 2}", lambda e, xb=xb, u=u: e.dma_start(
                        xb[:], xnf[:, :, u * 512:(u + 1) * 512]
                        .rearrange("c p t -> p c t")))
                    ks = kstg[u % 2]
                    em.wait("tensor", f"d:xkb{u % 2}")
                    for hp in range(8):
                        kconv(2)
                        kp = kfp[hp % 2]
                        em.wait_snap("tensor", "scalar", "kfp_evac", u * 8 + hp - 2)
                        for c in range(8):
                            em.op("tensor", lambda e, kp=kp, c=c, hp=hp, xb=xb: e.matmul(
                                kp[:], wkt[:, c, hp * 128:(hp + 1) * 128],
                                xb[:, c, :], start=(c == 0), stop=(c == 7)),
                                tick=(c == 7))
                        em.wait("scalar", "tensor")
                        if hp == 0:
                            em.wait("scalar", "gpsimd")
                            em.wait_snap("scalar", f"d:kst{u % 2}", "kfm_sent", u - 2)
                        em.op("scalar", lambda e, kp=kp, ks=ks, hp=hp: e.copy(
                            ks[0:64, (2 * hp) * 512:(2 * hp + 1) * 512], kp[0:64, :]))
                        em.op("scalar", lambda e, kp=kp, ks=ks, hp=hp: e.copy(
                            ks[0:64, (2 * hp + 1) * 512:(2 * hp + 2) * 512],
                            kp[64:128, :]), tick=True)
                        em.snap("kfp_evac", "scalar")
                    em.wait("sync", "scalar")
                    em.dma("sync", f"kst{u % 2}", lambda e, ks=ks, u=u: e.dma_start(kfm[u], ks[:]))
                    em.snap("kfm_sent", f"d:kst{u % 2}")
                    kt = ktst[u % 2]
                    for t in range(4):
                        em.wait_snap("tensor", "scalar", "ktp_evac", u * 4 + t - 1)
                        for c in range(8):
                            for hh in range(2):
                                em.op("tensor", lambda e, xb=xb, c=c, t=t, hh=hh: e.matmul(
                                    ktp[:, hh * 512:(hh + 1) * 512],
                                    xb[:, c, t * 128:(t + 1) * 128],
                                    wkt[:, c, hh * 512:(hh + 1) * 512],
                                    start=(c == 0), stop=(c == 7)),
                                    tick=(c == 7 and hh == 1))
                        em.wait("scalar", "tensor")
                        if t == 0:
                            em.wait_snap("scalar", f"d:ktt{u % 2}", "ktk_sent", u - 2)
                        em.op("scalar", lambda e, kt=kt, t=t: e.copy(
                            kt[:].rearrange("p t (h o) -> p t h o", o=65)
                            [:, t, :, 0:64],
                            ktp[:].rearrange("p (h d) -> p h d", d=64)), tick=True)
                        em.snap("ktp_evac", "scalar")
                    em.wait("sync", "scalar")
                    em.dma("sync", f"ktt{u % 2}", lambda e, kt=kt, u=u: e.dma_start(ktk[u], kt[:]))
                    em.snap("ktk_sent", f"d:ktt{u % 2}")
                    vt = vtst[u % 2]
                    for t in range(4):
                        em.wait_snap("tensor", "scalar", "vtp_evac", u * 4 + t - 1)
                        for c in range(8):
                            for hh in range(2):
                                em.op("tensor", lambda e, xb=xb, c=c, t=t, hh=hh: e.matmul(
                                    vtp[:, hh * 512:(hh + 1) * 512],
                                    xb[:, c, t * 128:(t + 1) * 128],
                                    wvt[:, c, hh * 512:(hh + 1) * 512],
                                    start=(c == 0), stop=(c == 7)),
                                    tick=(c == 7 and hh == 1))
                        em.wait("scalar", "tensor")
                        if t == 0:
                            em.wait_snap("scalar", f"d:vtt{u % 2}", "vtk_sent", u - 2)
                        em.op("scalar", lambda e, vt=vt, t=t: e.copy(
                            vt[:].rearrange("p t (h o) -> p t h o", o=65)
                            [:, t, :, 0:64],
                            vtp[:].rearrange("p (h d) -> p h d", d=64)), tick=True)
                        em.snap("vtp_evac", "scalar")
                    em.snap("xkb_free", "tensor")
                    em.wait("sync", "scalar")
                    em.dma("sync", f"vtt{u % 2}", lambda e, vt=vt, u=u: e.dma_start(vtk[u], vt[:]))
                    em.snap("vtk_sent", f"d:vtt{u % 2}")
                while kst["ck"] < len(kv_chunks):
                    kconv(2)
                for kp in list(kst["pend"]):
                    pass
                if True:
                    for kp in kst["pend"]:
                        _, dstP, rp, cp = kv_chunks[kp]
                        em.wait("vector", f"d:kvi{kp % 2}")
                        em.wait_snap("vector", f"d:kvo{kp % 2}",
                                     "kvo_sent", kp - 2)
                        em.op("vector", lambda e, kp=kp: e.tensor_copy(
                            kvout[kp % 2][:], kvin[kp % 2][:]), tick=True)
                        em.snap("kvc", "vector")
                        em.wait("gpsimd", "vector")
                        em.dma("gpsimd", f"kvo{kp % 2}",
                               lambda e, kp=kp, dstP=dstP, rp=rp, cp=cp:
                               e.dma_start(
                                   dstP[rp:rp + 128, cp:cp + 1024],
                                   kvout[kp % 2][:]))
                        em.snap("kvo_sent", f"d:kvo{kp % 2}")
                    kst["pend"] = []
                em.wait("gpsimd", "d:kvo0")
                em.wait("gpsimd", "d:kvo1")
                for s_ in ("kst0", "kst1", "ktt0", "ktt1", "vtt0", "vtt1"):
                    em.wait("sync", "d:" + s_)

            em.run("kv", kv)

        stackQ = ExitStack()
        qbuf = stackQ.enter_context(nc.sbuf_tensor("qbuf", [65, NH * TOK], BF16,
                                                   side="right"))
        with ExitStack() as es:
            psb = lambda *a: es.enter_context(nc.psum_tensor(*a))
            qps = [psb(f"qps{i}", [128, 512], F32) for i in range(2)]

            def attn_q(em):
                em.op("gpsimd", lambda e: e.memset(qbuf[64:65, :], 0.0),
                      tick=True)
                for hp in range(8):
                    for tc in range(4):
                        it = hp * 4 + tc
                        qp = qps[it % 2]
                        em.wait_snap("tensor", "scalar", "qp_evac", it - 2)
                        for c in range(8):
                            em.op("tensor", lambda e, qp=qp, c=c, hp=hp, tc=tc: e.matmul(
                                qp[:], wqt[:, c, hp * 128:(hp + 1) * 128],
                                xnown[:, c, tc * 512:(tc + 1) * 512],
                                start=(c == 0), stop=(c == 7)), tick=(c == 7))
                        em.wait("scalar", "tensor")
                        h0, h1 = 2 * hp, 2 * hp + 1
                        em.op("scalar", lambda e, qp=qp, h0=h0, tc=tc: e.activation(
                            qbuf[0:64, h0 * TOK + tc * 512:h0 * TOK + (tc + 1) * 512],
                            qp[0:64, :], AF.Copy, scale=0.125))
                        em.op("scalar", lambda e, qp=qp, h1=h1, tc=tc: e.activation(
                            qbuf[0:64, h1 * TOK + tc * 512:h1 * TOK + (tc + 1) * 512],
                            qp[64:128, :], AF.Copy, scale=0.125), tick=True)
                        em.snap("qp_evac", "scalar")

            em.run("attn_q", attn_q)
        stackA.close()

        stackC = ExitStack()
        ntbig = stackC.enter_context(nc.sbuf_tensor("ntbig", [128, 8 * TOK], BF16))
        with ExitStack() as es:
            psb = lambda *a: es.enter_context(nc.psum_tensor(*a))
            lsb = lambda *a: es.enter_context(nc.sbuf_tensor(*a))
            sps = [psb(f"s2ps{i}", [128, 2, 512], F32) for i in range(3)]
            ozp = [psb(f"ozp{i}", [65, 512], F32) for i in range(2)]
            urp = [p[64:65, :] for p in ozp]
            bcp = [p[0:64, :] for p in ozp]
            kb2 = [lsb(f"kb2{i}", [65, GH * 512], BF16) for i in range(2)]
            ntb_ = [lsb(f"ntb{i}", [128, 4, GH * 65], BF16) for i in range(2)]
            ttl = [lsb(f"ttl{i}", [128, 4, 512], BF16) for i in range(2)]
            uza = lsb("uza", [65, GH * TOK], F32)
            prod = lsb("prod", [64, TOK], BF16)
            rowA = lsb("rowA", [1, TOK], F32)
            betaro = lsb("betaro", [1, TOK], F32)
            cvin = [lsb(f"cvin{i}", [128, 1024], F32) for i in range(2)]
            cvout = [lsb(f"cvout{i}", [128, 1024], BF16) for i in range(2)]
            zpart = lsb("zpart", [128, 64], F32)
            upart = lsb("upart", [128, 64], F32)
            tA = lsb("tA", [128, 64], F32)
            tB = lsb("tB", [128, 64], F32)
            tC = lsb("tC", [128, 64], F32)

            def chunks_of(u):
                out, off, nq = [], u * 128, TOK - u * 128
                while nq > 0:
                    w = min(512, nq)
                    out.append((off, w))
                    off += w
                    nq -= w
                return out

            conv_chunks = []
            for r in range(32):
                for cc in range(16):
                    conv_chunks.append((w1, w1b, r * 128, cc * 1024))

            def attn_b(em):
                state = dict(nbc=0, nur=0, ck=0, pend_conv=[])

                def conv_step(n):
                    for kp in state["pend_conv"]:
                        srcP, dstP, rp, cp = conv_chunks[kp]
                        em.wait("vector", f"d:cvi{kp % 2}")
                        em.wait_snap("vector", f"d:cvo{kp % 2}",
                                     "cvo_sent", kp - 2)
                        em.op("vector", lambda e, kp=kp: e.tensor_copy(
                            cvout[kp % 2][:], cvin[kp % 2][:]), tick=True)
                        em.snap("cvc", "vector")
                        em.wait("gpsimd", "vector")
                        em.dma("gpsimd", f"cvo{kp % 2}",
                               lambda e, kp=kp, dstP=dstP, rp=rp, cp=cp:
                               e.dma_start(
                                   dstP[rp:rp + 128, cp:cp + 1024],
                                   cvout[kp % 2][:]))
                        em.snap("cvo_sent", f"d:cvo{kp % 2}")
                    state["pend_conv"] = []
                    for _ in range(n):
                        k = state["ck"]
                        if k >= len(conv_chunks):
                            return
                        state["ck"] += 1
                        srcT, dstT, r0, c0 = conv_chunks[k]
                        em.wait_snap("gpsimd", "vector", "cvc", k - 2)
                        em.dma("gpsimd", f"cvi{k % 2}",
                               lambda e, srcT=srcT, r0=r0, c0=c0, k=k:
                               e.dma_start(cvin[k % 2][:],
                                           srcT[r0:r0 + 128, c0:c0 + 1024]))
                        state["pend_conv"].append(k)

                def conv_last(em):
                    for kp in state["pend_conv"]:
                        srcP, dstP, rp, cp = conv_chunks[kp]
                        em.wait("vector", f"d:cvi{kp % 2}")
                        em.wait_snap("vector", f"d:cvo{kp % 2}",
                                     "cvo_sent", kp - 2)
                        em.op("vector", lambda e, kp=kp: e.tensor_copy(
                            cvout[kp % 2][:], cvin[kp % 2][:]), tick=True)
                        em.snap("cvc", "vector")
                        em.wait("gpsimd", "vector")
                        em.dma("gpsimd", f"cvo{kp % 2}",
                               lambda e, kp=kp, dstP=dstP, rp=rp, cp=cp:
                               e.dma_start(
                                   dstP[rp:rp + 128, cp:cp + 1024],
                                   cvout[kp % 2][:]))
                        em.snap("cvo_sent", f"d:cvo{kp % 2}")
                    state["pend_conv"] = []
                    em.wait("gpsimd", "d:cvo0")
                    em.wait("gpsimd", "d:cvo1")

                def sweep(g, which, sw):
                    em.wait("vector", "scalar")
                    em.wait("vector", "d:fin")
                    em.wait("tensor", "vector")
                    em.wait("sync", "tensor")
                    st = dict(nt=0, nsp=0, nd=0, pend=None)

                    def flush():
                        if st["pend"] is None:
                            return
                        tb, ntb, hl, qoff, qw, expidx, i, dmidx = st["pend"]
                        st["pend"] = None
                        oz = ozp[i % 2]
                        em.wait_snap("tensor", "vector", f"oz_{sw}", i - 2)
                        em.wait_snap("tensor", "scalar", f"exp_{sw}", expidx)
                        if dmidx >= 0:
                            em.wait_snap("tensor", "vector", f"dm_{sw}", dmidx)
                        for t in range(4):
                            em.op("tensor", lambda e, oz=oz, ntb=ntb, tb=tb,
                                  t=t, hl=hl, qw=qw: e.matmul(
                                      oz[:, 0:qw],
                                      ntb[:, t, hl * 65:(hl + 1) * 65],
                                      tb[:, t, 0:qw],
                                      start=(t == 0), stop=(t == 3)),
                                  tick=(t == 3))
                        em.snap(f"ozmm_{sw}", "tensor")
                        em.wait("vector", "tensor")
                        if i < GH * 4:
                            em.op("vector", lambda e, oz=oz, hl=hl, qoff=qoff,
                                  qw=qw: e.tensor_copy(
                                      uza[0:65, hl * TOK + qoff:hl * TOK + qoff + qw],
                                      oz[:, 0:qw]), tick=True)
                        else:
                            em.op("vector", lambda e, oz=oz, hl=hl, qoff=qoff,
                                  qw=qw: e.tensor_tensor(
                                      uza[0:65, hl * TOK + qoff:hl * TOK + qoff + qw],
                                      uza[0:65, hl * TOK + qoff:hl * TOK + qoff + qw],
                                      oz[:, 0:qw], ALU.add), tick=True)
                        em.snap(f"oz_{sw}", "vector")

                    for u in range(NU):
                        kb = kb2[u % 2]
                        src = ktk if which == "uz" else vtk
                        ntb = ntb_[u % 2]
                        em.wait_snap("sync", "tensor", f"kb2f_{sw}", u - 2)
                        em.dma("sync", f"kbn{u % 2}", lambda e, kb=kb, u=u, g=g: e.dma_start(
                            kb[:], kfm[u].rearrange("o (h t) -> o h t", t=512)
                            [:, 4 * g:4 * g + 4, :]))
                        em.dma("sync", f"kbn{u % 2}", lambda e, ntb=ntb, u=u, g=g, src=src: e.dma_start(
                            ntb[:], src[u].rearrange("p t (h o) -> p t h o", o=65)
                            [:, :, 4 * g:4 * g + 4, :]))
                        em.wait("tensor", f"d:kbn{u % 2}")
                        for hl in range(GH):
                            conv_step(1)
                            h = 4 * g + hl
                            for ci, (qoff, qw) in enumerate(chunks_of(u)):
                                nt = st["nt"]
                                tb = ttl[nt % 2]
                                for tp in range(2):
                                    nsp = st["nsp"]
                                    sp = sps[nsp % 3]
                                    em.wait_snap("tensor", "scalar", f"exp_{sw}",
                                                 nsp - 3)
                                    for tt in range(2):
                                        t = 2 * tp + tt
                                        em.op("tensor", lambda e, sp=sp, kb=kb,
                                              hl=hl, t=t, tt=tt, qoff=qoff, qw=qw,
                                              h=h: e.matmul(
                                                  sp[:, tt, 0:qw],
                                                  kb[0:65, hl * 512 + t * 128:
                                                     hl * 512 + (t + 1) * 128],
                                                  qbuf[0:65, h * TOK + qoff:
                                                       h * TOK + qoff + qw],
                                                  start=True, stop=True),
                                              tick=(tt == 1))
                                    if tp == 0:
                                        em.wait_snap("scalar", "tensor",
                                                     f"ozmm_{sw}", nt - 2)
                                    em.wait("scalar", "tensor")
                                    em.op("scalar", lambda e, sp=sp, tb=tb, tp=tp,
                                          qw=qw: e.activation(
                                              tb[:, 2 * tp:2 * tp + 2, 0:qw],
                                              sp[:, :, 0:qw], AF.Exp),
                                          tick=True)
                                    em.snap(f"exp_{sw}", "scalar")
                                    st["nsp"] += 1
                                    if ci == 0:
                                        em.wait("vector", "scalar")
                                        for tt in range(2):
                                            t = 2 * tp + tt
                                            em.op("vector", lambda e, tb=tb, t=t:
                                                  e.tensor_tensor(
                                                      tb[:, t, 0:128],
                                                      tb[:, t, 0:128],
                                                      m01[:, t * 128:(t + 1) * 128],
                                                      ALU.mult), tick=(tt == 1))
                                        if tp == 1:
                                            em.snap(f"dm_{sw}", "vector")
                                dmidx = -1
                                if ci == 0:
                                    dmidx = st["nd"]
                                    st["nd"] += 1
                                flush()
                                st["pend"] = (tb, ntb, hl, qoff, qw,
                                              st["nsp"] - 1, nt, dmidx)
                                st["nt"] += 1
                        flush()
                        em.snap(f"kb2f_{sw}", "tensor")

                for g in range(NG):
                    sweep(g, "uz", 2 * g)
                    for hl in range(GH):
                        h = 4 * g + hl
                        hc = hl * TOK
                        em.wait("vector", "tensor")
                        em.op("vector", lambda e, h=h, hc=hc: e.tensor_tensor(
                            prod[:], qbuf[0:64, h * TOK:(h + 1) * TOK],
                            uza[0:64, hc:hc + TOK], ALU.mult), tick=True)
                        for c in range(4):
                            ur = urp[c % 2]
                            em.wait("tensor", "vector")
                            em.wait_snap("tensor", "scalar", "ur_evac",
                                         state["nur"] - 2)
                            em.op("tensor", lambda e, ur=ur, c=c: e.matmul(
                                ur[:], ones_c_b[:], prod[0:64, c * 512:(c + 1) * 512],
                                start=True, stop=True), tick=True)
                            em.wait("scalar", "tensor")
                            em.op("scalar", lambda e, ur=ur, c=c: e.copy(
                                rowA[:, c * 512:(c + 1) * 512],
                                ur[:]), tick=True)
                            em.snap("ur_evac", "scalar")
                            state["nur"] += 1
                        em.wait("sync", "scalar")
                        em.dma("sync", "fin", lambda e, hl=hl: e.dma_start(
                            upart[hl * 32:(hl + 1) * 32, :], rowA[:]))
                    em.wait("sync", "vector")
                    em.dma("sync", "fin", lambda e: e.dma_start(
                        zpart[:], uza[64:65, :]))
                    em.wait("vector", "d:fin")
                    em.wait("scalar", "d:fin")
                    em.op("vector", lambda e: e.reciprocal(tA[:], zpart[:]),
                          tick=True)
                    em.op("scalar", lambda e: e.activation(
                        tC[:], zpart[:], AF.Ln), tick=True)
                    em.wait("vector", "scalar")
                    em.op("vector", lambda e: e.tensor_tensor(
                        tA[:], upart[:], tA[:], ALU.mult))
                    em.op("vector", lambda e: e.tensor_tensor(
                        tA[:], tC[:], tA[:], ALU.subtract))
                    em.op("vector", lambda e: e.tensor_tensor(
                        tB[:], tA[:], tA[:], ALU.mult))
                    em.op("vector", lambda e: e.tensor_scalar(
                        tC[:], tB[:], POLY[0], POLY[2], ALU.mult, ALU.add))
                    em.op("vector", lambda e: e.tensor_tensor(
                        tC[:], tC[:], tB[:], ALU.mult))
                    em.op("vector", lambda e: e.tensor_scalar(
                        tB[:], tB[:], POLY[1], POLY[3], ALU.mult, ALU.add))
                    em.op("vector", lambda e: e.tensor_tensor(
                        tB[:], tB[:], tA[:], ALU.mult))
                    em.op("vector", lambda e: e.tensor_tensor(
                        tC[:], tC[:], tB[:], ALU.add))
                    em.op("vector", lambda e: e.tensor_scalar(
                        tC[:], tC[:], POLY[4], 1.0, ALU.add, ALU.max))
                    em.op("vector", lambda e: e.tensor_scalar(
                        tB[:], tA[:], 0.5, None, ALU.is_gt, ALU.bypass))
                    em.op("vector", lambda e: e.tensor_scalar(
                        tC[:], tC[:], -1.0, None, ALU.add, ALU.bypass))
                    em.op("vector", lambda e: e.tensor_tensor(
                        tC[:], tB[:], tC[:], ALU.mult))
                    em.op("vector", lambda e: e.tensor_scalar(
                        tC[:], tC[:], 1.0, None, ALU.add, ALU.bypass), tick=True)
                    for hl in range(GH):
                        h = 4 * g + hl
                        em.wait("sync", "vector")
                        em.wait("sync", "tensor")
                        em.dma("sync", "finb", lambda e, hl=hl: e.dma_start(
                            betaro[:], tC[hl * 32:(hl + 1) * 32, :]))
                        em.wait("tensor", "d:finb")
                        for c in range(4):
                            bc = bcp[c % 2]
                            em.wait_snap("tensor", "vector", "bcu", state["nbc"] - 2)
                            em.op("tensor", lambda e, bc=bc, c=c: e.matmul(
                                bc[:], ones_r_f[0:1, 0:64],
                                betaro[:, c * 512:(c + 1) * 512],
                                start=True, stop=True), tick=True)
                            em.wait("vector", "tensor")
                            em.op("vector", lambda e, bc=bc, h=h, c=c: e.tensor_tensor(
                                qbuf[0:64, h * TOK + c * 512:h * TOK + (c + 1) * 512],
                                qbuf[0:64, h * TOK + c * 512:h * TOK + (c + 1) * 512],
                                bc[:], ALU.mult), tick=True)
                            em.snap("bcu", "vector")
                            state["nbc"] += 1
                    em.wait("tensor", "vector")
                    sweep(g, "av", 2 * g + 1)
                    em.wait("sync", "vector")
                    em.dma("sync", "fin", lambda e: e.dma_start(
                        zpart[:], uza[64:65, :]))
                    em.wait("vector", "d:fin")
                    em.op("vector", lambda e: e.reciprocal(tA[:], zpart[:]),
                          tick=True)
                    for hl in range(GH):
                        h = 4 * g + hl
                        hc = hl * TOK
                        em.wait("sync", "vector")
                        em.wait("sync", "tensor")
                        em.dma("sync", "finb", lambda e, hl=hl: e.dma_start(
                            betaro[:], tA[hl * 32:(hl + 1) * 32, :]))
                        em.wait("tensor", "d:finb")
                        for c in range(4):
                            bc = bcp[c % 2]
                            em.wait_snap("tensor", "vector", "bcu", state["nbc"] - 2)
                            em.op("tensor", lambda e, bc=bc, c=c: e.matmul(
                                bc[:], ones_r_f[0:1, 0:64],
                                betaro[:, c * 512:(c + 1) * 512],
                                start=True, stop=True), tick=True)
                            em.wait("vector", "tensor")
                            em.op("vector", lambda e, bc=bc, h=h, hc=hc, c=c: e.tensor_tensor(
                                ntbig[(h % 2) * 64:(h % 2) * 64 + 64,
                                      (h // 2) * TOK + c * 512:(h // 2) * TOK + (c + 1) * 512],
                                uza[0:64, hc + c * 512:hc + (c + 1) * 512],
                                bc[:], ALU.mult), tick=True)
                            em.snap("bcu", "vector")
                            state["nbc"] += 1
                while state["ck"] < len(conv_chunks):
                    conv_step(2)
                conv_last(em)

            em.run("attn_b", attn_b)
        stackQ.close()
        stackAcc = ExitStack()
        acctok = stackAcc.enter_context(
            nc.sbuf_tensor("acctok", [128, NSLOT, DM], BF16, side="right"))

        with ExitStack() as es:
            psb = lambda *a: es.enter_context(nc.psum_tensor(*a))
            lsb = lambda *a: es.enter_context(nc.sbuf_tensor(*a))
            wps = [psb(f"wps{i}", [128, 512], F32) for i in range(2)]
            atp = [psb(f"atp{i}", [128, 512], F32) for i in range(2)]
            afs = [lsb(f"afs{i}", [128, 512], F32) for i in range(2)]

            def wo_phase(em):
                for dc in range(8):
                    for tc in range(4):
                        it = dc * 4 + tc
                        wp = wps[it % 2]
                        em.wait_snap("tensor", "scalar", "wp_evac", it - 2)
                        for pc in range(8):
                            em.op("tensor", lambda e, wp=wp, pc=pc, dc=dc, tc=tc: e.matmul(
                                wp[:], wot[:, pc, dc * 128:(dc + 1) * 128],
                                ntbig[:, pc * TOK + tc * 512:pc * TOK + (tc + 1) * 512],
                                start=(pc == 0), stop=(pc == 7)), tick=(pc == 7))
                        af = afs[it % 2]
                        em.wait("scalar", "tensor")
                        em.wait_snap("scalar", "tensor", "afs_used", it - 2)
                        em.op("scalar", lambda e, wp=wp, af=af, dc=dc: e.activation(
                            af[:], wp[:], AF.Identity, bias=bot[:, dc:dc + 1]),
                            tick=True)
                        em.snap("wp_evac", "scalar")
                        at = atp[it % 2]
                        em.wait("tensor", "scalar")
                        em.wait_snap("tensor", "scalar", "atp_evac", it - 2)
                        for s in range(4):
                            em.op("tensor", lambda e, at=at, af=af, s=s: e.transpose(
                                at[:, s * 128:(s + 1) * 128],
                                af[:, s * 128:(s + 1) * 128], idf[:]), tick=(s == 3))
                        em.snap("afs_used", "tensor")
                        em.wait("scalar", "tensor")
                        em.op("scalar", lambda e, at=at, dc=dc, tc=tc: e.copy(
                            acctok[:, tc * 4:tc * 4 + 4, dc * 128:(dc + 1) * 128],
                            at[:].rearrange("p (t q) -> p t q", t=4)), tick=True)
                        em.snap("atp_evac", "scalar")

            em.run("wo", wo_phase)
        stackC.close()
        stackW.close()

        with ExitStack() as es:
            psb = lambda *a: es.enter_context(nc.psum_tensor(*a))
            lsb = lambda *a: es.enter_context(nc.sbuf_tensor(*a))
            w1g = [lsb(f"w1g{i}", [128, 8, 2048], BF16) for i in range(2)]
            w2g = [lsb(f"w2g{i}", [128, 8, 1024], BF16) for i in range(2)]
            xgr = [lsb(f"xgr{i}", [128, DM], BF16) for i in range(2)]
            xnt = [lsb(f"xnt{i}", [128, 8, 128], BF16) for i in range(2)]
            b1r = [lsb(f"b1raw{i}", [64, 128], F32) for i in range(2)]
            b1tA = lsb("b1tA", [128, NTILE, 64], F32)
            gmt = [lsb(f"gmt{i}", [128, 1], I32) for i in range(2)]
            ysacc = lsb("ysacc", [128, NTILE, DM], BF16)
            sgb = [lsb(f"sgb{i}", [128, 128], BF16) for i in range(2)]
            ubf = [lsb(f"ubf{i}", [128, 128], BF16) for i in range(2)]
            yps = [psb(f"yps{i}", [128, DM], F32) for i in range(2)]
            agp = [psb(f"agp{i}", [128, 256], F32) for i in range(2)]
            xtp = [psb(f"xtp{i}", [128, 128], BF16) for i in range(2)]

            regs = {}

            def ffn(em):
                if em.mode == "gpsimd" and "b" not in regs:
                    regs["b"] = em.eng.to_reg(32767)
                state = dict(nhc=0)
                em.dma("gpsimd", "gmt0", lambda e: e.dma_start(
                    gmt[0][:], gmap[0:128, :]))
                for qt in range(4):
                    for t in range(NTILE):
                        nq = qt * NTILE + t
                        par = t % 2
                        em.wait("gpsimd", f"d:gmt{par}")
                        em.wait_snap("gpsimd", "tensor", "xgr_used", nq - 2)
                        em.dma("gpsimd", f"xg{par}", lambda e, par=par: e.indirect_dma_start(
                            xgr[par][:], None, xtk,
                            bass.IndirectOffsetOnAxis(gmt[par][:], 0),
                            bounds_check=regs["b"], oob_is_err=False))
                        if qt == 0:
                            em.wait_snap("gpsimd", "tensor", "b1r_used", t - 2)
                            em.dma("gpsimd", f"b1g{par}", lambda e, par=par, t=t: e.indirect_dma_start(
                                b1r[par][:], None, b1flat,
                                bass.IndirectOffsetOnAxis(widxb1I[:, t:t + 1], 0)))
                        em.wait_snap("gpsimd", "tensor", "wg_free", nq - 2)
                        em.dma("gpsimd", f"wg{par}",
                               lambda e, par=par, t=t, qt=qt: e.indirect_dma_start(
                                   w1g[par][:].rearrange("p c j -> p (c j)"),
                                   None, w1b,
                                   bass.IndirectOffsetOnAxis(
                                       widxQ[:, qt, t:t + 1], 0),
                                   bounds_check=regs["b"], oob_is_err=False))
                        em.dma("gpsimd", f"wg{par}",
                               lambda e, par=par, t=t, qt=qt: e.indirect_dma_start(
                                   w2g[par][:].rearrange("p c j -> p (c j)"),
                                   None, w2b,
                                   bass.IndirectOffsetOnAxis(
                                       widxQ[:, qt, t:t + 1], 0),
                                   bounds_check=regs["b"], oob_is_err=False))
                        if nq + 1 < 4 * NTILE:
                            tn = (t + 1) % NTILE
                            em.dma("gpsimd", f"gmt{(t + 1) % 2}",
                                   lambda e, tn=tn: e.dma_start(
                                       gmt[tn % 2][:],
                                       gmap[tn * 128:(tn + 1) * 128, :]))
                        em.wait("tensor", f"d:xg{par}")
                        for c in range(8):
                            nxt = nq * 8 + c
                            em.wait_snap("tensor", "scalar", "xtp_evac", nxt - 2)
                            em.op("tensor", lambda e, c=c, par=par: e.transpose(
                                xtp[c % 2][:], xgr[par][:, c * 128:(c + 1) * 128],
                                idb[:]), tick=True)
                            em.wait("scalar", "tensor")
                            if c == 0:
                                em.wait_snap("scalar", "tensor", "xnt_used",
                                             nq - 2)
                            em.op("scalar", lambda e, c=c, par=par: e.copy(
                                xnt[par][:, c, :], xtp[c % 2][:]), tick=True)
                            em.snap("xtp_evac", "scalar")
                        em.snap("xgr_used", "tensor")
                        if qt == 0:
                            em.wait("tensor", f"d:b1g{par}")
                            em.wait_snap("tensor", "vector", "ag_free",
                                         state["nhc"] - 1)
                            em.op("tensor", lambda e, par=par: e.transpose(
                                agp[par][:, 0:64], b1r[par][:], idf[0:64, 0:64]),
                                tick=True)
                            em.snap("b1r_used", "tensor")
                            em.wait("scalar", "tensor")
                            em.op("scalar", lambda e, par=par, t=t: e.copy(
                                b1tA[:, t, :], agp[par][:, 0:64]), tick=True)
                        em.wait("tensor", f"d:wg{par}")
                        em.wait("tensor", "scalar")

                        def emit_y(hc):
                            ubp = ubf[hc % 2]
                            if hc == 0:
                                em.wait_snap("tensor", "vector", "yps_free",
                                             nq - 2)
                            em.wait("tensor", "vector")
                            for half in range(2):
                                em.op("tensor", lambda e, half=half, ubp=ubp,
                                      par=par, hc=hc: e.matmul(
                                          yps[par][:, half * 512:(half + 1) * 512],
                                          ubp[:], w2g[par][:, hc, half * 512:
                                                           (half + 1) * 512],
                                          start=(hc == 0), stop=(hc == 7)),
                                      tick=(half == 1))
                            em.snap("ub_used", "tensor")

                        for hc in range(8):
                            nhc = state["nhc"]
                            agq = agp[nhc % 2]
                            em.wait_snap("tensor", "vector", "ag_free", nhc - 2)
                            for c in range(8):
                                em.op("tensor", lambda e, agq=agq, par=par, c=c,
                                      hc=hc: e.matmul(
                                          agq[:, 0:128],
                                          w1g[par][:, c, hc * 128:(hc + 1) * 128],
                                          xnt[par][:, c, :],
                                          start=(c == 0), stop=(c == 7)),
                                      tick=(c == 7))
                            for c in range(8):
                                em.op("tensor", lambda e, agq=agq, par=par, c=c,
                                      hc=hc: e.matmul(
                                          agq[:, 128:256],
                                          w1g[par][:, c, 1024 + hc * 128:
                                                   1024 + (hc + 1) * 128],
                                          xnt[par][:, c, :],
                                          start=(c == 0), stop=(c == 7)),
                                      tick=(c == 7))
                            sg = sgb[nhc % 2]
                            ub = ubf[nhc % 2]
                            em.wait("scalar", "tensor")
                            em.wait_snap("scalar", "vector", "sg_used", nhc - 2)
                            em.op("scalar", lambda e, agq=agq, sg=sg, qt=qt,
                                  hc=hc, t=t: e.activation(
                                      sg[:], agq[:, 128:256], AF.Sigmoid,
                                      bias=b1tA[:, t, 32 + qt * 8 + hc:
                                                33 + qt * 8 + hc]),
                                  tick=True)
                            if hc > 0:
                                emit_y(hc - 1)
                            em.wait("vector", "scalar")
                            em.wait_snap("vector", "tensor", "ub_used", nhc - 2)
                            em.op("vector", lambda e, agq=agq, sg=sg, ub=ub,
                                  qt=qt, hc=hc, t=t: e.scalar_tensor_tensor(
                                      ub[:], agq[:, 0:128],
                                      b1tA[:, t, qt * 8 + hc:qt * 8 + hc + 1],
                                      sg[:], ALU.add, ALU.mult), tick=True)
                            em.snap("ag_free", "vector")
                            em.snap("sg_used", "vector")
                            state["nhc"] += 1
                        emit_y(7)
                        em.snap("wg_free", "tensor")
                        if qt == 3:
                            em.snap("xnt_used", "tensor")
                        else:
                            em.snap("xnt_used", "tensor")
                        em.wait("vector", "tensor")
                        if qt == 0:
                            em.op("vector", lambda e, par=par, t=t: e.tensor_copy(
                                ysacc[:, t, :], yps[par][:]), tick=True)
                        else:
                            em.op("vector", lambda e, par=par, t=t: e.tensor_tensor(
                                ysacc[:, t, :], ysacc[:, t, :], yps[par][:],
                                ALU.add), tick=True)
                        em.snap("yps_free", "vector")
                em.wait("gpsimd", "vector")
                for t in range(NTILE):
                    par = t % 2
                    em.dma("gpsimd", f"gmt{par}", lambda e, par=par, t=t: e.dma_start(
                        gmt[par][:], gmap[t * 128:(t + 1) * 128, :]))
                    em.wait("gpsimd", f"d:gmt{par}")
                    em.dma("gpsimd", f"ys{par}", lambda e, par=par, t=t: e.indirect_dma_start(
                        ytok, bass.IndirectOffsetOnAxis(gmt[par][:], 0),
                        ysacc[:, t, :], None,
                        bounds_check=regs["b"], oob_is_err=False))
                em.wait("gpsimd", "d:ys0")
                em.wait("gpsimd", "d:ys1")

            em.run("ffn", ffn)

        with ExitStack() as es:
            psb = lambda *a: es.enter_context(nc.psum_tensor(*a))
            lsb = lambda *a: es.enter_context(nc.sbuf_tensor(*a))
            xo = [lsb(f"xo{i}", [128, DM], F32) for i in range(2)]
            ytb = [lsb(f"ytb{i}", [128, DM], BF16) for i in range(2)]
            mTs = [lsb(f"mTs{i}", [8, 128], F32) for i in range(2)]
            ob = [lsb(f"ob{i}", [128, DM], F32) for i in range(2)]
            b2S = lsb("b2S", [8, DM], F32)
            b2ps = [psb(f"b2ps{i}", [128, DM], F32) for i in range(2)]
            mtp2 = psb("mtp2", [8, 128], F32)

            def out_phase(em):
                em.dma("sync", "b2l", lambda e: e.dma_start(b2S[:], b2))
                for i in range(NSLOT):
                    xb = xo[i % 2]
                    yb = ytb[i % 2]
                    em.wait_snap("sync", "vector", "xo_free", i - 2)
                    em.dma("sync", f"xo{i % 2}", lambda e, xb=xb, i=i: e.dma_start(
                        xb[:], xe[(4 * i) * 128:(4 * i + 1) * 128, :]))
                    em.dma("sync", f"xo{i % 2}", lambda e, yb=yb, i=i: e.dma_start(
                        yb[:], ytok[i * 128:(i + 1) * 128, :]))
                    em.wait_snap("tensor", "scalar", "mts_evac", i - 1)
                    em.op("tensor", lambda e, i=i: e.transpose(
                        mtp2[:], maskS[:].rearrange("p (e i) -> p i e", i=NSLOT)
                        [:, i, :], idf[:]), tick=True)
                    em.wait("scalar", "tensor")
                    em.op("scalar", lambda e, i=i: e.copy(
                        mTs[i % 2][:], mtp2[:]), tick=True)
                    em.snap("mts_evac", "scalar")
                    em.wait("tensor", "scalar")
                    if i == 0:
                        em.wait("tensor", "d:b2l")
                    em.wait_snap("tensor", "vector", "b2ps_used", i - 2)
                    for half in range(2):
                        em.op("tensor", lambda e, i=i, half=half: e.matmul(
                            b2ps[i % 2][:, half * 512:(half + 1) * 512],
                            mTs[i % 2][:], b2S[:, half * 512:(half + 1) * 512],
                            start=True, stop=True), tick=(half == 1))
                    o = ob[i % 2]
                    em.wait("vector", f"d:xo{i % 2}")
                    em.wait("vector", "tensor")
                    em.wait_snap("vector", f"d:yst{i % 2}", "ob_sent", i - 2)
                    em.op("vector", lambda e, xb=xb, o=o, i=i: e.tensor_tensor(
                        o[:], xb[:], acctok[:, i, :], ALU.add), tick=True)
                    em.op("vector", lambda e, o=o, yb=yb: e.tensor_tensor(
                        o[:], o[:], yb[:], ALU.add), tick=True)
                    em.op("vector", lambda e, o=o, i=i: e.tensor_tensor(
                        o[:], o[:], b2ps[i % 2][:], ALU.add), tick=True)
                    em.snap("xo_free", "vector")
                    em.snap("b2ps_used", "vector")
                    em.wait("sync", "vector")
                    em.dma("sync", f"yst{i % 2}", lambda e, o=o, i=i: e.dma_start(
                        y[i * 128:(i + 1) * 128, :], o[:]))
                    em.snap("ob_sent", f"d:yst{i % 2}")
                em.wait("sync", "d:yst0")
                em.wait("sync", "d:yst1")

            em.run("out", out_phase)
        stackAcc.close()

    return nc


def _perm_and_masks(j):
    sig = [j] + [m for m in range(4) if m != j]
    tile_order = []
    for u in range(16):
        for r in range(4):
            tile_order.append(4 * u + sig[r])
    m1 = np.zeros((128, 512), np.float32)
    qi = np.arange(128)[:, None]
    ki = np.arange(128)[None, :]
    for r in range(4):
        if r == 0:
            m1[:, 0:128] = np.where(ki > qi, NEG, 0.0)
        elif sig[r] > j:
            m1[:, r * 128:(r + 1) * 128] = NEG
    m2 = np.zeros((128, 512), np.float32)
    for t in range(4):
        m2[:, t * 128:(t + 1) * 128] = m1[:, t * 128:(t + 1) * 128].T
    return tile_order, m1, m2


def kernel(L0_attn_x, L0_attn_Wq, L0_attn_Wk, L0_attn_Wv, L0_attn_Wo,
           L0_attn_bo, L0_ffn_router_w, L0_ffn_router_b, L0_ffn_w1,
           L0_ffn_b1, L0_ffn_w2, L0_ffn_b2):
    x = np.ascontiguousarray(np.asarray(L0_attn_x, dtype=np.float32))
    args = dict(
        wq=np.ascontiguousarray(np.asarray(L0_attn_Wq, np.float32)),
        wk=np.ascontiguousarray(np.asarray(L0_attn_Wk, np.float32)),
        wv=np.ascontiguousarray(np.asarray(L0_attn_Wv, np.float32)),
        wo=np.ascontiguousarray(np.asarray(L0_attn_Wo, np.float32)),
        bo=np.ascontiguousarray(np.asarray(L0_attn_bo, np.float32)),
        rw=np.ascontiguousarray(np.asarray(L0_ffn_router_w, np.float32)),
        rb=np.ascontiguousarray(np.asarray(L0_ffn_router_b, np.float32)),
        w1=np.ascontiguousarray(
            np.asarray(L0_ffn_w1, np.float32).reshape(8, 8, 128, 2, 4, 1024)
            .transpose(0, 4, 2, 1, 3, 5).reshape(4096, 16384)),
        b1=np.ascontiguousarray(np.asarray(L0_ffn_b1, np.float32)),
        w2=np.ascontiguousarray(
            np.asarray(L0_ffn_w2, np.float32).reshape(8, 4, 8, 128, 1024)
            .transpose(0, 1, 3, 2, 4).reshape(4096, 8192)),
        b2=np.ascontiguousarray(np.asarray(L0_ffn_b2, np.float32)),
    )
    nc = build_nc()
    in_maps = []
    metas = []
    for c in range(8):
        b, j = c // 4, c % 4
        order, m1, m2 = _perm_and_masks(j)
        row_idx = np.concatenate(
            [np.arange(t * 128, (t + 1) * 128) for t in order])
        im = dict(args)
        im["xe"] = np.ascontiguousarray(x[b][row_idx])
        im["m1"] = m1
        im["m2"] = m2
        in_maps.append(im)
        metas.append((b, j))
    res = run_bass_kernel_spmd(nc, in_maps, core_ids=list(range(8)))
    out = np.empty((2, 8192, 1024), np.float32)
    for c in range(8):
        b, j = metas[c]
        yc = res.results[c]["y"]
        for i in range(NSLOT):
            t = 4 * i + j
            out[b, t * 128:(t + 1) * 128, :] = yc[i * 128:(i + 1) * 128, :]
    return out


# version 15
# speedup vs baseline: 1.1041x; 1.1041x over previous
import numpy as np
from contextlib import ExitStack

import concourse.bass as bass
import concourse.mybir as mybir
from concourse.bass_utils import run_bass_kernel_spmd
from concourse.alu_op_type import AluOpType as ALU

F32 = mybir.dt.float32
BF16 = mybir.dt.bfloat16
I32 = mybir.dt.int32
AF = mybir.ActivationFunctionType
AX = mybir.AxisListType

DM = 1024
NH = 16
HD = 64
NT = 64
NU = 16
NSLOT = 16
TOK = 2048
NE = 8
HID = 4096
NTILE = 24
NG = 4
GH = 4
POLY = [-0.037, 0.481, -2.3, 4.917, -1.791]
NEG = -1.0e30


class Emitter:

    def __init__(self, nc):
        self.nc = nc

    def _reset(self):
        self.cnt = dict(sync=0, gpsimd=0, tensor=0, vector=0, scalar=0)
        self.waits = []
        self.snaps = {}
        self.wi = 0

    CHAINED = ("vector", "scalar", "gpsimd")

    def op(self, eng, fn, tick=False):
        chained = eng in self.CHAINED
        if self.mode == "count":
            if tick or chained:
                self.cnt[eng] += 1
            return
        if self.mode == eng:
            if chained:
                n = self.ecnt.get(eng, 0)
                if n > 0:
                    self.eng.wait_ge(self.sems[eng], n)
                self.ecnt[eng] = n + 1
            inst = fn(self.eng)
            if tick or chained:
                inst.then_inc(self.sems[eng], 1)

    def dma(self, queue, stream, fn):
        key = "d:" + stream
        if self.mode == "count":
            self.cnt[key] = self.cnt.get(key, 0) + 16
            return
        if self.mode == queue:
            fn(self.eng).then_inc(self.sems[key], 16)

    def wait(self, waiter, src):
        if self.mode == "count":
            self.waits.append((src, self.cnt.get(src, 0)))
            return
        src_, val = self.waits[self.wi]
        self.wi += 1
        if self.mode == waiter and val > 0:
            self.eng.wait_ge(self.sems[src_], val)

    def snap(self, key, src):
        if self.mode == "count":
            self.snaps.setdefault(key, []).append(self.cnt.get(src, 0))

    def wait_snap(self, waiter, src, key, idx):
        if idx < 0:
            return
        if self.mode == "count":
            self.waits.append((src, ("s", key, idx)))
            return
        src_, val = self.waits[self.wi]
        self.wi += 1
        if isinstance(val, tuple):
            val = self.snaps[val[1]][val[2]]
        if self.mode == waiter and val > 0:
            self.eng.wait_ge(self.sems[src_], val)

    def run(self, name, phase_fn):
        nc = self.nc
        self.mode = "count"
        self._reset()
        phase_fn(self)
        self.waits = [(s, self.snaps[v[1]][v[2]] if isinstance(v, tuple) else v)
                      for (s, v) in self.waits]
        with ExitStack() as es:
            keys = ["sync", "gpsimd", "tensor", "vector", "scalar"]
            keys += [k for k in self.cnt if k.startswith("d:")]
            sems = {k: es.enter_context(nc.semaphore(f"{name}_{k}".replace(":", "_")))
                    for k in keys}
            block = es.enter_context(nc.Block(name))

            def mk(eng_name):
                def body(eng_obj):
                    self.mode = eng_name
                    self.eng = eng_obj
                    self.sems = sems
                    self.wi = 0
                    self.ecnt = {}
                    phase_fn(self)
                return body

            block.sync(mk("sync"))
            block.gpsimd(mk("gpsimd"))
            block.tensor(mk("tensor"))
            block.vector(mk("vector"))
            block.scalar(mk("scalar"))


def build_nc():
    nc = bass.Bass("TRN2", target_bir_lowering=False)

    xe = nc.dram_tensor("xe", [8192, DM], F32, kind="ExternalInput").ap()
    wq = nc.dram_tensor("wq", [DM, DM], F32, kind="ExternalInput").ap()
    wk = nc.dram_tensor("wk", [DM, DM], F32, kind="ExternalInput").ap()
    wv = nc.dram_tensor("wv", [DM, DM], F32, kind="ExternalInput").ap()
    wo = nc.dram_tensor("wo", [DM, DM], F32, kind="ExternalInput").ap()
    bo = nc.dram_tensor("bo", [DM], F32, kind="ExternalInput").ap()
    rw = nc.dram_tensor("rw", [DM, NE], F32, kind="ExternalInput").ap()
    rb = nc.dram_tensor("rb", [NE], F32, kind="ExternalInput").ap()
    w1 = nc.dram_tensor("w1", [NE * 4 * 128, 8 * 2048], F32, kind="ExternalInput").ap()
    b1 = nc.dram_tensor("b1", [NE, 2 * HID], F32, kind="ExternalInput").ap()
    w2 = nc.dram_tensor("w2", [NE * 4 * 128, 8 * 1024], F32, kind="ExternalInput").ap()
    b2 = nc.dram_tensor("b2", [NE, DM], F32, kind="ExternalInput").ap()
    m1 = nc.dram_tensor("m1", [128, 512], F32, kind="ExternalInput").ap()
    m2 = nc.dram_tensor("m2", [128, 512], F32, kind="ExternalInput").ap()
    y = nc.dram_tensor("y", [TOK, DM], F32, kind="ExternalOutput").ap()

    xnf = nc.dram_tensor("xnf", [8, 128, 8192], BF16, kind="Internal").ap()
    kfm = nc.dram_tensor("kfm", [NU, 65, NH * 512], BF16, kind="Internal").ap()
    ktk = nc.dram_tensor("ktk", [NU, 128, 4, NH * 65], BF16, kind="Internal").ap()
    vtk = nc.dram_tensor("vtk", [NU, 128, 4, NH * 65], BF16, kind="Internal").ap()
    xtk = nc.dram_tensor("xtk", [TOK, DM], BF16, kind="Internal").ap()
    gmap = nc.dram_tensor("gmap", [NTILE * 128, 1], I32, kind="Internal").ap()
    ytok = nc.dram_tensor("ytok", [TOK, DM], BF16, kind="Internal").ap()
    w1b = nc.dram_tensor("w1b", [NE * 4 * 128, 8 * 2048], BF16, kind="Internal").ap()
    w2b = nc.dram_tensor("w2b", [NE * 4 * 128, 8 * 1024], BF16, kind="Internal").ap()
    b1flat = b1.rearrange("e (r f) -> (e r) f", f=128)

    em = Emitter(nc)
    outer = ExitStack()
    sb = lambda *a: outer.enter_context(nc.sbuf_tensor(*a))

    m2s = sb("m2s", [128, 512], F32)
    idb = sb("idb", [128, 128], BF16)
    m01 = sb("m01", [128, 512], BF16)
    idf = sb("idf", [128, 128], F32)
    ones_r_b = sb("ones_r_b", [1, 128], BF16)
    ones_r_f = sb("ones_r_f", [1, 128], F32)
    ones_c_f = sb("ones_c_f", [64, 1], F32)
    ones_c_b = sb("ones_c_b", [64, 1], BF16)
    rwS = sb("rwS", [128, 8, NE], F32)
    rbS = sb("rbS", [1, NE], F32)
    maskS = sb("maskS", [128, NSLOT * NE], F32)
    Lstrict = sb("Lstrict", [128, 128], F32)
    Acomb = sb("Acomb", [128, 128], F32)
    Bmat8 = sb("Bmat8", [8, 128], F32)
    Sie = sb("Sie", [128, 8], F32)
    thr8f = sb("thr8f", [8, 16], F32)
    t128r = sb("t128r", [8, NTILE], F32)
    ones_c128 = sb("ones_c128", [128, 1], F32)
    rowofff = sb("rowofff", [128, 1], F32)
    rowoffq = sb("rowoffq", [128, 4], F32)
    tokid = sb("tokid", [128, NSLOT], I32)
    widxQ = sb("widxQ", [128, 4, NTILE], I32)
    widxb1I = sb("widxb1I", [64, NTILE], I32)
    stackA = ExitStack()
    sbA = lambda *a: stackA.enter_context(nc.sbuf_tensor(*a))
    stackW = ExitStack()
    wot = stackW.enter_context(nc.sbuf_tensor("wot", [128, 8, DM], BF16))
    bot = stackW.enter_context(nc.sbuf_tensor("bot", [128, 8], F32))
    wqt = sbA("wqt", [128, 8, DM], BF16)
    wkt = sbA("wkt", [128, 8, DM], BF16)
    wvt = sbA("wvt", [128, 8, DM], BF16)
    xnown = sbA("xnown", [128, 8, TOK], BF16)

    with outer:
        with ExitStack() as es:
            psb = lambda *a: es.enter_context(nc.psum_tensor(*a))
            lsb = lambda *a: es.enter_context(nc.sbuf_tensor(*a))
            iti = lsb("iti", [128, 128], I32)
            wrow = [lsb(f"wrow{i}", [128, DM], F32) for i in range(2)]
            bo8 = lsb("bo8", [8, 128], F32)
            it8 = lsb("it8", [8, 128], I32)
            ithr = lsb("ithr", [8, 16], I32)
            it24 = lsb("it24", [8, NTILE], I32)
            irow = lsb("irow", [128, 1], I32)
            irq = lsb("irq", [128, 4], I32)
            s16b = lsb("s16b", [8, 128], F32)
            b16sb = lsb("b16sb", [128, 128], F32)
            trp = [psb(f"trp{i}", [128, 128], F32) for i in range(2)]

            def prep(em):
                em.dma("sync", "cst", lambda e: e.dma_start(m2s[:], m2))
                em.dma("sync", "cst", lambda e: e.dma_start(rbS[:], rb[None, :]))
                for c in range(8):
                    em.dma("sync", "cst", lambda e, c=c: e.dma_start(
                        rwS[:, c, :], rw[c * 128:(c + 1) * 128, :]))
                em.dma("sync", "cst", lambda e: e.dma_start(
                    bo8[:], bo.rearrange("(c p) -> c p", p=128)))
                em.op("gpsimd", lambda e: e.iota(
                    iti[:], pattern=[[1, 128]], base=0, channel_multiplier=-1),
                    tick=True)
                em.op("gpsimd", lambda e: e.memset(ones_r_b[:], 1.0), tick=True)
                em.op("gpsimd", lambda e: e.memset(ones_r_f[:], 1.0), tick=True)
                em.op("gpsimd", lambda e: e.memset(ones_c_f[:], 1.0), tick=True)
                em.op("gpsimd", lambda e: e.memset(ones_c_b[:], 1.0), tick=True)
                em.wait("vector", "gpsimd")
                em.op("vector", lambda e: e.tensor_scalar(
                    idb[:], iti[:], 0, None, ALU.is_equal, ALU.bypass), tick=True)
                em.wait("vector", "d:cst")
                em.op("vector", lambda e: e.tensor_scalar(
                    m01[:], m2s[:], 0, None, ALU.is_equal, ALU.bypass), tick=True)
                em.op("vector", lambda e: e.tensor_scalar(
                    idf[:], iti[:], 0, None, ALU.is_equal, ALU.bypass), tick=True)

                wlist = [(wq, wqt), (wk, wkt), (wv, wvt), (wo, wot)]
                it = 0
                for wsrc, wdst in wlist:
                    for r in range(8):
                        buf = wrow[it % 2]
                        em.wait_snap("sync", "tensor", "wrow_free", it - 2)
                        em.dma("sync", f"wrow{it % 2}", lambda e, buf=buf, wsrc=wsrc, r=r: e.dma_start(
                            buf[:], wsrc[r * 128:(r + 1) * 128, :]))
                        em.wait("tensor", f"d:wrow{it % 2}")
                        em.wait("tensor", "vector")
                        for c in range(8):
                            pt = trp[c % 2]
                            em.wait_snap("tensor", "scalar", "wtr_evac",
                                         it * 8 + c - 2)
                            em.op("tensor", lambda e, pt=pt, buf=buf, c=c: e.transpose(
                                pt[:], buf[:, c * 128:(c + 1) * 128], idf[:]),
                                tick=True)
                            em.wait("scalar", "tensor")
                            em.op("scalar", lambda e, pt=pt, wdst=wdst, c=c, r=r: e.copy(
                                wdst[:, c, r * 128:(r + 1) * 128], pt[:]), tick=True)
                            em.snap("wtr_evac", "scalar")
                        em.snap("wrow_free", "tensor")
                        it += 1
                em.wait("tensor", "d:cst")
                em.wait_snap("tensor", "scalar", "wtr_evac", 8 * 32 - 2)
                em.op("tensor", lambda e: e.transpose(
                    trp[0][0:128, 0:8], bo8[:], idf[0:8, 0:8]), tick=True)
                em.wait("scalar", "tensor")
                em.op("scalar", lambda e: e.copy(bot[:], trp[0][0:128, 0:8]),
                      tick=True)
                em.op("gpsimd", lambda e: e.iota(
                    it8[:].rearrange("p (b j) -> p b j", b=8),
                    pattern=[[1, 8], [0, 16]], base=0, channel_multiplier=-1),
                    tick=True)
                em.op("gpsimd", lambda e: e.iota(
                    ithr[:], pattern=[[128, 16]], base=0, channel_multiplier=0),
                    tick=True)
                em.op("gpsimd", lambda e: e.iota(
                    it24[:], pattern=[[128, NTILE]], base=0,
                    channel_multiplier=0), tick=True)
                em.op("gpsimd", lambda e: e.iota(
                    irow[:], pattern=[[0, 1]], base=0, channel_multiplier=1),
                    tick=True)
                em.op("gpsimd", lambda e: e.iota(
                    irq[:], pattern=[[128, 4]], base=0, channel_multiplier=1),
                    tick=True)
                em.op("gpsimd", lambda e: e.iota(
                    tokid[:], pattern=[[128, NSLOT]], base=0,
                    channel_multiplier=1), tick=True)
                em.op("gpsimd", lambda e: e.memset(ones_c128[:], 1.0), tick=True)
                em.wait("vector", "gpsimd")
                em.op("vector", lambda e: e.tensor_scalar(
                    Lstrict[:], iti[:], 0, None, ALU.is_gt, ALU.bypass), tick=True)
                em.op("vector", lambda e: e.tensor_scalar(
                    s16b[:], it8[:], 0, None, ALU.is_equal, ALU.bypass), tick=True)
                em.op("vector", lambda e: e.tensor_scalar(
                    Bmat8[:], it8[:], 0, None, ALU.is_gt, ALU.bypass), tick=True)
                em.op("vector", lambda e: e.tensor_scalar(
                    thr8f[:], ithr[:], 1.0, None, ALU.mult, ALU.bypass), tick=True)
                em.op("vector", lambda e: e.tensor_scalar(
                    t128r[:], it24[:], 1.0, None, ALU.mult, ALU.bypass), tick=True)
                em.op("vector", lambda e: e.tensor_scalar(
                    rowofff[:], irow[:], 1.0, None, ALU.mult, ALU.bypass), tick=True)
                em.op("vector", lambda e: e.tensor_scalar(
                    rowoffq[:], irq[:], 1.0, None, ALU.mult, ALU.bypass),
                    tick=True)
                em.wait("tensor", "vector")
                em.wait("tensor", "scalar")
                em.op("tensor", lambda e: e.transpose(
                    trp[0][0:128, 0:8], s16b[:], idf[0:8, 0:8]), tick=True)
                em.op("tensor", lambda e: e.matmul(
                    trp[1][:], s16b[:], s16b[:], start=True, stop=True), tick=True)
                em.wait("scalar", "tensor")
                em.op("scalar", lambda e: e.copy(Sie[:], trp[0][0:128, 0:8]),
                      tick=True)
                em.op("scalar", lambda e: e.copy(b16sb[:], trp[1][:]), tick=True)
                em.wait("vector", "scalar")
                em.op("vector", lambda e: e.tensor_tensor(
                    Acomb[:], b16sb[:], Lstrict[:], ALU.mult), tick=True)
                em.wait("sync", "d:cst")

            em.run("prep", prep)

        with ExitStack() as es:
            psb = lambda *a: es.enter_context(nc.psum_tensor(*a))
            lsb = lambda *a: es.enter_context(nc.sbuf_tensor(*a))
            xT = [lsb(f"xT{i}", [128, DM], F32) for i in range(2)]
            ssum = [lsb(f"ssum{i}", [128, 1], F32) for i in range(2)]
            nmean = [lsb(f"nmean{i}", [128, 1], F32) for i in range(2)]
            ssq = [lsb(f"ssq{i}", [128, 1], F32) for i in range(2)]
            vstd = [lsb(f"vstd{i}", [128, 1], F32) for i in range(2)]
            rstd = [lsb(f"rstd{i}", [128, 1], F32) for i in range(2)]
            sqj = lsb("sqj", [128, DM], F32)
            xnb = [lsb(f"xnb{i}", [128, DM], BF16) for i in range(2)]
            xn32 = lsb("xn32", [128, DM], F32)
            xf32 = lsb("xf32", [128, 8, 128], F32)
            xfst = [lsb(f"xfst{i}", [128, 8, 512], BF16) for i in range(2)]
            rmax8 = lsb("rmax8", [128, 1], F32)
            rraw = lsb("rraw", [128, 1], F32)
            ravail = lsb("ravail", [128, 1], F32)
            rtaken = lsb("rtaken", [128, 1], F32)
            trb = psb("trb", [128, DM], BF16)
            trf = psb("trf", [128, DM], F32)
            lp = psb("lp", [128, NE], F32)

            def p0(em):
                def stats(p):
                    q = p % 2
                    buf = xT[q]
                    em.wait("vector", f"d:x{q}")
                    em.op("vector", lambda e, buf=buf, q=q: e.reduce_sum(
                        ssum[q][:], buf[:], axis=AX.X))
                    em.op("vector", lambda e, q=q: e.tensor_scalar(
                        nmean[q][:], ssum[q][:], -1.0 / DM, None, ALU.mult,
                        ALU.bypass), tick=True)
                    em.wait("scalar", "vector")
                    em.op("scalar", lambda e, buf=buf, q=q: e.activation(
                        sqj[:], buf[:], AF.Square, bias=nmean[q][:], scale=1.0),
                        tick=True)
                    em.wait("vector", "scalar")
                    em.op("vector", lambda e, q=q: e.reduce_sum(
                        ssq[q][:], sqj[:], axis=AX.X))
                    em.op("vector", lambda e, q=q: e.tensor_scalar(
                        vstd[q][:], ssq[q][:], 1.0 / DM, 1.0e-5, ALU.mult,
                        ALU.add), tick=True)
                    em.wait("scalar", "vector")
                    em.op("scalar", lambda e, q=q: e.activation(
                        vstd[q][:], vstd[q][:], AF.Sqrt), tick=True)
                    em.wait("vector", "scalar")
                    em.op("vector", lambda e, q=q: e.reciprocal(
                        rstd[q][:], vstd[q][:]), tick=True)

                def apply(p):
                    u, r = p // 4, p % 4
                    own = (r == 0)
                    slot = u
                    q = p % 2
                    buf = xT[q]
                    xb = xnb[q]
                    em.wait_snap("vector", "tensor", "xnb_free", p - 2)
                    em.op("vector", lambda e, buf=buf, xb=xb, q=q: e.tensor_scalar(
                        xb[:], buf[:], nmean[q][:], rstd[q][:], ALU.add,
                        ALU.mult), tick=True)
                    if own:
                        em.wait_snap("vector", "tensor", "trf_done", slot - 1)
                        em.wait_snap("vector", "d:xtk", "xtk_sent", slot - 1)
                        em.op("vector", lambda e, buf=buf, q=q: e.tensor_scalar(
                            xn32[:], buf[:], nmean[q][:], rstd[q][:], ALU.add,
                            ALU.mult), tick=True)
                        em.wait("gpsimd", "vector")
                        em.dma("gpsimd", "xtk", lambda e, slot=slot: e.dma_start(
                            xtk[slot * 128:(slot + 1) * 128, :], xn32[:]))
                        em.snap("xtk_sent", "d:xtk")
                    em.snap("xT_free", "vector")
                    em.wait("tensor", "vector")
                    em.wait_snap("tensor", "scalar", "trb_evac", p - 1)
                    for c in range(8):
                        em.op("tensor", lambda e, xb=xb, c=c: e.transpose(
                            trb[:, c * 128:(c + 1) * 128],
                            xb[:, c * 128:(c + 1) * 128], idb[:]), tick=(c == 7))
                    em.snap("xnb_free", "tensor")
                    st = xfst[u % 2]
                    em.wait("scalar", "tensor")
                    if r == 0:
                        em.wait_snap("scalar", f"d:xf{u % 2}", "xfst_sent", u - 2)
                    em.op("scalar", lambda e, st=st, r=r: e.copy(
                        st[:, :, r * 128:(r + 1) * 128],
                        trb[:].rearrange("p (c t) -> p c t", c=8)), tick=True)
                    if own:
                        em.op("scalar", lambda e, slot=slot: e.copy(
                            xnown[:, :, slot * 128:(slot + 1) * 128],
                            trb[:].rearrange("p (c t) -> p c t", c=8)), tick=True)
                    em.snap("trb_evac", "scalar")
                    if own:
                        em.wait("tensor", "vector")
                        em.wait_snap("tensor", "scalar", "trf_evac", slot - 1)
                        for c in range(8):
                            em.op("tensor", lambda e, c=c: e.transpose(
                                trf[:, c * 128:(c + 1) * 128],
                                xn32[:, c * 128:(c + 1) * 128], idf[:]),
                                tick=(c == 7))
                        em.snap("trf_done", "tensor")
                        em.wait("scalar", "tensor")
                        em.op("scalar", lambda e: e.copy(
                            xf32[:], trf[:].rearrange("p (c t) -> p c t", c=8)),
                            tick=True)
                        em.snap("trf_evac", "scalar")
                        em.wait("tensor", "scalar")
                        em.wait_snap("tensor", "vector", "lp_done", slot - 1)
                        for c in range(8):
                            em.op("tensor", lambda e, c=c: e.matmul(
                                lp[:], xf32[:, c, :], rwS[:, c, :],
                                start=(c == 0), stop=False))
                        em.op("tensor", lambda e: e.matmul(
                            lp[:], ones_r_f[:], rbS[:], start=False, stop=True),
                            tick=True)
                        em.wait("vector", "tensor")
                        em.op("vector", lambda e: e.reduce_max(
                            rmax8[:], lp[:], axis=AX.X))
                        em.op("vector", lambda e: e.memset(rtaken[:], 0.0))
                        for ei in range(NE):
                            mcol = maskS[:, ei * NSLOT + slot:ei * NSLOT + slot + 1]
                            em.op("vector", lambda e, ei=ei: e.tensor_tensor(
                                rraw[:], lp[:, ei:ei + 1], rmax8[:], ALU.is_ge))
                            em.op("vector", lambda e: e.tensor_scalar(
                                ravail[:], rtaken[:], -1.0, 1.0, ALU.mult, ALU.add))
                            em.op("vector", lambda e, mcol=mcol: e.tensor_tensor(
                                mcol, rraw[:], ravail[:], ALU.mult))
                            em.op("vector", lambda e, mcol=mcol: e.tensor_tensor(
                                rtaken[:], rtaken[:], mcol, ALU.max),
                                tick=(ei == NE - 1))
                        em.snap("lp_done", "vector")
                    if r == 3:
                        em.wait("sync", "scalar")
                        em.dma("sync", f"xf{u % 2}", lambda e, st=st, u=u: e.dma_start(
                            xnf[:, :, u * 512:(u + 1) * 512]
                            .rearrange("c p t -> p c t"), st[:]))
                        em.snap("xfst_sent", f"d:xf{u % 2}")

                for p in range(NT + 1):
                    if p < NT:
                        em.wait_snap("sync", "vector", "xT_free", p - 2)
                        em.dma("sync", f"x{p % 2}", lambda e, p=p: e.dma_start(
                            xT[p % 2][:], xe[p * 128:(p + 1) * 128, :]))
                        stats(p)
                    if p >= 1:
                        apply(p - 1)
                em.wait("sync", "d:xf0")
                em.wait("sync", "d:xf1")
                em.wait("gpsimd", "d:xtk")

            em.run("p0", p0)

        with ExitStack() as es:
            psb = lambda *a: es.enter_context(nc.psum_tensor(*a))
            lsb = lambda *a: es.enter_context(nc.sbuf_tensor(*a))
            rpA = psb("rpA", [128, 128], F32)
            rpB = psb("rpB", [128, 128], F32)
            cnts = lsb("cnts", [128, 1], F32)
            c8s = lsb("c8s", [8, 1], F32)
            nt16 = lsb("nt16", [8, 16], F32)
            ntile8 = lsb("ntile8", [8, 1], F32)
            capG = lsb("capG", [8, 1], F32)
            baseEs = lsb("baseEs", [8, 1], F32)
            colbases = lsb("colbases", [128, 1], F32)
            colbrow = lsb("colbrow", [1, 128], F32)
            posF = lsb("posF", [128, 128], F32)
            posm = lsb("posm", [128, 128], F32)
            postok = lsb("postok", [128, NSLOT], F32)
            sidxI = lsb("sidxI", [128, NSLOT], I32)
            m8t = lsb("m8t", [8, NTILE], F32)
            etraw = lsb("etraw", [1, NTILE], F32)
            etrow = lsb("etrow", [1, NTILE], F32)
            skiprow = lsb("skiprow", [1, NTILE], F32)
            srowBig = lsb("srowBig", [1, NTILE], F32)
            ebs0 = lsb("ebs0", [128, NTILE], F32)
            skipbc = lsb("skipbc", [128, NTILE], F32)
            ebs = lsb("ebs", [128, NTILE], F32)
            sentI = lsb("sentI", [NTILE, 128], I32)

            def route(em):
                em.op("tensor", lambda e: e.matmul(
                    rpB[:, 0:1], maskS[:], ones_c128[:], start=True, stop=True),
                    tick=True)
                em.wait("scalar", "tensor")
                em.op("scalar", lambda e: e.copy(cnts[:], rpB[:, 0:1]), tick=True)
                em.wait("tensor", "scalar")
                em.op("tensor", lambda e: e.matmul(
                    rpB[0:8, 1:2], Sie[:], cnts[:], start=True, stop=True),
                    tick=True)
                em.wait("scalar", "tensor")
                em.op("scalar", lambda e: e.copy(c8s[:], rpB[0:8, 1:2]), tick=True)
                em.wait("vector", "scalar")
                em.op("vector", lambda e: e.tensor_scalar(
                    nt16[:], thr8f[:], c8s[:], None, ALU.is_lt, ALU.bypass),
                    tick=True)
                em.op("vector", lambda e: e.reduce_sum(
                    ntile8[:], nt16[:], axis=AX.X))
                em.op("vector", lambda e: e.tensor_scalar(
                    capG[:], ntile8[:], 128.0, None, ALU.mult, ALU.bypass),
                    tick=True)
                em.wait("tensor", "vector")
                em.op("tensor", lambda e: e.matmul(
                    rpB[0:8, 2:3], Lstrict[0:8, 0:8], capG[:], start=True,
                    stop=True), tick=True)
                em.wait("scalar", "tensor")
                em.op("scalar", lambda e: e.copy(baseEs[:], rpB[0:8, 2:3]),
                      tick=True)
                em.wait("tensor", "scalar")
                em.op("tensor", lambda e: e.matmul(
                    rpB[:, 3:4], Acomb[:], cnts[:], start=True, stop=False))
                em.op("tensor", lambda e: e.matmul(
                    rpB[:, 3:4], Bmat8[:], capG[:], start=False, stop=True),
                    tick=True)
                em.wait("scalar", "tensor")
                em.op("scalar", lambda e: e.copy(colbases[:], rpB[:, 3:4]),
                      tick=True)
                em.wait("tensor", "scalar")
                em.op("tensor", lambda e: e.transpose(
                    rpA[0:1, :], colbases[:], idf[:]), tick=True)
                em.wait("scalar", "tensor")
                em.op("scalar", lambda e: e.copy(colbrow[:], rpA[0:1, :]),
                      tick=True)
                em.wait("tensor", "scalar")
                em.op("tensor", lambda e: e.matmul(
                    rpA[:], Lstrict[:], maskS[:], start=True, stop=False))
                em.op("tensor", lambda e: e.matmul(
                    rpA[:], ones_r_f[:], colbrow[:], start=False, stop=True),
                    tick=True)
                em.wait("scalar", "tensor")
                em.op("scalar", lambda e: e.copy(posF[:], rpA[:]), tick=True)
                em.wait("vector", "scalar")
                em.op("vector", lambda e: e.tensor_tensor(
                    posm[:], posF[:], maskS[:], ALU.mult))
                em.op("vector", lambda e: e.reduce_sum(
                    postok[:], posm[:].rearrange("p (e i) -> p i e", i=NSLOT),
                    axis=AX.X))
                em.op("vector", lambda e: e.tensor_scalar(
                    sidxI[:], postok[:], 1.0, None, ALU.mult, ALU.bypass),
                    tick=True)
                em.op("gpsimd", lambda e: e.iota(
                    sentI[:], pattern=[[1, 128]], base=1000000,
                    channel_multiplier=128), tick=True)
                em.dma("gpsimd", "sent", lambda e: e.dma_start(
                    gmap.rearrange("(t p) o -> t (p o)", p=128), sentI[:]))
                em.wait("gpsimd", "d:sent")
                em.wait("gpsimd", "vector")
                for i in range(NSLOT):
                    em.dma("gpsimd", "gsc", lambda e, i=i: e.indirect_dma_start(
                        gmap, bass.IndirectOffsetOnAxis(sidxI[:, i:i + 1], 0),
                        tokid[:, i:i + 1], None))
                em.op("vector", lambda e: e.tensor_scalar(
                    m8t[:], t128r[:], baseEs[:], None, ALU.is_ge, ALU.bypass),
                    tick=True)
                em.wait("tensor", "vector")
                em.op("tensor", lambda e: e.matmul(
                    rpB[0:1, 8:8 + NTILE], ones_c128[0:8, :], m8t[:],
                    start=True, stop=True), tick=True)
                em.wait("scalar", "tensor")
                em.op("scalar", lambda e: e.copy(etraw[:], rpB[0:1, 8:8 + NTILE]),
                      tick=True)
                em.wait("vector", "scalar")
                em.op("vector", lambda e: e.tensor_scalar(
                    etrow[:], etraw[:], -1.0, None, ALU.add, ALU.bypass))
                em.op("vector", lambda e: e.memset(skiprow[:], 0.0))
                em.op("vector", lambda e: e.tensor_tensor(
                    skiprow[:, 2:NTILE], etrow[:, 2:NTILE], etrow[:, 0:NTILE - 2],
                    ALU.is_equal))
                em.op("vector", lambda e: e.tensor_scalar(
                    srowBig[:], skiprow[:], 100000.0, None, ALU.mult, ALU.bypass),
                    tick=True)
                em.wait("tensor", "vector")
                em.op("tensor", lambda e: e.matmul(
                    rpB[:, 32:32 + NTILE], ones_r_f[:], etrow[:], start=True,
                    stop=True))
                em.op("tensor", lambda e: e.matmul(
                    rpB[:, 64:64 + NTILE], ones_r_f[:], srowBig[:], start=True,
                    stop=True), tick=True)
                em.wait("scalar", "tensor")
                em.op("scalar", lambda e: e.copy(ebs0[:], rpB[:, 32:32 + NTILE]))
                em.op("scalar", lambda e: e.copy(skipbc[:], rpB[:, 64:64 + NTILE]),
                      tick=True)
                em.wait("vector", "scalar")
                em.op("vector", lambda e: e.tensor_tensor(
                    ebs[:], ebs0[:], skipbc[:], ALU.add), tick=True)
                for k in range(4):
                    em.op("vector", lambda e, k=k: e.tensor_scalar(
                        widxQ[:, k, :], ebs[:], 512.0, rowoffq[:, k:k + 1],
                        ALU.mult, ALU.add), tick=(k == 3))
                em.op("vector", lambda e: e.tensor_scalar(
                    widxb1I[:], ebs0[0:64, :], 64.0, rowofff[0:64, :],
                    ALU.mult, ALU.add), tick=True)
                em.wait("gpsimd", "d:gsc")

            em.run("route", route)

        with ExitStack() as es:
            psb = lambda *a: es.enter_context(nc.psum_tensor(*a))
            lsb = lambda *a: es.enter_context(nc.sbuf_tensor(*a))
            xkb = [lsb(f"xkb{i}", [128, 8, 512], BF16) for i in range(2)]
            kvin = [lsb(f"kvin{i}", [128, 1024], F32) for i in range(2)]
            kvout = [lsb(f"kvout{i}", [128, 1024], BF16) for i in range(2)]
            kstg = [lsb(f"kstg{i}", [65, NH * 512], BF16) for i in range(2)]
            ktst = [lsb(f"ktst{i}", [128, 4, NH * 65], BF16) for i in range(2)]
            vtst = [lsb(f"vtst{i}", [128, 4, NH * 65], BF16) for i in range(2)]
            kfp = [psb(f"kfp{i}", [128, 512], F32) for i in range(2)]
            ktp = psb("ktp", [128, DM], F32)
            vtp = psb("vtp", [128, DM], F32)

            kv_chunks = []
            for r in range(32):
                for cc in range(8):
                    kv_chunks.append((w2, w2b, r * 128, cc * 1024))

            def kv(em):
                kst = dict(ck=0, pend=[])

                def kconv(n):
                    for kp in kst["pend"]:
                        _, dstP, rp, cp = kv_chunks[kp]
                        em.wait("vector", f"d:kvi{kp % 2}")
                        em.wait_snap("vector", f"d:kvo{kp % 2}",
                                     "kvo_sent", kp - 2)
                        em.op("vector", lambda e, kp=kp: e.tensor_copy(
                            kvout[kp % 2][:], kvin[kp % 2][:]), tick=True)
                        em.snap("kvc", "vector")
                        em.wait("gpsimd", "vector")
                        em.dma("gpsimd", f"kvo{kp % 2}",
                               lambda e, kp=kp, dstP=dstP, rp=rp, cp=cp:
                               e.dma_start(
                                   dstP[rp:rp + 128, cp:cp + 1024],
                                   kvout[kp % 2][:]))
                        em.snap("kvo_sent", f"d:kvo{kp % 2}")
                    kst["pend"] = []
                    for _ in range(n):
                        k = kst["ck"]
                        if k >= len(kv_chunks):
                            return
                        kst["ck"] += 1
                        srcT, dstT, r0, c0 = kv_chunks[k]
                        em.wait_snap("gpsimd", "vector", "kvc", k - 2)
                        em.dma("gpsimd", f"kvi{k % 2}",
                               lambda e, srcT=srcT, r0=r0, c0=c0, k=k:
                               e.dma_start(kvin[k % 2][:],
                                           srcT[r0:r0 + 128, c0:c0 + 1024]))
                        kst["pend"].append(k)
                for i in range(2):
                    em.op("gpsimd", lambda e, i=i: e.memset(
                        kstg[i][64:65, :], 1.0), tick=True)
                    em.op("gpsimd", lambda e, i=i: e.memset(
                        ktst[i][:].rearrange("p t (h o) -> p t h o", o=65)
                        [:, :, :, 64:65], 1.0), tick=True)
                    em.op("gpsimd", lambda e, i=i: e.memset(
                        vtst[i][:].rearrange("p t (h o) -> p t h o", o=65)
                        [:, :, :, 64:65], 1.0), tick=True)
                for u in range(NU):
                    xb = xkb[u % 2]
                    em.wait_snap("sync", "tensor", "xkb_free", u - 2)
                    em.dma("sync", f"xkb{u % 2}", lambda e, xb=xb, u=u: e.dma_start(
                        xb[:], xnf[:, :, u * 512:(u + 1) * 512]
                        .rearrange("c p t -> p c t")))
                    ks = kstg[u % 2]
                    em.wait("tensor", f"d:xkb{u % 2}")
                    for hp in range(8):
                        kconv(2)
                        kp = kfp[hp % 2]
                        em.wait_snap("tensor", "scalar", "kfp_evac", u * 8 + hp - 2)
                        for c in range(8):
                            em.op("tensor", lambda e, kp=kp, c=c, hp=hp, xb=xb: e.matmul(
                                kp[:], wkt[:, c, hp * 128:(hp + 1) * 128],
                                xb[:, c, :], start=(c == 0), stop=(c == 7)),
                                tick=(c == 7))
                        em.wait("scalar", "tensor")
                        if hp == 0:
                            em.wait("scalar", "gpsimd")
                            em.wait_snap("scalar", f"d:kst{u % 2}", "kfm_sent", u - 2)
                        em.op("scalar", lambda e, kp=kp, ks=ks, hp=hp: e.copy(
                            ks[0:64, (2 * hp) * 512:(2 * hp + 1) * 512], kp[0:64, :]))
                        em.op("scalar", lambda e, kp=kp, ks=ks, hp=hp: e.copy(
                            ks[0:64, (2 * hp + 1) * 512:(2 * hp + 2) * 512],
                            kp[64:128, :]), tick=True)
                        em.snap("kfp_evac", "scalar")
                    em.wait("sync", "scalar")
                    em.dma("sync", f"kst{u % 2}", lambda e, ks=ks, u=u: e.dma_start(kfm[u], ks[:]))
                    em.snap("kfm_sent", f"d:kst{u % 2}")
                    kt = ktst[u % 2]
                    for t in range(4):
                        em.wait_snap("tensor", "scalar", "ktp_evac", u * 4 + t - 1)
                        for c in range(8):
                            for hh in range(2):
                                em.op("tensor", lambda e, xb=xb, c=c, t=t, hh=hh: e.matmul(
                                    ktp[:, hh * 512:(hh + 1) * 512],
                                    xb[:, c, t * 128:(t + 1) * 128],
                                    wkt[:, c, hh * 512:(hh + 1) * 512],
                                    start=(c == 0), stop=(c == 7)),
                                    tick=(c == 7 and hh == 1))
                        em.wait("scalar", "tensor")
                        if t == 0:
                            em.wait_snap("scalar", f"d:ktt{u % 2}", "ktk_sent", u - 2)
                        em.op("scalar", lambda e, kt=kt, t=t: e.copy(
                            kt[:].rearrange("p t (h o) -> p t h o", o=65)
                            [:, t, :, 0:64],
                            ktp[:].rearrange("p (h d) -> p h d", d=64)), tick=True)
                        em.snap("ktp_evac", "scalar")
                    em.wait("sync", "scalar")
                    em.dma("sync", f"ktt{u % 2}", lambda e, kt=kt, u=u: e.dma_start(ktk[u], kt[:]))
                    em.snap("ktk_sent", f"d:ktt{u % 2}")
                    vt = vtst[u % 2]
                    for t in range(4):
                        em.wait_snap("tensor", "scalar", "vtp_evac", u * 4 + t - 1)
                        for c in range(8):
                            for hh in range(2):
                                em.op("tensor", lambda e, xb=xb, c=c, t=t, hh=hh: e.matmul(
                                    vtp[:, hh * 512:(hh + 1) * 512],
                                    xb[:, c, t * 128:(t + 1) * 128],
                                    wvt[:, c, hh * 512:(hh + 1) * 512],
                                    start=(c == 0), stop=(c == 7)),
                                    tick=(c == 7 and hh == 1))
                        em.wait("scalar", "tensor")
                        if t == 0:
                            em.wait_snap("scalar", f"d:vtt{u % 2}", "vtk_sent", u - 2)
                        em.op("scalar", lambda e, vt=vt, t=t: e.copy(
                            vt[:].rearrange("p t (h o) -> p t h o", o=65)
                            [:, t, :, 0:64],
                            vtp[:].rearrange("p (h d) -> p h d", d=64)), tick=True)
                        em.snap("vtp_evac", "scalar")
                    em.snap("xkb_free", "tensor")
                    em.wait("sync", "scalar")
                    em.dma("sync", f"vtt{u % 2}", lambda e, vt=vt, u=u: e.dma_start(vtk[u], vt[:]))
                    em.snap("vtk_sent", f"d:vtt{u % 2}")
                while kst["ck"] < len(kv_chunks):
                    kconv(2)
                for kp in list(kst["pend"]):
                    pass
                if True:
                    for kp in kst["pend"]:
                        _, dstP, rp, cp = kv_chunks[kp]
                        em.wait("vector", f"d:kvi{kp % 2}")
                        em.wait_snap("vector", f"d:kvo{kp % 2}",
                                     "kvo_sent", kp - 2)
                        em.op("vector", lambda e, kp=kp: e.tensor_copy(
                            kvout[kp % 2][:], kvin[kp % 2][:]), tick=True)
                        em.snap("kvc", "vector")
                        em.wait("gpsimd", "vector")
                        em.dma("gpsimd", f"kvo{kp % 2}",
                               lambda e, kp=kp, dstP=dstP, rp=rp, cp=cp:
                               e.dma_start(
                                   dstP[rp:rp + 128, cp:cp + 1024],
                                   kvout[kp % 2][:]))
                        em.snap("kvo_sent", f"d:kvo{kp % 2}")
                    kst["pend"] = []
                em.wait("gpsimd", "d:kvo0")
                em.wait("gpsimd", "d:kvo1")
                for s_ in ("kst0", "kst1", "ktt0", "ktt1", "vtt0", "vtt1"):
                    em.wait("sync", "d:" + s_)

            em.run("kv", kv)

        stackQ = ExitStack()
        qbuf = stackQ.enter_context(nc.sbuf_tensor("qbuf", [65, NH * TOK], BF16,
                                                   side="right"))
        with ExitStack() as es:
            psb = lambda *a: es.enter_context(nc.psum_tensor(*a))
            qps = [psb(f"qps{i}", [128, 512], F32) for i in range(2)]

            def attn_q(em):
                em.op("gpsimd", lambda e: e.memset(qbuf[64:65, :], 0.0),
                      tick=True)
                for hp in range(8):
                    for tc in range(4):
                        it = hp * 4 + tc
                        qp = qps[it % 2]
                        em.wait_snap("tensor", "scalar", "qp_evac", it - 2)
                        for c in range(8):
                            em.op("tensor", lambda e, qp=qp, c=c, hp=hp, tc=tc: e.matmul(
                                qp[:], wqt[:, c, hp * 128:(hp + 1) * 128],
                                xnown[:, c, tc * 512:(tc + 1) * 512],
                                start=(c == 0), stop=(c == 7)), tick=(c == 7))
                        em.wait("scalar", "tensor")
                        h0, h1 = 2 * hp, 2 * hp + 1
                        em.op("scalar", lambda e, qp=qp, h0=h0, tc=tc: e.activation(
                            qbuf[0:64, h0 * TOK + tc * 512:h0 * TOK + (tc + 1) * 512],
                            qp[0:64, :], AF.Copy, scale=0.125))
                        em.op("scalar", lambda e, qp=qp, h1=h1, tc=tc: e.activation(
                            qbuf[0:64, h1 * TOK + tc * 512:h1 * TOK + (tc + 1) * 512],
                            qp[64:128, :], AF.Copy, scale=0.125), tick=True)
                        em.snap("qp_evac", "scalar")

            em.run("attn_q", attn_q)
        stackA.close()

        stackC = ExitStack()
        ntbig = stackC.enter_context(nc.sbuf_tensor("ntbig", [128, 8 * TOK], BF16))
        with ExitStack() as es:
            psb = lambda *a: es.enter_context(nc.psum_tensor(*a))
            lsb = lambda *a: es.enter_context(nc.sbuf_tensor(*a))
            sps = [psb(f"s2ps{i}", [128, 2, 512], F32) for i in range(3)]
            ozp = [psb(f"ozp{i}", [65, 512], F32) for i in range(2)]
            urp = [p[64:65, :] for p in ozp]
            bcp = [p[0:64, :] for p in ozp]
            kb2 = [lsb(f"kb2{i}", [65, GH * 512], BF16) for i in range(2)]
            ntb_ = [lsb(f"ntb{i}", [128, 4, GH * 65], BF16) for i in range(2)]
            ttl = [lsb(f"ttl{i}", [128, 4, 512], BF16) for i in range(2)]
            uza = lsb("uza", [65, GH * TOK], F32)
            prod = lsb("prod", [64, TOK], BF16)
            rowA = lsb("rowA", [1, TOK], F32)
            betaro = lsb("betaro", [1, TOK], F32)
            cvin = [lsb(f"cvin{i}", [128, 1024], F32) for i in range(2)]
            cvout = [lsb(f"cvout{i}", [128, 1024], BF16) for i in range(2)]
            zpart = lsb("zpart", [128, 64], F32)
            upart = lsb("upart", [128, 64], F32)
            tA = lsb("tA", [128, 64], F32)
            tB = lsb("tB", [128, 64], F32)
            tC = lsb("tC", [128, 64], F32)

            def chunks_of(u):
                out, off, nq = [], u * 128, TOK - u * 128
                while nq > 0:
                    w = min(512, nq)
                    out.append((off, w))
                    off += w
                    nq -= w
                return out

            conv_chunks = []
            for r in range(32):
                for cc in range(16):
                    conv_chunks.append((w1, w1b, r * 128, cc * 1024))

            def attn_b(em):
                state = dict(nbc=0, nur=0, ck=0, pend_conv=[])

                def conv_step(n):
                    for kp in state["pend_conv"]:
                        srcP, dstP, rp, cp = conv_chunks[kp]
                        em.wait("vector", f"d:cvi{kp % 2}")
                        em.wait_snap("vector", f"d:cvo{kp % 2}",
                                     "cvo_sent", kp - 2)
                        em.op("vector", lambda e, kp=kp: e.tensor_copy(
                            cvout[kp % 2][:], cvin[kp % 2][:]), tick=True)
                        em.snap("cvc", "vector")
                        em.wait("gpsimd", "vector")
                        em.dma("gpsimd", f"cvo{kp % 2}",
                               lambda e, kp=kp, dstP=dstP, rp=rp, cp=cp:
                               e.dma_start(
                                   dstP[rp:rp + 128, cp:cp + 1024],
                                   cvout[kp % 2][:]))
                        em.snap("cvo_sent", f"d:cvo{kp % 2}")
                    state["pend_conv"] = []
                    for _ in range(n):
                        k = state["ck"]
                        if k >= len(conv_chunks):
                            return
                        state["ck"] += 1
                        srcT, dstT, r0, c0 = conv_chunks[k]
                        em.wait_snap("gpsimd", "vector", "cvc", k - 2)
                        em.dma("gpsimd", f"cvi{k % 2}",
                               lambda e, srcT=srcT, r0=r0, c0=c0, k=k:
                               e.dma_start(cvin[k % 2][:],
                                           srcT[r0:r0 + 128, c0:c0 + 1024]))
                        state["pend_conv"].append(k)

                def conv_last(em):
                    for kp in state["pend_conv"]:
                        srcP, dstP, rp, cp = conv_chunks[kp]
                        em.wait("vector", f"d:cvi{kp % 2}")
                        em.wait_snap("vector", f"d:cvo{kp % 2}",
                                     "cvo_sent", kp - 2)
                        em.op("vector", lambda e, kp=kp: e.tensor_copy(
                            cvout[kp % 2][:], cvin[kp % 2][:]), tick=True)
                        em.snap("cvc", "vector")
                        em.wait("gpsimd", "vector")
                        em.dma("gpsimd", f"cvo{kp % 2}",
                               lambda e, kp=kp, dstP=dstP, rp=rp, cp=cp:
                               e.dma_start(
                                   dstP[rp:rp + 128, cp:cp + 1024],
                                   cvout[kp % 2][:]))
                        em.snap("cvo_sent", f"d:cvo{kp % 2}")
                    state["pend_conv"] = []
                    em.wait("gpsimd", "d:cvo0")
                    em.wait("gpsimd", "d:cvo1")

                def sweep(g, which, sw):
                    em.wait("vector", "scalar")
                    em.wait("vector", "d:fin")
                    em.wait("tensor", "vector")
                    em.wait("sync", "tensor")
                    st = dict(nt=0, nsp=0, nd=0, pend=None)

                    def flush():
                        if st["pend"] is None:
                            return
                        tb, ntb, hl, qoff, qw, expidx, i, dmidx = st["pend"]
                        st["pend"] = None
                        oz = ozp[i % 2]
                        em.wait_snap("tensor", "vector", f"oz_{sw}", i - 2)
                        em.wait_snap("tensor", "scalar", f"exp_{sw}", expidx)
                        if dmidx >= 0:
                            em.wait_snap("tensor", "vector", f"dm_{sw}", dmidx)
                        for t in range(4):
                            em.op("tensor", lambda e, oz=oz, ntb=ntb, tb=tb,
                                  t=t, hl=hl, qw=qw: e.matmul(
                                      oz[:, 0:qw],
                                      ntb[:, t, hl * 65:(hl + 1) * 65],
                                      tb[:, t, 0:qw],
                                      start=(t == 0), stop=(t == 3)),
                                  tick=(t == 3))
                        em.snap(f"ozmm_{sw}", "tensor")
                        em.wait("vector", "tensor")
                        if i < GH * 4:
                            em.op("vector", lambda e, oz=oz, hl=hl, qoff=qoff,
                                  qw=qw: e.tensor_copy(
                                      uza[0:65, hl * TOK + qoff:hl * TOK + qoff + qw],
                                      oz[:, 0:qw]), tick=True)
                        else:
                            em.op("vector", lambda e, oz=oz, hl=hl, qoff=qoff,
                                  qw=qw: e.tensor_tensor(
                                      uza[0:65, hl * TOK + qoff:hl * TOK + qoff + qw],
                                      uza[0:65, hl * TOK + qoff:hl * TOK + qoff + qw],
                                      oz[:, 0:qw], ALU.add), tick=True)
                        em.snap(f"oz_{sw}", "vector")

                    for u in range(NU):
                        kb = kb2[u % 2]
                        src = ktk if which == "uz" else vtk
                        ntb = ntb_[u % 2]
                        em.wait_snap("sync", "tensor", f"kb2f_{sw}", u - 2)
                        em.dma("sync", f"kbn{u % 2}", lambda e, kb=kb, u=u, g=g: e.dma_start(
                            kb[:], kfm[u].rearrange("o (h t) -> o h t", t=512)
                            [:, 4 * g:4 * g + 4, :]))
                        em.dma("sync", f"kbn{u % 2}", lambda e, ntb=ntb, u=u, g=g, src=src: e.dma_start(
                            ntb[:], src[u].rearrange("p t (h o) -> p t h o", o=65)
                            [:, :, 4 * g:4 * g + 4, :]))
                        em.wait("tensor", f"d:kbn{u % 2}")
                        for hl in range(GH):
                            conv_step(1)
                            h = 4 * g + hl
                            for ci, (qoff, qw) in enumerate(chunks_of(u)):
                                nt = st["nt"]
                                tb = ttl[nt % 2]
                                for tp in range(2):
                                    nsp = st["nsp"]
                                    sp = sps[nsp % 3]
                                    em.wait_snap("tensor", "scalar", f"exp_{sw}",
                                                 nsp - 3)
                                    for tt in range(2):
                                        t = 2 * tp + tt
                                        em.op("tensor", lambda e, sp=sp, kb=kb,
                                              hl=hl, t=t, tt=tt, qoff=qoff, qw=qw,
                                              h=h: e.matmul(
                                                  sp[:, tt, 0:qw],
                                                  kb[0:65, hl * 512 + t * 128:
                                                     hl * 512 + (t + 1) * 128],
                                                  qbuf[0:65, h * TOK + qoff:
                                                       h * TOK + qoff + qw],
                                                  start=True, stop=True),
                                              tick=(tt == 1))
                                    if tp == 0:
                                        em.wait_snap("scalar", "tensor",
                                                     f"ozmm_{sw}", nt - 2)
                                    em.wait("scalar", "tensor")
                                    em.op("scalar", lambda e, sp=sp, tb=tb, tp=tp,
                                          qw=qw: e.activation(
                                              tb[:, 2 * tp:2 * tp + 2, 0:qw],
                                              sp[:, :, 0:qw], AF.Exp),
                                          tick=True)
                                    em.snap(f"exp_{sw}", "scalar")
                                    st["nsp"] += 1
                                    if ci == 0:
                                        em.wait("vector", "scalar")
                                        for tt in range(2):
                                            t = 2 * tp + tt
                                            em.op("vector", lambda e, tb=tb, t=t:
                                                  e.tensor_tensor(
                                                      tb[:, t, 0:128],
                                                      tb[:, t, 0:128],
                                                      m01[:, t * 128:(t + 1) * 128],
                                                      ALU.mult), tick=(tt == 1))
                                        if tp == 1:
                                            em.snap(f"dm_{sw}", "vector")
                                dmidx = -1
                                if ci == 0:
                                    dmidx = st["nd"]
                                    st["nd"] += 1
                                flush()
                                st["pend"] = (tb, ntb, hl, qoff, qw,
                                              st["nsp"] - 1, nt, dmidx)
                                st["nt"] += 1
                        flush()
                        em.snap(f"kb2f_{sw}", "tensor")

                for g in range(NG):
                    sweep(g, "uz", 2 * g)
                    for hl in range(GH):
                        h = 4 * g + hl
                        hc = hl * TOK
                        em.wait("vector", "tensor")
                        em.op("vector", lambda e, h=h, hc=hc: e.tensor_tensor(
                            prod[:], qbuf[0:64, h * TOK:(h + 1) * TOK],
                            uza[0:64, hc:hc + TOK], ALU.mult), tick=True)
                        for c in range(4):
                            ur = urp[c % 2]
                            em.wait("tensor", "vector")
                            em.wait_snap("tensor", "scalar", "ur_evac",
                                         state["nur"] - 2)
                            em.op("tensor", lambda e, ur=ur, c=c: e.matmul(
                                ur[:], ones_c_b[:], prod[0:64, c * 512:(c + 1) * 512],
                                start=True, stop=True), tick=True)
                            em.wait("scalar", "tensor")
                            em.op("scalar", lambda e, ur=ur, c=c: e.copy(
                                rowA[:, c * 512:(c + 1) * 512],
                                ur[:]), tick=True)
                            em.snap("ur_evac", "scalar")
                            state["nur"] += 1
                        em.wait("sync", "scalar")
                        em.dma("sync", "fin", lambda e, hl=hl: e.dma_start(
                            upart[hl * 32:(hl + 1) * 32, :], rowA[:]))
                    em.wait("sync", "vector")
                    em.dma("sync", "fin", lambda e: e.dma_start(
                        zpart[:], uza[64:65, :]))
                    em.wait("vector", "d:fin")
                    em.wait("scalar", "d:fin")
                    em.op("vector", lambda e: e.reciprocal(tA[:], zpart[:]),
                          tick=True)
                    em.op("scalar", lambda e: e.activation(
                        tC[:], zpart[:], AF.Ln), tick=True)
                    em.wait("vector", "scalar")
                    em.op("vector", lambda e: e.tensor_tensor(
                        tA[:], upart[:], tA[:], ALU.mult))
                    em.op("vector", lambda e: e.tensor_tensor(
                        tA[:], tC[:], tA[:], ALU.subtract))
                    em.op("vector", lambda e: e.tensor_tensor(
                        tB[:], tA[:], tA[:], ALU.mult))
                    em.op("vector", lambda e: e.tensor_scalar(
                        tC[:], tB[:], POLY[0], POLY[2], ALU.mult, ALU.add))
                    em.op("vector", lambda e: e.tensor_tensor(
                        tC[:], tC[:], tB[:], ALU.mult))
                    em.op("vector", lambda e: e.tensor_scalar(
                        tB[:], tB[:], POLY[1], POLY[3], ALU.mult, ALU.add))
                    em.op("vector", lambda e: e.tensor_tensor(
                        tB[:], tB[:], tA[:], ALU.mult))
                    em.op("vector", lambda e: e.tensor_tensor(
                        tC[:], tC[:], tB[:], ALU.add))
                    em.op("vector", lambda e: e.tensor_scalar(
                        tC[:], tC[:], POLY[4], 1.0, ALU.add, ALU.max))
                    em.op("vector", lambda e: e.tensor_scalar(
                        tB[:], tA[:], 0.5, None, ALU.is_gt, ALU.bypass))
                    em.op("vector", lambda e: e.tensor_scalar(
                        tC[:], tC[:], -1.0, None, ALU.add, ALU.bypass))
                    em.op("vector", lambda e: e.tensor_tensor(
                        tC[:], tB[:], tC[:], ALU.mult))
                    em.op("vector", lambda e: e.tensor_scalar(
                        tC[:], tC[:], 1.0, None, ALU.add, ALU.bypass), tick=True)
                    for hl in range(GH):
                        h = 4 * g + hl
                        em.wait("sync", "vector")
                        em.wait("sync", "tensor")
                        em.dma("sync", "finb", lambda e, hl=hl: e.dma_start(
                            betaro[:], tC[hl * 32:(hl + 1) * 32, :]))
                        em.wait("tensor", "d:finb")
                        for c in range(4):
                            bc = bcp[c % 2]
                            em.wait_snap("tensor", "vector", "bcu", state["nbc"] - 2)
                            em.op("tensor", lambda e, bc=bc, c=c: e.matmul(
                                bc[:], ones_r_f[0:1, 0:64],
                                betaro[:, c * 512:(c + 1) * 512],
                                start=True, stop=True), tick=True)
                            em.wait("vector", "tensor")
                            em.op("vector", lambda e, bc=bc, h=h, c=c: e.tensor_tensor(
                                qbuf[0:64, h * TOK + c * 512:h * TOK + (c + 1) * 512],
                                qbuf[0:64, h * TOK + c * 512:h * TOK + (c + 1) * 512],
                                bc[:], ALU.mult), tick=True)
                            em.snap("bcu", "vector")
                            state["nbc"] += 1
                    em.wait("tensor", "vector")
                    sweep(g, "av", 2 * g + 1)
                    em.wait("sync", "vector")
                    em.dma("sync", "fin", lambda e: e.dma_start(
                        zpart[:], uza[64:65, :]))
                    em.wait("vector", "d:fin")
                    em.op("vector", lambda e: e.reciprocal(tA[:], zpart[:]),
                          tick=True)
                    for hl in range(GH):
                        h = 4 * g + hl
                        hc = hl * TOK
                        em.wait("sync", "vector")
                        em.wait("sync", "tensor")
                        em.dma("sync", "finb", lambda e, hl=hl: e.dma_start(
                            betaro[:], tA[hl * 32:(hl + 1) * 32, :]))
                        em.wait("tensor", "d:finb")
                        for c in range(4):
                            bc = bcp[c % 2]
                            em.wait_snap("tensor", "vector", "bcu", state["nbc"] - 2)
                            em.op("tensor", lambda e, bc=bc, c=c: e.matmul(
                                bc[:], ones_r_f[0:1, 0:64],
                                betaro[:, c * 512:(c + 1) * 512],
                                start=True, stop=True), tick=True)
                            em.wait("vector", "tensor")
                            em.op("vector", lambda e, bc=bc, h=h, hc=hc, c=c: e.tensor_tensor(
                                ntbig[(h % 2) * 64:(h % 2) * 64 + 64,
                                      (h // 2) * TOK + c * 512:(h // 2) * TOK + (c + 1) * 512],
                                uza[0:64, hc + c * 512:hc + (c + 1) * 512],
                                bc[:], ALU.mult), tick=True)
                            em.snap("bcu", "vector")
                            state["nbc"] += 1
                while state["ck"] < len(conv_chunks):
                    conv_step(2)
                conv_last(em)

            em.run("attn_b", attn_b)
        stackQ.close()
        stackAcc = ExitStack()
        acctok = stackAcc.enter_context(
            nc.sbuf_tensor("acctok", [128, NSLOT, DM], BF16, side="right"))

        with ExitStack() as es:
            psb = lambda *a: es.enter_context(nc.psum_tensor(*a))
            lsb = lambda *a: es.enter_context(nc.sbuf_tensor(*a))
            wps = [psb(f"wps{i}", [128, 512], F32) for i in range(2)]
            atp = [psb(f"atp{i}", [128, 512], F32) for i in range(2)]
            afs = [lsb(f"afs{i}", [128, 512], F32) for i in range(2)]

            def wo_phase(em):
                for dc in range(8):
                    for tc in range(4):
                        it = dc * 4 + tc
                        wp = wps[it % 2]
                        em.wait_snap("tensor", "scalar", "wp_evac", it - 2)
                        for pc in range(8):
                            em.op("tensor", lambda e, wp=wp, pc=pc, dc=dc, tc=tc: e.matmul(
                                wp[:], wot[:, pc, dc * 128:(dc + 1) * 128],
                                ntbig[:, pc * TOK + tc * 512:pc * TOK + (tc + 1) * 512],
                                start=(pc == 0), stop=(pc == 7)), tick=(pc == 7))
                        af = afs[it % 2]
                        em.wait("scalar", "tensor")
                        em.wait_snap("scalar", "tensor", "afs_used", it - 2)
                        em.op("scalar", lambda e, wp=wp, af=af, dc=dc: e.activation(
                            af[:], wp[:], AF.Identity, bias=bot[:, dc:dc + 1]),
                            tick=True)
                        em.snap("wp_evac", "scalar")
                        at = atp[it % 2]
                        em.wait("tensor", "scalar")
                        em.wait_snap("tensor", "scalar", "atp_evac", it - 2)
                        for s in range(4):
                            em.op("tensor", lambda e, at=at, af=af, s=s: e.transpose(
                                at[:, s * 128:(s + 1) * 128],
                                af[:, s * 128:(s + 1) * 128], idf[:]), tick=(s == 3))
                        em.snap("afs_used", "tensor")
                        em.wait("scalar", "tensor")
                        em.op("scalar", lambda e, at=at, dc=dc, tc=tc: e.copy(
                            acctok[:, tc * 4:tc * 4 + 4, dc * 128:(dc + 1) * 128],
                            at[:].rearrange("p (t q) -> p t q", t=4)), tick=True)
                        em.snap("atp_evac", "scalar")

            em.run("wo", wo_phase)
        stackC.close()
        stackW.close()

        with ExitStack() as es:
            psb = lambda *a: es.enter_context(nc.psum_tensor(*a))
            lsb = lambda *a: es.enter_context(nc.sbuf_tensor(*a))
            w1g = [lsb(f"w1g{i}", [128, 8, 2048], BF16) for i in range(2)]
            w2g = [lsb(f"w2g{i}", [128, 8, 1024], BF16) for i in range(2)]
            xgr = [lsb(f"xgr{i}", [128, DM], BF16) for i in range(2)]
            xnt = [lsb(f"xnt{i}", [128, 8, 128], BF16) for i in range(2)]
            b1r = [lsb(f"b1raw{i}", [64, 128], F32) for i in range(2)]
            b1tA = lsb("b1tA", [128, NTILE, 64], F32)
            gmt = [lsb(f"gmt{i}", [128, 1], I32) for i in range(2)]
            ysacc = lsb("ysacc", [128, NTILE, DM], BF16)
            sgb = [lsb(f"sgb{i}", [128, 128], BF16) for i in range(2)]
            ubf = [lsb(f"ubf{i}", [128, 128], BF16) for i in range(2)]
            yps = [psb(f"yps{i}", [128, DM], F32) for i in range(2)]
            agp = [psb(f"agp{i}", [128, 256], F32) for i in range(2)]
            xtp = [psb(f"xtp{i}", [128, 128], BF16) for i in range(2)]

            regs = {}

            def ffn(em):
                if em.mode == "gpsimd" and "b" not in regs:
                    regs["b"] = em.eng.to_reg(32767)
                state = dict(nhc=0)
                em.dma("gpsimd", "gmt0", lambda e: e.dma_start(
                    gmt[0][:], gmap[0:128, :]))
                for qt in range(4):
                    for t in range(NTILE):
                        nq = qt * NTILE + t
                        par = t % 2
                        em.wait("gpsimd", f"d:gmt{par}")
                        em.wait_snap("gpsimd", "tensor", "xgr_used", nq - 2)
                        em.dma("gpsimd", f"xg{par}", lambda e, par=par: e.indirect_dma_start(
                            xgr[par][:], None, xtk,
                            bass.IndirectOffsetOnAxis(gmt[par][:], 0),
                            bounds_check=regs["b"], oob_is_err=False))
                        if qt == 0:
                            em.wait_snap("gpsimd", "tensor", "b1r_used", t - 2)
                            em.dma("gpsimd", f"b1g{par}", lambda e, par=par, t=t: e.indirect_dma_start(
                                b1r[par][:], None, b1flat,
                                bass.IndirectOffsetOnAxis(widxb1I[:, t:t + 1], 0)))
                        em.wait_snap("gpsimd", "tensor", "wg_free", nq - 2)
                        em.dma("gpsimd", f"wg{par}",
                               lambda e, par=par, t=t, qt=qt: e.indirect_dma_start(
                                   w1g[par][:].rearrange("p c j -> p (c j)"),
                                   None, w1b,
                                   bass.IndirectOffsetOnAxis(
                                       widxQ[:, qt, t:t + 1], 0),
                                   bounds_check=regs["b"], oob_is_err=False))
                        em.dma("gpsimd", f"wg{par}",
                               lambda e, par=par, t=t, qt=qt: e.indirect_dma_start(
                                   w2g[par][:].rearrange("p c j -> p (c j)"),
                                   None, w2b,
                                   bass.IndirectOffsetOnAxis(
                                       widxQ[:, qt, t:t + 1], 0),
                                   bounds_check=regs["b"], oob_is_err=False))
                        if nq + 1 < 4 * NTILE:
                            tn = (t + 1) % NTILE
                            em.dma("gpsimd", f"gmt{(t + 1) % 2}",
                                   lambda e, tn=tn: e.dma_start(
                                       gmt[tn % 2][:],
                                       gmap[tn * 128:(tn + 1) * 128, :]))
                        em.wait("tensor", f"d:xg{par}")
                        for c in range(8):
                            nxt = nq * 8 + c
                            em.wait_snap("tensor", "scalar", "xtp_evac", nxt - 2)
                            em.op("tensor", lambda e, c=c, par=par: e.transpose(
                                xtp[c % 2][:], xgr[par][:, c * 128:(c + 1) * 128],
                                idb[:]), tick=True)
                            em.wait("scalar", "tensor")
                            if c == 0:
                                em.wait_snap("scalar", "tensor", "xnt_used",
                                             nq - 2)
                            em.op("scalar", lambda e, c=c, par=par: e.copy(
                                xnt[par][:, c, :], xtp[c % 2][:]), tick=True)
                            em.snap("xtp_evac", "scalar")
                        em.snap("xgr_used", "tensor")
                        if qt == 0:
                            em.wait("tensor", f"d:b1g{par}")
                            em.wait_snap("tensor", "vector", "ag_free",
                                         state["nhc"] - 1)
                            em.op("tensor", lambda e, par=par: e.transpose(
                                agp[par][:, 0:64], b1r[par][:], idf[0:64, 0:64]),
                                tick=True)
                            em.snap("b1r_used", "tensor")
                            em.wait("scalar", "tensor")
                            em.op("scalar", lambda e, par=par, t=t: e.copy(
                                b1tA[:, t, :], agp[par][:, 0:64]), tick=True)
                        em.wait("tensor", f"d:wg{par}")
                        em.wait("tensor", "scalar")

                        def emit_y(hc):
                            ubp = ubf[hc % 2]
                            if hc == 0:
                                em.wait_snap("tensor", "vector", "yps_free",
                                             nq - 2)
                            em.wait("tensor", "vector")
                            for half in range(2):
                                em.op("tensor", lambda e, half=half, ubp=ubp,
                                      par=par, hc=hc: e.matmul(
                                          yps[par][:, half * 512:(half + 1) * 512],
                                          ubp[:], w2g[par][:, hc, half * 512:
                                                           (half + 1) * 512],
                                          start=(hc == 0), stop=(hc == 7)),
                                      tick=(half == 1))
                            em.snap("ub_used", "tensor")

                        for hc in range(8):
                            nhc = state["nhc"]
                            agq = agp[nhc % 2]
                            em.wait_snap("tensor", "vector", "ag_free", nhc - 2)
                            for c in range(8):
                                em.op("tensor", lambda e, agq=agq, par=par, c=c,
                                      hc=hc: e.matmul(
                                          agq[:, 0:128],
                                          w1g[par][:, c, hc * 128:(hc + 1) * 128],
                                          xnt[par][:, c, :],
                                          start=(c == 0), stop=(c == 7)),
                                      tick=(c == 7))
                            for c in range(8):
                                em.op("tensor", lambda e, agq=agq, par=par, c=c,
                                      hc=hc: e.matmul(
                                          agq[:, 128:256],
                                          w1g[par][:, c, 1024 + hc * 128:
                                                   1024 + (hc + 1) * 128],
                                          xnt[par][:, c, :],
                                          start=(c == 0), stop=(c == 7)),
                                      tick=(c == 7))
                            sg = sgb[nhc % 2]
                            ub = ubf[nhc % 2]
                            em.wait("scalar", "tensor")
                            em.wait_snap("scalar", "vector", "sg_used", nhc - 2)
                            em.op("scalar", lambda e, agq=agq, sg=sg, qt=qt,
                                  hc=hc, t=t: e.activation(
                                      sg[:], agq[:, 128:256], AF.Sigmoid,
                                      bias=b1tA[:, t, 32 + qt * 8 + hc:
                                                33 + qt * 8 + hc]),
                                  tick=True)
                            if hc > 0:
                                emit_y(hc - 1)
                            em.wait("vector", "scalar")
                            em.wait_snap("vector", "tensor", "ub_used", nhc - 2)
                            em.op("vector", lambda e, agq=agq, sg=sg, ub=ub,
                                  qt=qt, hc=hc, t=t: e.scalar_tensor_tensor(
                                      ub[:], agq[:, 0:128],
                                      b1tA[:, t, qt * 8 + hc:qt * 8 + hc + 1],
                                      sg[:], ALU.add, ALU.mult), tick=True)
                            em.snap("ag_free", "vector")
                            em.snap("sg_used", "vector")
                            state["nhc"] += 1
                        emit_y(7)
                        em.snap("wg_free", "tensor")
                        if qt == 3:
                            em.snap("xnt_used", "tensor")
                        else:
                            em.snap("xnt_used", "tensor")
                        em.wait("vector", "tensor")
                        if qt == 0:
                            em.op("vector", lambda e, par=par, t=t: e.tensor_copy(
                                ysacc[:, t, :], yps[par][:]), tick=True)
                        else:
                            em.op("vector", lambda e, par=par, t=t: e.tensor_tensor(
                                ysacc[:, t, :], ysacc[:, t, :], yps[par][:],
                                ALU.add), tick=True)
                        em.snap("yps_free", "vector")
                em.wait("gpsimd", "vector")
                for t in range(NTILE):
                    par = t % 2
                    em.dma("gpsimd", f"gmt{par}", lambda e, par=par, t=t: e.dma_start(
                        gmt[par][:], gmap[t * 128:(t + 1) * 128, :]))
                    em.wait("gpsimd", f"d:gmt{par}")
                    em.dma("gpsimd", f"ys{par}", lambda e, par=par, t=t: e.indirect_dma_start(
                        ytok, bass.IndirectOffsetOnAxis(gmt[par][:], 0),
                        ysacc[:, t, :], None,
                        bounds_check=regs["b"], oob_is_err=False))
                em.wait("gpsimd", "d:ys0")
                em.wait("gpsimd", "d:ys1")

            em.run("ffn", ffn)

        with ExitStack() as es:
            psb = lambda *a: es.enter_context(nc.psum_tensor(*a))
            lsb = lambda *a: es.enter_context(nc.sbuf_tensor(*a))
            xo = [lsb(f"xo{i}", [128, DM], F32) for i in range(2)]
            ytb = [lsb(f"ytb{i}", [128, DM], BF16) for i in range(2)]
            mTs = [lsb(f"mTs{i}", [8, 128], F32) for i in range(2)]
            ob = [lsb(f"ob{i}", [128, DM], F32) for i in range(2)]
            b2S = lsb("b2S", [8, DM], F32)
            b2ps = [psb(f"b2ps{i}", [128, DM], F32) for i in range(2)]
            mtp2 = psb("mtp2", [8, 128], F32)

            def out_phase(em):
                em.dma("sync", "b2l", lambda e: e.dma_start(b2S[:], b2))
                for i in range(NSLOT):
                    xb = xo[i % 2]
                    yb = ytb[i % 2]
                    em.wait_snap("sync", "vector", "xo_free", i - 2)
                    em.dma("sync", f"xo{i % 2}", lambda e, xb=xb, i=i: e.dma_start(
                        xb[:], xe[(4 * i) * 128:(4 * i + 1) * 128, :]))
                    em.dma("sync", f"xo{i % 2}", lambda e, yb=yb, i=i: e.dma_start(
                        yb[:], ytok[i * 128:(i + 1) * 128, :]))
                    em.wait_snap("tensor", "scalar", "mts_evac", i - 1)
                    em.op("tensor", lambda e, i=i: e.transpose(
                        mtp2[:], maskS[:].rearrange("p (e i) -> p i e", i=NSLOT)
                        [:, i, :], idf[:]), tick=True)
                    em.wait("scalar", "tensor")
                    em.op("scalar", lambda e, i=i: e.copy(
                        mTs[i % 2][:], mtp2[:]), tick=True)
                    em.snap("mts_evac", "scalar")
                    em.wait("tensor", "scalar")
                    if i == 0:
                        em.wait("tensor", "d:b2l")
                    em.wait_snap("tensor", "vector", "b2ps_used", i - 2)
                    for half in range(2):
                        em.op("tensor", lambda e, i=i, half=half: e.matmul(
                            b2ps[i % 2][:, half * 512:(half + 1) * 512],
                            mTs[i % 2][:], b2S[:, half * 512:(half + 1) * 512],
                            start=True, stop=True), tick=(half == 1))
                    o = ob[i % 2]
                    em.wait("vector", f"d:xo{i % 2}")
                    em.wait("vector", "tensor")
                    em.wait_snap("vector", f"d:yst{i % 2}", "ob_sent", i - 2)
                    em.op("vector", lambda e, xb=xb, o=o, i=i: e.tensor_tensor(
                        o[:], xb[:], acctok[:, i, :], ALU.add), tick=True)
                    em.op("vector", lambda e, o=o, yb=yb: e.tensor_tensor(
                        o[:], o[:], yb[:], ALU.add), tick=True)
                    em.op("vector", lambda e, o=o, i=i: e.tensor_tensor(
                        o[:], o[:], b2ps[i % 2][:], ALU.add), tick=True)
                    em.snap("xo_free", "vector")
                    em.snap("b2ps_used", "vector")
                    em.wait("sync", "vector")
                    em.dma("sync", f"yst{i % 2}", lambda e, o=o, i=i: e.dma_start(
                        y[i * 128:(i + 1) * 128, :], o[:]))
                    em.snap("ob_sent", f"d:yst{i % 2}")
                em.wait("sync", "d:yst0")
                em.wait("sync", "d:yst1")

            em.run("out", out_phase)
        stackAcc.close()

    return nc


def _perm_and_masks(j):
    sig = [j] + [m for m in range(4) if m != j]
    tile_order = []
    for u in range(16):
        for r in range(4):
            tile_order.append(4 * u + sig[r])
    m1 = np.zeros((128, 512), np.float32)
    qi = np.arange(128)[:, None]
    ki = np.arange(128)[None, :]
    for r in range(4):
        if r == 0:
            m1[:, 0:128] = np.where(ki > qi, NEG, 0.0)
        elif sig[r] > j:
            m1[:, r * 128:(r + 1) * 128] = NEG
    m2 = np.zeros((128, 512), np.float32)
    for t in range(4):
        m2[:, t * 128:(t + 1) * 128] = m1[:, t * 128:(t + 1) * 128].T
    return tile_order, m1, m2


def kernel(L0_attn_x, L0_attn_Wq, L0_attn_Wk, L0_attn_Wv, L0_attn_Wo,
           L0_attn_bo, L0_ffn_router_w, L0_ffn_router_b, L0_ffn_w1,
           L0_ffn_b1, L0_ffn_w2, L0_ffn_b2):
    x = np.ascontiguousarray(np.asarray(L0_attn_x, dtype=np.float32))
    args = dict(
        wq=np.ascontiguousarray(np.asarray(L0_attn_Wq, np.float32)),
        wk=np.ascontiguousarray(np.asarray(L0_attn_Wk, np.float32)),
        wv=np.ascontiguousarray(np.asarray(L0_attn_Wv, np.float32)),
        wo=np.ascontiguousarray(np.asarray(L0_attn_Wo, np.float32)),
        bo=np.ascontiguousarray(np.asarray(L0_attn_bo, np.float32)),
        rw=np.ascontiguousarray(np.asarray(L0_ffn_router_w, np.float32)),
        rb=np.ascontiguousarray(np.asarray(L0_ffn_router_b, np.float32)),
        w1=np.ascontiguousarray(
            np.asarray(L0_ffn_w1, np.float32).reshape(8, 8, 128, 2, 4, 1024)
            .transpose(0, 4, 2, 1, 3, 5).reshape(4096, 16384)),
        b1=np.ascontiguousarray(np.asarray(L0_ffn_b1, np.float32)),
        w2=np.ascontiguousarray(
            np.asarray(L0_ffn_w2, np.float32).reshape(8, 4, 8, 128, 1024)
            .transpose(0, 1, 3, 2, 4).reshape(4096, 8192)),
        b2=np.ascontiguousarray(np.asarray(L0_ffn_b2, np.float32)),
    )
    nc = build_nc()
    in_maps = []
    metas = []
    for c in range(8):
        b, j = c // 4, c % 4
        order, m1, m2 = _perm_and_masks(j)
        row_idx = np.concatenate(
            [np.arange(t * 128, (t + 1) * 128) for t in order])
        im = dict(args)
        im["xe"] = np.ascontiguousarray(x[b][row_idx])
        im["m1"] = m1
        im["m2"] = m2
        in_maps.append(im)
        metas.append((b, j))
    res = run_bass_kernel_spmd(nc, in_maps, core_ids=list(range(8)))
    out = np.empty((2, 8192, 1024), np.float32)
    for c in range(8):
        b, j = metas[c]
        yc = res.results[c]["y"]
        for i in range(NSLOT):
            t = 4 * i + j
            out[b, t * 128:(t + 1) * 128, :] = yc[i * 128:(i + 1) * 128, :]
    return out


# version 17
# speedup vs baseline: 1.1441x; 1.0363x over previous
import numpy as np
from contextlib import ExitStack

import concourse.bass as bass
import concourse.mybir as mybir
from concourse.bass_utils import run_bass_kernel_spmd
from concourse.alu_op_type import AluOpType as ALU

F32 = mybir.dt.float32
BF16 = mybir.dt.bfloat16
I32 = mybir.dt.int32
AF = mybir.ActivationFunctionType
AX = mybir.AxisListType

DM = 1024
NH = 16
HD = 64
NT = 64
NU = 16
NSLOT = 16
TOK = 2048
NE = 8
HID = 4096
NTILE = 24
NG = 4
GH = 4
POLY = [-0.037, 0.481, -2.3, 4.917, -1.791]
NEG = -1.0e30


class Emitter:

    def __init__(self, nc):
        self.nc = nc

    def _reset(self):
        self.cnt = dict(sync=0, gpsimd=0, tensor=0, vector=0, scalar=0)
        self.waits = []
        self.snaps = {}
        self.wi = 0

    CHAINED = ("vector", "scalar", "gpsimd")

    def op(self, eng, fn, tick=False):
        chained = eng in self.CHAINED
        if self.mode == "count":
            if tick or chained:
                self.cnt[eng] += 1
            return
        if self.mode == eng:
            if chained:
                n = self.ecnt.get(eng, 0)
                if n > 0:
                    self.eng.wait_ge(self.sems[eng], n)
                self.ecnt[eng] = n + 1
            inst = fn(self.eng)
            if tick or chained:
                inst.then_inc(self.sems[eng], 1)

    def dma(self, queue, stream, fn):
        key = "d:" + stream
        if self.mode == "count":
            self.cnt[key] = self.cnt.get(key, 0) + 16
            return
        if self.mode == queue:
            fn(self.eng).then_inc(self.sems[key], 16)

    def wait(self, waiter, src):
        if self.mode == "count":
            self.waits.append((src, self.cnt.get(src, 0)))
            return
        src_, val = self.waits[self.wi]
        self.wi += 1
        if self.mode == waiter and val > 0:
            self.eng.wait_ge(self.sems[src_], val)

    def snap(self, key, src):
        if self.mode == "count":
            self.snaps.setdefault(key, []).append(self.cnt.get(src, 0))

    def wait_snap(self, waiter, src, key, idx):
        if idx < 0:
            return
        if self.mode == "count":
            self.waits.append((src, ("s", key, idx)))
            return
        src_, val = self.waits[self.wi]
        self.wi += 1
        if isinstance(val, tuple):
            val = self.snaps[val[1]][val[2]]
        if self.mode == waiter and val > 0:
            self.eng.wait_ge(self.sems[src_], val)

    def run(self, name, phase_fn):
        nc = self.nc
        self.mode = "count"
        self._reset()
        phase_fn(self)
        self.waits = [(s, self.snaps[v[1]][v[2]] if isinstance(v, tuple) else v)
                      for (s, v) in self.waits]
        with ExitStack() as es:
            keys = ["sync", "gpsimd", "tensor", "vector", "scalar"]
            keys += [k for k in self.cnt if k.startswith("d:")]
            sems = {k: es.enter_context(nc.semaphore(f"{name}_{k}".replace(":", "_")))
                    for k in keys}
            block = es.enter_context(nc.Block(name, no_gpsimd_drain=True))

            def mk(eng_name):
                def body(eng_obj):
                    self.mode = eng_name
                    self.eng = eng_obj
                    self.sems = sems
                    self.wi = 0
                    self.ecnt = {}
                    phase_fn(self)
                return body

            block.sync(mk("sync"))
            block.gpsimd(mk("gpsimd"))
            block.tensor(mk("tensor"))
            block.vector(mk("vector"))
            block.scalar(mk("scalar"))


def build_nc():
    nc = bass.Bass("TRN2", target_bir_lowering=False)

    xe = nc.dram_tensor("xe", [8192, DM], F32, kind="ExternalInput").ap()
    wq = nc.dram_tensor("wq", [DM, DM], F32, kind="ExternalInput").ap()
    wk = nc.dram_tensor("wk", [DM, DM], F32, kind="ExternalInput").ap()
    wv = nc.dram_tensor("wv", [DM, DM], F32, kind="ExternalInput").ap()
    wo = nc.dram_tensor("wo", [DM, DM], F32, kind="ExternalInput").ap()
    bo = nc.dram_tensor("bo", [DM], F32, kind="ExternalInput").ap()
    rw = nc.dram_tensor("rw", [DM, NE], F32, kind="ExternalInput").ap()
    rb = nc.dram_tensor("rb", [NE], F32, kind="ExternalInput").ap()
    w1 = nc.dram_tensor("w1", [NE * 4 * 128, 8 * 2048], F32, kind="ExternalInput").ap()
    b1 = nc.dram_tensor("b1", [NE, 2 * HID], F32, kind="ExternalInput").ap()
    w2 = nc.dram_tensor("w2", [NE * 4 * 128, 8 * 1024], F32, kind="ExternalInput").ap()
    b2 = nc.dram_tensor("b2", [NE, DM], F32, kind="ExternalInput").ap()
    m1 = nc.dram_tensor("m1", [128, 512], F32, kind="ExternalInput").ap()
    m2 = nc.dram_tensor("m2", [128, 512], F32, kind="ExternalInput").ap()
    y = nc.dram_tensor("y", [TOK, DM], F32, kind="ExternalOutput").ap()

    xnf = nc.dram_tensor("xnf", [8, 128, 8192], BF16, kind="Internal").ap()
    kfm = nc.dram_tensor("kfm", [NU, 65, NH * 512], BF16, kind="Internal").ap()
    ktk = nc.dram_tensor("ktk", [NU, 128, 4, NH * 65], BF16, kind="Internal").ap()
    vtk = nc.dram_tensor("vtk", [NU, 128, 4, NH * 65], BF16, kind="Internal").ap()
    xtk = nc.dram_tensor("xtk", [TOK, DM], BF16, kind="Internal").ap()
    gmap = nc.dram_tensor("gmap", [NTILE * 128, 1], I32, kind="Internal").ap()
    ytok = nc.dram_tensor("ytok", [TOK, DM], BF16, kind="Internal").ap()
    w1b = nc.dram_tensor("w1b", [NE * 4 * 128, 8 * 2048], BF16, kind="Internal").ap()
    w2b = nc.dram_tensor("w2b", [NE * 4 * 128, 8 * 1024], BF16, kind="Internal").ap()
    b1flat = b1.rearrange("e (r f) -> (e r) f", f=128)

    em = Emitter(nc)
    outer = ExitStack()
    sb = lambda *a: outer.enter_context(nc.sbuf_tensor(*a))

    m2s = sb("m2s", [128, 512], F32)
    idb = sb("idb", [128, 128], BF16)
    m01 = sb("m01", [128, 512], BF16)
    idf = sb("idf", [128, 128], F32)
    ones_r_b = sb("ones_r_b", [1, 128], BF16)
    ones_r_f = sb("ones_r_f", [1, 128], F32)
    ones_c_f = sb("ones_c_f", [64, 1], F32)
    ones_c_b = sb("ones_c_b", [64, 1], BF16)
    rwS = sb("rwS", [128, 8, NE], F32)
    rbS = sb("rbS", [1, NE], F32)
    maskS = sb("maskS", [128, NSLOT * NE], F32)
    Lstrict = sb("Lstrict", [128, 128], F32)
    Acomb = sb("Acomb", [128, 128], F32)
    Bmat8 = sb("Bmat8", [8, 128], F32)
    Sie = sb("Sie", [128, 8], F32)
    thr8f = sb("thr8f", [8, 16], F32)
    t128r = sb("t128r", [8, NTILE], F32)
    ones_c128 = sb("ones_c128", [128, 1], F32)
    rowofff = sb("rowofff", [128, 1], F32)
    rowoffq = sb("rowoffq", [128, 4], F32)
    tokid = sb("tokid", [128, NSLOT], I32)
    widxQ = sb("widxQ", [128, 4, NTILE], I32)
    widxb1I = sb("widxb1I", [64, NTILE], I32)
    stackA = ExitStack()
    sbA = lambda *a: stackA.enter_context(nc.sbuf_tensor(*a))
    stackW = ExitStack()
    wot = stackW.enter_context(nc.sbuf_tensor("wot", [128, 8, DM], BF16))
    bot = stackW.enter_context(nc.sbuf_tensor("bot", [128, 8], F32))
    wqt = sbA("wqt", [128, 8, DM], BF16)
    wkt = sbA("wkt", [128, 8, DM], BF16)
    wvt = sbA("wvt", [128, 8, DM], BF16)
    xnown = sbA("xnown", [128, 8, TOK], BF16)

    with outer:
        with ExitStack() as es:
            psb = lambda *a: es.enter_context(nc.psum_tensor(*a))
            lsb = lambda *a: es.enter_context(nc.sbuf_tensor(*a))
            iti = lsb("iti", [128, 128], I32)
            wrow = [lsb(f"wrow{i}", [128, DM], F32) for i in range(2)]
            bo8 = lsb("bo8", [8, 128], F32)
            it8 = lsb("it8", [8, 128], I32)
            ithr = lsb("ithr", [8, 16], I32)
            it24 = lsb("it24", [8, NTILE], I32)
            irow = lsb("irow", [128, 1], I32)
            irq = lsb("irq", [128, 4], I32)
            s16b = lsb("s16b", [8, 128], F32)
            b16sb = lsb("b16sb", [128, 128], F32)
            trp = [psb(f"trp{i}", [128, 128], F32) for i in range(2)]

            def prep(em):
                em.dma("sync", "cst", lambda e: e.dma_start(m2s[:], m2))
                em.dma("sync", "cst", lambda e: e.dma_start(rbS[:], rb[None, :]))
                for c in range(8):
                    em.dma("sync", "cst", lambda e, c=c: e.dma_start(
                        rwS[:, c, :], rw[c * 128:(c + 1) * 128, :]))
                em.dma("sync", "cst", lambda e: e.dma_start(
                    bo8[:], bo.rearrange("(c p) -> c p", p=128)))
                em.op("gpsimd", lambda e: e.iota(
                    iti[:], pattern=[[1, 128]], base=0, channel_multiplier=-1),
                    tick=True)
                em.op("gpsimd", lambda e: e.memset(ones_r_b[:], 1.0), tick=True)
                em.op("gpsimd", lambda e: e.memset(ones_r_f[:], 1.0), tick=True)
                em.op("gpsimd", lambda e: e.memset(ones_c_f[:], 1.0), tick=True)
                em.op("gpsimd", lambda e: e.memset(ones_c_b[:], 1.0), tick=True)
                em.wait("vector", "gpsimd")
                em.op("vector", lambda e: e.tensor_scalar(
                    idb[:], iti[:], 0, None, ALU.is_equal, ALU.bypass), tick=True)
                em.wait("vector", "d:cst")
                em.op("vector", lambda e: e.tensor_scalar(
                    m01[:], m2s[:], 0, None, ALU.is_equal, ALU.bypass), tick=True)
                em.op("vector", lambda e: e.tensor_scalar(
                    idf[:], iti[:], 0, None, ALU.is_equal, ALU.bypass), tick=True)

                wlist = [(wq, wqt), (wk, wkt), (wv, wvt), (wo, wot)]
                it = 0
                for wsrc, wdst in wlist:
                    for r in range(8):
                        buf = wrow[it % 2]
                        em.wait_snap("sync", "tensor", "wrow_free", it - 2)
                        em.dma("sync", f"wrow{it % 2}", lambda e, buf=buf, wsrc=wsrc, r=r: e.dma_start(
                            buf[:], wsrc[r * 128:(r + 1) * 128, :]))
                        em.wait("tensor", f"d:wrow{it % 2}")
                        em.wait("tensor", "vector")
                        for c in range(8):
                            pt = trp[c % 2]
                            em.wait_snap("tensor", "scalar", "wtr_evac",
                                         it * 8 + c - 2)
                            em.op("tensor", lambda e, pt=pt, buf=buf, c=c: e.transpose(
                                pt[:], buf[:, c * 128:(c + 1) * 128], idf[:]),
                                tick=True)
                            em.wait("scalar", "tensor")
                            em.op("scalar", lambda e, pt=pt, wdst=wdst, c=c, r=r: e.copy(
                                wdst[:, c, r * 128:(r + 1) * 128], pt[:]), tick=True)
                            em.snap("wtr_evac", "scalar")
                        em.snap("wrow_free", "tensor")
                        it += 1
                em.wait("tensor", "d:cst")
                em.wait_snap("tensor", "scalar", "wtr_evac", 8 * 32 - 2)
                em.op("tensor", lambda e: e.transpose(
                    trp[0][0:128, 0:8], bo8[:], idf[0:8, 0:8]), tick=True)
                em.wait("scalar", "tensor")
                em.op("scalar", lambda e: e.copy(bot[:], trp[0][0:128, 0:8]),
                      tick=True)
                em.op("gpsimd", lambda e: e.iota(
                    it8[:].rearrange("p (b j) -> p b j", b=8),
                    pattern=[[1, 8], [0, 16]], base=0, channel_multiplier=-1),
                    tick=True)
                em.op("gpsimd", lambda e: e.iota(
                    ithr[:], pattern=[[128, 16]], base=0, channel_multiplier=0),
                    tick=True)
                em.op("gpsimd", lambda e: e.iota(
                    it24[:], pattern=[[128, NTILE]], base=0,
                    channel_multiplier=0), tick=True)
                em.op("gpsimd", lambda e: e.iota(
                    irow[:], pattern=[[0, 1]], base=0, channel_multiplier=1),
                    tick=True)
                em.op("gpsimd", lambda e: e.iota(
                    irq[:], pattern=[[128, 4]], base=0, channel_multiplier=1),
                    tick=True)
                em.op("gpsimd", lambda e: e.iota(
                    tokid[:], pattern=[[128, NSLOT]], base=0,
                    channel_multiplier=1), tick=True)
                em.op("gpsimd", lambda e: e.memset(ones_c128[:], 1.0), tick=True)
                em.wait("vector", "gpsimd")
                em.op("vector", lambda e: e.tensor_scalar(
                    Lstrict[:], iti[:], 0, None, ALU.is_gt, ALU.bypass), tick=True)
                em.op("vector", lambda e: e.tensor_scalar(
                    s16b[:], it8[:], 0, None, ALU.is_equal, ALU.bypass), tick=True)
                em.op("vector", lambda e: e.tensor_scalar(
                    Bmat8[:], it8[:], 0, None, ALU.is_gt, ALU.bypass), tick=True)
                em.op("vector", lambda e: e.tensor_scalar(
                    thr8f[:], ithr[:], 1.0, None, ALU.mult, ALU.bypass), tick=True)
                em.op("vector", lambda e: e.tensor_scalar(
                    t128r[:], it24[:], 1.0, None, ALU.mult, ALU.bypass), tick=True)
                em.op("vector", lambda e: e.tensor_scalar(
                    rowofff[:], irow[:], 1.0, None, ALU.mult, ALU.bypass), tick=True)
                em.op("vector", lambda e: e.tensor_scalar(
                    rowoffq[:], irq[:], 1.0, None, ALU.mult, ALU.bypass),
                    tick=True)
                em.wait("tensor", "vector")
                em.wait("tensor", "scalar")
                em.op("tensor", lambda e: e.transpose(
                    trp[0][0:128, 0:8], s16b[:], idf[0:8, 0:8]), tick=True)
                em.op("tensor", lambda e: e.matmul(
                    trp[1][:], s16b[:], s16b[:], start=True, stop=True), tick=True)
                em.wait("scalar", "tensor")
                em.op("scalar", lambda e: e.copy(Sie[:], trp[0][0:128, 0:8]),
                      tick=True)
                em.op("scalar", lambda e: e.copy(b16sb[:], trp[1][:]), tick=True)
                em.wait("vector", "scalar")
                em.op("vector", lambda e: e.tensor_tensor(
                    Acomb[:], b16sb[:], Lstrict[:], ALU.mult), tick=True)
                em.wait("sync", "d:cst")

            em.run("prep", prep)

        with ExitStack() as es:
            psb = lambda *a: es.enter_context(nc.psum_tensor(*a))
            lsb = lambda *a: es.enter_context(nc.sbuf_tensor(*a))
            xT = [lsb(f"xT{i}", [128, DM], F32) for i in range(2)]
            ssum = lsb("ssum", [128, 1], F32)
            nmean = lsb("nmean", [128, 1], F32)
            ssq = lsb("ssq", [128, 1], F32)
            vstd = lsb("vstd", [128, 1], F32)
            rstd = lsb("rstd", [128, 1], F32)
            sqj = lsb("sqj", [128, DM], F32)
            xnb = [lsb(f"xnb{i}", [128, DM], BF16) for i in range(2)]
            xn32 = lsb("xn32", [128, DM], F32)
            xf32 = lsb("xf32", [128, 8, 128], F32)
            xfst = [lsb(f"xfst{i}", [128, 8, 512], BF16) for i in range(2)]
            rmax8 = lsb("rmax8", [128, 1], F32)
            rraw = lsb("rraw", [128, 1], F32)
            ravail = lsb("ravail", [128, 1], F32)
            rtaken = lsb("rtaken", [128, 1], F32)
            trb = psb("trb", [128, DM], BF16)
            trf = psb("trf", [128, DM], F32)
            lp = psb("lp", [128, NE], F32)

            def p0(em):
                for p in range(NT):
                    u, r = p // 4, p % 4
                    own = (r == 0)
                    slot = u
                    buf = xT[p % 2]
                    em.wait_snap("sync", "vector", "xT_free", p - 2)
                    em.dma("sync", f"x{p % 2}", lambda e, buf=buf, p=p: e.dma_start(
                        buf[:], xe[p * 128:(p + 1) * 128, :]))
                    em.wait("vector", f"d:x{p % 2}")
                    em.wait_snap("vector", "scalar", "sqrt_done", p - 1)
                    em.op("vector", lambda e, buf=buf: e.reduce_sum(
                        ssum[:], buf[:], axis=AX.X))
                    em.op("vector", lambda e: e.tensor_scalar(
                        nmean[:], ssum[:], -1.0 / DM, None, ALU.mult, ALU.bypass),
                        tick=True)
                    em.wait("scalar", "vector")
                    em.op("scalar", lambda e, buf=buf: e.activation(
                        sqj[:], buf[:], AF.Square, bias=nmean[:], scale=1.0),
                        tick=True)
                    em.wait("vector", "scalar")
                    em.op("vector", lambda e: e.reduce_sum(
                        ssq[:], sqj[:], axis=AX.X))
                    em.op("vector", lambda e: e.tensor_scalar(
                        vstd[:], ssq[:], 1.0 / DM, 1.0e-5, ALU.mult, ALU.add),
                        tick=True)
                    em.wait("scalar", "vector")
                    em.op("scalar", lambda e: e.activation(
                        vstd[:], vstd[:], AF.Sqrt), tick=True)
                    em.snap("sqrt_done", "scalar")
                    em.wait("vector", "scalar")
                    em.op("vector", lambda e: e.reciprocal(rstd[:], vstd[:]))
                    xb = xnb[p % 2]
                    em.wait_snap("vector", "tensor", "xnb_free", p - 2)
                    em.op("vector", lambda e, buf=buf, xb=xb: e.tensor_scalar(
                        xb[:], buf[:], nmean[:], rstd[:], ALU.add, ALU.mult),
                        tick=True)
                    if own:
                        em.wait_snap("vector", "tensor", "trf_done", slot - 1)
                        em.wait_snap("vector", "d:xtk", "xtk_sent", slot - 1)
                        em.op("vector", lambda e, buf=buf: e.tensor_scalar(
                            xn32[:], buf[:], nmean[:], rstd[:], ALU.add, ALU.mult),
                            tick=True)
                        em.wait("gpsimd", "vector")
                        em.dma("gpsimd", "xtk", lambda e, slot=slot: e.dma_start(
                            xtk[slot * 128:(slot + 1) * 128, :], xn32[:]))
                        em.snap("xtk_sent", "d:xtk")
                    em.snap("xT_free", "vector")
                    em.wait("tensor", "vector")
                    em.wait_snap("tensor", "scalar", "trb_evac", p - 1)
                    for c in range(8):
                        em.op("tensor", lambda e, xb=xb, c=c: e.transpose(
                            trb[:, c * 128:(c + 1) * 128],
                            xb[:, c * 128:(c + 1) * 128], idb[:]), tick=(c == 7))
                    em.snap("xnb_free", "tensor")
                    st = xfst[u % 2]
                    em.wait("scalar", "tensor")
                    if r == 0:
                        em.wait_snap("scalar", f"d:xf{u % 2}", "xfst_sent", u - 2)
                    em.op("scalar", lambda e, st=st, r=r: e.copy(
                        st[:, :, r * 128:(r + 1) * 128],
                        trb[:].rearrange("p (c t) -> p c t", c=8)), tick=True)
                    if own:
                        em.op("scalar", lambda e, slot=slot: e.copy(
                            xnown[:, :, slot * 128:(slot + 1) * 128],
                            trb[:].rearrange("p (c t) -> p c t", c=8)), tick=True)
                    em.snap("trb_evac", "scalar")
                    if own:
                        em.wait("tensor", "vector")
                        em.wait_snap("tensor", "scalar", "trf_evac", slot - 1)
                        for c in range(8):
                            em.op("tensor", lambda e, c=c: e.transpose(
                                trf[:, c * 128:(c + 1) * 128],
                                xn32[:, c * 128:(c + 1) * 128], idf[:]),
                                tick=(c == 7))
                        em.snap("trf_done", "tensor")
                        em.wait("scalar", "tensor")
                        em.op("scalar", lambda e: e.copy(
                            xf32[:], trf[:].rearrange("p (c t) -> p c t", c=8)),
                            tick=True)
                        em.snap("trf_evac", "scalar")
                        em.wait("tensor", "scalar")
                        em.wait_snap("tensor", "vector", "lp_done", slot - 1)
                        for c in range(8):
                            em.op("tensor", lambda e, c=c: e.matmul(
                                lp[:], xf32[:, c, :], rwS[:, c, :],
                                start=(c == 0), stop=False))
                        em.op("tensor", lambda e: e.matmul(
                            lp[:], ones_r_f[:], rbS[:], start=False, stop=True),
                            tick=True)
                        em.wait("vector", "tensor")
                        em.op("vector", lambda e: e.reduce_max(
                            rmax8[:], lp[:], axis=AX.X))
                        em.op("vector", lambda e: e.memset(rtaken[:], 0.0))
                        for ei in range(NE):
                            mcol = maskS[:, ei * NSLOT + slot:ei * NSLOT + slot + 1]
                            em.op("vector", lambda e, ei=ei: e.tensor_tensor(
                                rraw[:], lp[:, ei:ei + 1], rmax8[:], ALU.is_ge))
                            em.op("vector", lambda e: e.tensor_scalar(
                                ravail[:], rtaken[:], -1.0, 1.0, ALU.mult, ALU.add))
                            em.op("vector", lambda e, mcol=mcol: e.tensor_tensor(
                                mcol, rraw[:], ravail[:], ALU.mult))
                            em.op("vector", lambda e, mcol=mcol: e.tensor_tensor(
                                rtaken[:], rtaken[:], mcol, ALU.max),
                                tick=(ei == NE - 1))
                        em.snap("lp_done", "vector")
                    if r == 3:
                        em.wait("sync", "scalar")
                        em.dma("sync", f"xf{u % 2}", lambda e, st=st, u=u: e.dma_start(
                            xnf[:, :, u * 512:(u + 1) * 512]
                            .rearrange("c p t -> p c t"), st[:]))
                        em.snap("xfst_sent", f"d:xf{u % 2}")
                em.wait("sync", "d:xf0")
                em.wait("sync", "d:xf1")
                em.wait("gpsimd", "d:xtk")

            em.run("p0", p0)

        with ExitStack() as es:
            psb = lambda *a: es.enter_context(nc.psum_tensor(*a))
            lsb = lambda *a: es.enter_context(nc.sbuf_tensor(*a))
            rpA = psb("rpA", [128, 128], F32)
            rpB = psb("rpB", [128, 128], F32)
            cnts = lsb("cnts", [128, 1], F32)
            c8s = lsb("c8s", [8, 1], F32)
            nt16 = lsb("nt16", [8, 16], F32)
            ntile8 = lsb("ntile8", [8, 1], F32)
            capG = lsb("capG", [8, 1], F32)
            baseEs = lsb("baseEs", [8, 1], F32)
            colbases = lsb("colbases", [128, 1], F32)
            colbrow = lsb("colbrow", [1, 128], F32)
            posF = lsb("posF", [128, 128], F32)
            posm = lsb("posm", [128, 128], F32)
            postok = lsb("postok", [128, NSLOT], F32)
            sidxI = lsb("sidxI", [128, NSLOT], I32)
            m8t = lsb("m8t", [8, NTILE], F32)
            etraw = lsb("etraw", [1, NTILE], F32)
            etrow = lsb("etrow", [1, NTILE], F32)
            skiprow = lsb("skiprow", [1, NTILE], F32)
            srowBig = lsb("srowBig", [1, NTILE], F32)
            ebs0 = lsb("ebs0", [128, NTILE], F32)
            skipbc = lsb("skipbc", [128, NTILE], F32)
            ebs = lsb("ebs", [128, NTILE], F32)
            sentI = lsb("sentI", [NTILE, 128], I32)

            def route(em):
                em.op("tensor", lambda e: e.matmul(
                    rpB[:, 0:1], maskS[:], ones_c128[:], start=True, stop=True),
                    tick=True)
                em.wait("scalar", "tensor")
                em.op("scalar", lambda e: e.copy(cnts[:], rpB[:, 0:1]), tick=True)
                em.wait("tensor", "scalar")
                em.op("tensor", lambda e: e.matmul(
                    rpB[0:8, 1:2], Sie[:], cnts[:], start=True, stop=True),
                    tick=True)
                em.wait("scalar", "tensor")
                em.op("scalar", lambda e: e.copy(c8s[:], rpB[0:8, 1:2]), tick=True)
                em.wait("vector", "scalar")
                em.op("vector", lambda e: e.tensor_scalar(
                    nt16[:], thr8f[:], c8s[:], None, ALU.is_lt, ALU.bypass),
                    tick=True)
                em.op("vector", lambda e: e.reduce_sum(
                    ntile8[:], nt16[:], axis=AX.X))
                em.op("vector", lambda e: e.tensor_scalar(
                    capG[:], ntile8[:], 128.0, None, ALU.mult, ALU.bypass),
                    tick=True)
                em.wait("tensor", "vector")
                em.op("tensor", lambda e: e.matmul(
                    rpB[0:8, 2:3], Lstrict[0:8, 0:8], capG[:], start=True,
                    stop=True), tick=True)
                em.wait("scalar", "tensor")
                em.op("scalar", lambda e: e.copy(baseEs[:], rpB[0:8, 2:3]),
                      tick=True)
                em.wait("tensor", "scalar")
                em.op("tensor", lambda e: e.matmul(
                    rpB[:, 3:4], Acomb[:], cnts[:], start=True, stop=False))
                em.op("tensor", lambda e: e.matmul(
                    rpB[:, 3:4], Bmat8[:], capG[:], start=False, stop=True),
                    tick=True)
                em.wait("scalar", "tensor")
                em.op("scalar", lambda e: e.copy(colbases[:], rpB[:, 3:4]),
                      tick=True)
                em.wait("tensor", "scalar")
                em.op("tensor", lambda e: e.transpose(
                    rpA[0:1, :], colbases[:], idf[:]), tick=True)
                em.wait("scalar", "tensor")
                em.op("scalar", lambda e: e.copy(colbrow[:], rpA[0:1, :]),
                      tick=True)
                em.wait("tensor", "scalar")
                em.op("tensor", lambda e: e.matmul(
                    rpA[:], Lstrict[:], maskS[:], start=True, stop=False))
                em.op("tensor", lambda e: e.matmul(
                    rpA[:], ones_r_f[:], colbrow[:], start=False, stop=True),
                    tick=True)
                em.wait("scalar", "tensor")
                em.op("scalar", lambda e: e.copy(posF[:], rpA[:]), tick=True)
                em.wait("vector", "scalar")
                em.op("vector", lambda e: e.tensor_tensor(
                    posm[:], posF[:], maskS[:], ALU.mult))
                em.op("vector", lambda e: e.reduce_sum(
                    postok[:], posm[:].rearrange("p (e i) -> p i e", i=NSLOT),
                    axis=AX.X))
                em.op("vector", lambda e: e.tensor_scalar(
                    sidxI[:], postok[:], 1.0, None, ALU.mult, ALU.bypass),
                    tick=True)
                em.op("gpsimd", lambda e: e.iota(
                    sentI[:], pattern=[[1, 128]], base=1000000,
                    channel_multiplier=128), tick=True)
                em.dma("gpsimd", "sent", lambda e: e.dma_start(
                    gmap.rearrange("(t p) o -> t (p o)", p=128), sentI[:]))
                em.wait("gpsimd", "d:sent")
                em.wait("gpsimd", "vector")
                for i in range(NSLOT):
                    em.dma("gpsimd", "gsc", lambda e, i=i: e.indirect_dma_start(
                        gmap, bass.IndirectOffsetOnAxis(sidxI[:, i:i + 1], 0),
                        tokid[:, i:i + 1], None))
                em.op("vector", lambda e: e.tensor_scalar(
                    m8t[:], t128r[:], baseEs[:], None, ALU.is_ge, ALU.bypass),
                    tick=True)
                em.wait("tensor", "vector")
                em.op("tensor", lambda e: e.matmul(
                    rpB[0:1, 8:8 + NTILE], ones_c128[0:8, :], m8t[:],
                    start=True, stop=True), tick=True)
                em.wait("scalar", "tensor")
                em.op("scalar", lambda e: e.copy(etraw[:], rpB[0:1, 8:8 + NTILE]),
                      tick=True)
                em.wait("vector", "scalar")
                em.op("vector", lambda e: e.tensor_scalar(
                    etrow[:], etraw[:], -1.0, None, ALU.add, ALU.bypass))
                em.op("vector", lambda e: e.memset(skiprow[:], 0.0))
                em.op("vector", lambda e: e.tensor_tensor(
                    skiprow[:, 2:NTILE], etrow[:, 2:NTILE], etrow[:, 0:NTILE - 2],
                    ALU.is_equal))
                em.op("vector", lambda e: e.tensor_scalar(
                    srowBig[:], skiprow[:], 100000.0, None, ALU.mult, ALU.bypass),
                    tick=True)
                em.wait("tensor", "vector")
                em.op("tensor", lambda e: e.matmul(
                    rpB[:, 32:32 + NTILE], ones_r_f[:], etrow[:], start=True,
                    stop=True))
                em.op("tensor", lambda e: e.matmul(
                    rpB[:, 64:64 + NTILE], ones_r_f[:], srowBig[:], start=True,
                    stop=True), tick=True)
                em.wait("scalar", "tensor")
                em.op("scalar", lambda e: e.copy(ebs0[:], rpB[:, 32:32 + NTILE]))
                em.op("scalar", lambda e: e.copy(skipbc[:], rpB[:, 64:64 + NTILE]),
                      tick=True)
                em.wait("vector", "scalar")
                em.op("vector", lambda e: e.tensor_tensor(
                    ebs[:], ebs0[:], skipbc[:], ALU.add), tick=True)
                for k in range(4):
                    em.op("vector", lambda e, k=k: e.tensor_scalar(
                        widxQ[:, k, :], ebs[:], 512.0, rowoffq[:, k:k + 1],
                        ALU.mult, ALU.add), tick=(k == 3))
                em.op("vector", lambda e: e.tensor_scalar(
                    widxb1I[:], ebs0[0:64, :], 64.0, rowofff[0:64, :],
                    ALU.mult, ALU.add), tick=True)
                em.wait("gpsimd", "d:gsc")

            em.run("route", route)

        with ExitStack() as es:
            psb = lambda *a: es.enter_context(nc.psum_tensor(*a))
            lsb = lambda *a: es.enter_context(nc.sbuf_tensor(*a))
            xkb = [lsb(f"xkb{i}", [128, 8, 512], BF16) for i in range(2)]
            kvin = [lsb(f"kvin{i}", [128, 1024], F32) for i in range(2)]
            kvout = [lsb(f"kvout{i}", [128, 1024], BF16) for i in range(2)]
            kstg = [lsb(f"kstg{i}", [65, NH * 512], BF16) for i in range(2)]
            ktst = [lsb(f"ktst{i}", [128, 4, NH * 65], BF16) for i in range(2)]
            vtst = [lsb(f"vtst{i}", [128, 4, NH * 65], BF16) for i in range(2)]
            kfp = [psb(f"kfp{i}", [128, 512], F32) for i in range(2)]
            ktp = psb("ktp", [128, DM], F32)
            vtp = psb("vtp", [128, DM], F32)

            kv_chunks = []
            for r in range(32):
                for cc in range(8):
                    kv_chunks.append((w2, w2b, r * 128, cc * 1024))

            def kv(em):
                kst = dict(ck=0, pend=[])

                def kconv(n):
                    for kp in kst["pend"]:
                        _, dstP, rp, cp = kv_chunks[kp]
                        em.wait("vector", f"d:kvi{kp % 2}")
                        em.wait_snap("vector", f"d:kvo{kp % 2}",
                                     "kvo_sent", kp - 2)
                        em.op("vector", lambda e, kp=kp: e.tensor_copy(
                            kvout[kp % 2][:], kvin[kp % 2][:]), tick=True)
                        em.snap("kvc", "vector")
                        em.wait("gpsimd", "vector")
                        em.dma("gpsimd", f"kvo{kp % 2}",
                               lambda e, kp=kp, dstP=dstP, rp=rp, cp=cp:
                               e.dma_start(
                                   dstP[rp:rp + 128, cp:cp + 1024],
                                   kvout[kp % 2][:]))
                        em.snap("kvo_sent", f"d:kvo{kp % 2}")
                    kst["pend"] = []
                    for _ in range(n):
                        k = kst["ck"]
                        if k >= len(kv_chunks):
                            return
                        kst["ck"] += 1
                        srcT, dstT, r0, c0 = kv_chunks[k]
                        em.wait_snap("gpsimd", "vector", "kvc", k - 2)
                        em.dma("gpsimd", f"kvi{k % 2}",
                               lambda e, srcT=srcT, r0=r0, c0=c0, k=k:
                               e.dma_start(kvin[k % 2][:],
                                           srcT[r0:r0 + 128, c0:c0 + 1024]))
                        kst["pend"].append(k)
                for i in range(2):
                    em.op("gpsimd", lambda e, i=i: e.memset(
                        kstg[i][64:65, :], 1.0), tick=True)
                    em.op("gpsimd", lambda e, i=i: e.memset(
                        ktst[i][:].rearrange("p t (h o) -> p t h o", o=65)
                        [:, :, :, 64:65], 1.0), tick=True)
                    em.op("gpsimd", lambda e, i=i: e.memset(
                        vtst[i][:].rearrange("p t (h o) -> p t h o", o=65)
                        [:, :, :, 64:65], 1.0), tick=True)
                for u in range(NU):
                    xb = xkb[u % 2]
                    em.wait_snap("sync", "tensor", "xkb_free", u - 2)
                    em.dma("sync", f"xkb{u % 2}", lambda e, xb=xb, u=u: e.dma_start(
                        xb[:], xnf[:, :, u * 512:(u + 1) * 512]
                        .rearrange("c p t -> p c t")))
                    ks = kstg[u % 2]
                    em.wait("tensor", f"d:xkb{u % 2}")
                    for hp in range(8):
                        kconv(2)
                        kp = kfp[hp % 2]
                        em.wait_snap("tensor", "scalar", "kfp_evac", u * 8 + hp - 2)
                        for c in range(8):
                            em.op("tensor", lambda e, kp=kp, c=c, hp=hp, xb=xb: e.matmul(
                                kp[:], wkt[:, c, hp * 128:(hp + 1) * 128],
                                xb[:, c, :], start=(c == 0), stop=(c == 7)),
                                tick=(c == 7))
                        em.wait("scalar", "tensor")
                        if hp == 0:
                            em.wait("scalar", "gpsimd")
                            em.wait_snap("scalar", f"d:kst{u % 2}", "kfm_sent", u - 2)
                        em.op("scalar", lambda e, kp=kp, ks=ks, hp=hp: e.copy(
                            ks[0:64, (2 * hp) * 512:(2 * hp + 1) * 512], kp[0:64, :]))
                        em.op("scalar", lambda e, kp=kp, ks=ks, hp=hp: e.copy(
                            ks[0:64, (2 * hp + 1) * 512:(2 * hp + 2) * 512],
                            kp[64:128, :]), tick=True)
                        em.snap("kfp_evac", "scalar")
                    em.wait("sync", "scalar")
                    em.dma("sync", f"kst{u % 2}", lambda e, ks=ks, u=u: e.dma_start(kfm[u], ks[:]))
                    em.snap("kfm_sent", f"d:kst{u % 2}")
                    kt = ktst[u % 2]
                    for t in range(4):
                        em.wait_snap("tensor", "scalar", "ktp_evac", u * 4 + t - 1)
                        for c in range(8):
                            for hh in range(2):
                                em.op("tensor", lambda e, xb=xb, c=c, t=t, hh=hh: e.matmul(
                                    ktp[:, hh * 512:(hh + 1) * 512],
                                    xb[:, c, t * 128:(t + 1) * 128],
                                    wkt[:, c, hh * 512:(hh + 1) * 512],
                                    start=(c == 0), stop=(c == 7)),
                                    tick=(c == 7 and hh == 1))
                        em.wait("scalar", "tensor")
                        if t == 0:
                            em.wait_snap("scalar", f"d:ktt{u % 2}", "ktk_sent", u - 2)
                        em.op("scalar", lambda e, kt=kt, t=t: e.copy(
                            kt[:].rearrange("p t (h o) -> p t h o", o=65)
                            [:, t, :, 0:64],
                            ktp[:].rearrange("p (h d) -> p h d", d=64)), tick=True)
                        em.snap("ktp_evac", "scalar")
                    em.wait("sync", "scalar")
                    em.dma("sync", f"ktt{u % 2}", lambda e, kt=kt, u=u: e.dma_start(ktk[u], kt[:]))
                    em.snap("ktk_sent", f"d:ktt{u % 2}")
                    vt = vtst[u % 2]
                    for t in range(4):
                        em.wait_snap("tensor", "scalar", "vtp_evac", u * 4 + t - 1)
                        for c in range(8):
                            for hh in range(2):
                                em.op("tensor", lambda e, xb=xb, c=c, t=t, hh=hh: e.matmul(
                                    vtp[:, hh * 512:(hh + 1) * 512],
                                    xb[:, c, t * 128:(t + 1) * 128],
                                    wvt[:, c, hh * 512:(hh + 1) * 512],
                                    start=(c == 0), stop=(c == 7)),
                                    tick=(c == 7 and hh == 1))
                        em.wait("scalar", "tensor")
                        if t == 0:
                            em.wait_snap("scalar", f"d:vtt{u % 2}", "vtk_sent", u - 2)
                        em.op("scalar", lambda e, vt=vt, t=t: e.copy(
                            vt[:].rearrange("p t (h o) -> p t h o", o=65)
                            [:, t, :, 0:64],
                            vtp[:].rearrange("p (h d) -> p h d", d=64)), tick=True)
                        em.snap("vtp_evac", "scalar")
                    em.snap("xkb_free", "tensor")
                    em.wait("sync", "scalar")
                    em.dma("sync", f"vtt{u % 2}", lambda e, vt=vt, u=u: e.dma_start(vtk[u], vt[:]))
                    em.snap("vtk_sent", f"d:vtt{u % 2}")
                while kst["ck"] < len(kv_chunks):
                    kconv(2)
                for kp in list(kst["pend"]):
                    pass
                if True:
                    for kp in kst["pend"]:
                        _, dstP, rp, cp = kv_chunks[kp]
                        em.wait("vector", f"d:kvi{kp % 2}")
                        em.wait_snap("vector", f"d:kvo{kp % 2}",
                                     "kvo_sent", kp - 2)
                        em.op("vector", lambda e, kp=kp: e.tensor_copy(
                            kvout[kp % 2][:], kvin[kp % 2][:]), tick=True)
                        em.snap("kvc", "vector")
                        em.wait("gpsimd", "vector")
                        em.dma("gpsimd", f"kvo{kp % 2}",
                               lambda e, kp=kp, dstP=dstP, rp=rp, cp=cp:
                               e.dma_start(
                                   dstP[rp:rp + 128, cp:cp + 1024],
                                   kvout[kp % 2][:]))
                        em.snap("kvo_sent", f"d:kvo{kp % 2}")
                    kst["pend"] = []
                em.wait("gpsimd", "d:kvo0")
                em.wait("gpsimd", "d:kvo1")
                for s_ in ("kst0", "kst1", "ktt0", "ktt1", "vtt0", "vtt1"):
                    em.wait("sync", "d:" + s_)

            em.run("kv", kv)

        stackQ = ExitStack()
        qbuf = stackQ.enter_context(nc.sbuf_tensor("qbuf", [65, NH * TOK], BF16,
                                                   side="right"))
        with ExitStack() as es:
            psb = lambda *a: es.enter_context(nc.psum_tensor(*a))
            qps = [psb(f"qps{i}", [128, 512], F32) for i in range(2)]

            def attn_q(em):
                em.op("gpsimd", lambda e: e.memset(qbuf[64:65, :], 0.0),
                      tick=True)
                for hp in range(8):
                    for tc in range(4):
                        it = hp * 4 + tc
                        qp = qps[it % 2]
                        em.wait_snap("tensor", "scalar", "qp_evac", it - 2)
                        for c in range(8):
                            em.op("tensor", lambda e, qp=qp, c=c, hp=hp, tc=tc: e.matmul(
                                qp[:], wqt[:, c, hp * 128:(hp + 1) * 128],
                                xnown[:, c, tc * 512:(tc + 1) * 512],
                                start=(c == 0), stop=(c == 7)), tick=(c == 7))
                        em.wait("scalar", "tensor")
                        h0, h1 = 2 * hp, 2 * hp + 1
                        em.op("scalar", lambda e, qp=qp, h0=h0, tc=tc: e.activation(
                            qbuf[0:64, h0 * TOK + tc * 512:h0 * TOK + (tc + 1) * 512],
                            qp[0:64, :], AF.Copy, scale=0.125))
                        em.op("scalar", lambda e, qp=qp, h1=h1, tc=tc: e.activation(
                            qbuf[0:64, h1 * TOK + tc * 512:h1 * TOK + (tc + 1) * 512],
                            qp[64:128, :], AF.Copy, scale=0.125), tick=True)
                        em.snap("qp_evac", "scalar")

            em.run("attn_q", attn_q)
        stackA.close()

        stackC = ExitStack()
        ntbig = stackC.enter_context(nc.sbuf_tensor("ntbig", [128, 8 * TOK], BF16))
        with ExitStack() as es:
            psb = lambda *a: es.enter_context(nc.psum_tensor(*a))
            lsb = lambda *a: es.enter_context(nc.sbuf_tensor(*a))
            sps = [psb(f"s2ps{i}", [128, 2, 512], F32) for i in range(2)]
            ozp = [psb(f"ozp{i}", [65, 512], F32) for i in range(2)]
            fzp = [psb(f"fzp{i}", [65, 512], F32) for i in range(2)]
            urp = [p[64:65, :] for p in fzp]
            bcp = [p[0:64, :] for p in fzp]
            kb2 = [lsb(f"kb2{i}", [65, GH * 512], BF16) for i in range(2)]
            ntb_ = [lsb(f"ntb{i}", [128, 4, GH * 65], BF16) for i in range(2)]
            ttl = [lsb(f"ttl{i}", [128, 4, 512], BF16) for i in range(2)]
            uza = lsb("uza", [65, GH * TOK], F32)
            prod = lsb("prod", [64, TOK], BF16)
            rowA = lsb("rowA", [1, TOK], F32)
            betaro = lsb("betaro", [1, TOK], F32)
            cvin = [lsb(f"cvin{i}", [128, 1024], F32) for i in range(2)]
            cvout = [lsb(f"cvout{i}", [128, 1024], BF16) for i in range(2)]
            zpart = lsb("zpart", [128, 64], F32)
            upart = lsb("upart", [128, 64], F32)
            tA = lsb("tA", [128, 64], F32)
            tB = lsb("tB", [128, 64], F32)
            tC = lsb("tC", [128, 64], F32)

            def chunks_of(u):
                out, off, nq = [], u * 128, TOK - u * 128
                while nq > 0:
                    w = min(512, nq)
                    out.append((off, w))
                    off += w
                    nq -= w
                return out

            conv_chunks = []
            for r in range(32):
                for cc in range(16):
                    conv_chunks.append((w1, w1b, r * 128, cc * 1024))

            def attn_b(em):
                state = dict(nbc=0, nur=0, ck=0, pend_conv=[])

                def conv_step(n):
                    for kp in state["pend_conv"]:
                        srcP, dstP, rp, cp = conv_chunks[kp]
                        em.wait("vector", f"d:cvi{kp % 2}")
                        em.wait_snap("vector", f"d:cvo{kp % 2}",
                                     "cvo_sent", kp - 2)
                        em.op("vector", lambda e, kp=kp: e.tensor_copy(
                            cvout[kp % 2][:], cvin[kp % 2][:]), tick=True)
                        em.snap("cvc", "vector")
                        em.wait("gpsimd", "vector")
                        em.dma("gpsimd", f"cvo{kp % 2}",
                               lambda e, kp=kp, dstP=dstP, rp=rp, cp=cp:
                               e.dma_start(
                                   dstP[rp:rp + 128, cp:cp + 1024],
                                   cvout[kp % 2][:]))
                        em.snap("cvo_sent", f"d:cvo{kp % 2}")
                    state["pend_conv"] = []
                    for _ in range(n):
                        k = state["ck"]
                        if k >= len(conv_chunks):
                            return
                        state["ck"] += 1
                        srcT, dstT, r0, c0 = conv_chunks[k]
                        em.wait_snap("gpsimd", "vector", "cvc", k - 2)
                        em.dma("gpsimd", f"cvi{k % 2}",
                               lambda e, srcT=srcT, r0=r0, c0=c0, k=k:
                               e.dma_start(cvin[k % 2][:],
                                           srcT[r0:r0 + 128, c0:c0 + 1024]))
                        state["pend_conv"].append(k)

                def conv_last(em):
                    for kp in state["pend_conv"]:
                        srcP, dstP, rp, cp = conv_chunks[kp]
                        em.wait("vector", f"d:cvi{kp % 2}")
                        em.wait_snap("vector", f"d:cvo{kp % 2}",
                                     "cvo_sent", kp - 2)
                        em.op("vector", lambda e, kp=kp: e.tensor_copy(
                            cvout[kp % 2][:], cvin[kp % 2][:]), tick=True)
                        em.snap("cvc", "vector")
                        em.wait("gpsimd", "vector")
                        em.dma("gpsimd", f"cvo{kp % 2}",
                               lambda e, kp=kp, dstP=dstP, rp=rp, cp=cp:
                               e.dma_start(
                                   dstP[rp:rp + 128, cp:cp + 1024],
                                   cvout[kp % 2][:]))
                        em.snap("cvo_sent", f"d:cvo{kp % 2}")
                    state["pend_conv"] = []
                    em.wait("gpsimd", "d:cvo0")
                    em.wait("gpsimd", "d:cvo1")

                def sweep(g, which, sw):
                    em.wait("vector", "scalar")
                    em.wait("vector", "d:fin")
                    em.wait("sync", "tensor")
                    st = dict(nt=0, nsp=0, nd=0, pend=None)

                    def flush():
                        if st["pend"] is None:
                            return
                        tb, ntb, hl, qoff, qw, expidx, i, dmidx = st["pend"]
                        st["pend"] = None
                        oz = ozp[i % 2]
                        em.wait_snap("tensor", "vector", f"oz_{sw}", i - 2)
                        em.wait_snap("tensor", "scalar", f"exp_{sw}", expidx)
                        if dmidx >= 0:
                            em.wait_snap("tensor", "vector", f"dm_{sw}", dmidx)
                        for t in range(4):
                            em.op("tensor", lambda e, oz=oz, ntb=ntb, tb=tb,
                                  t=t, hl=hl, qw=qw: e.matmul(
                                      oz[:, 0:qw],
                                      ntb[:, t, hl * 65:(hl + 1) * 65],
                                      tb[:, t, 0:qw],
                                      start=(t == 0), stop=(t == 3)),
                                  tick=(t == 3))
                        em.snap(f"ozmm_{sw}", "tensor")
                        em.wait("vector", "tensor")
                        if i < GH * 4:
                            em.op("vector", lambda e, oz=oz, hl=hl, qoff=qoff,
                                  qw=qw: e.tensor_copy(
                                      uza[0:65, hl * TOK + qoff:hl * TOK + qoff + qw],
                                      oz[:, 0:qw]), tick=True)
                        else:
                            em.op("vector", lambda e, oz=oz, hl=hl, qoff=qoff,
                                  qw=qw: e.tensor_tensor(
                                      uza[0:65, hl * TOK + qoff:hl * TOK + qoff + qw],
                                      uza[0:65, hl * TOK + qoff:hl * TOK + qoff + qw],
                                      oz[:, 0:qw], ALU.add), tick=True)
                        em.snap(f"oz_{sw}", "vector")

                    for u in range(NU):
                        kb = kb2[u % 2]
                        src = ktk if which == "uz" else vtk
                        ntb = ntb_[u % 2]
                        em.wait_snap("sync", "tensor", f"kb2f_{sw}", u - 2)
                        em.dma("sync", f"kbn{u % 2}", lambda e, kb=kb, u=u, g=g: e.dma_start(
                            kb[:], kfm[u].rearrange("o (h t) -> o h t", t=512)
                            [:, 4 * g:4 * g + 4, :]))
                        em.dma("sync", f"kbn{u % 2}", lambda e, ntb=ntb, u=u, g=g, src=src: e.dma_start(
                            ntb[:], src[u].rearrange("p t (h o) -> p t h o", o=65)
                            [:, :, 4 * g:4 * g + 4, :]))
                        em.wait("tensor", f"d:kbn{u % 2}")
                        for hl in range(GH):
                            conv_step(1)
                            h = 4 * g + hl
                            for ci, (qoff, qw) in enumerate(chunks_of(u)):
                                nt = st["nt"]
                                tb = ttl[nt % 2]
                                for tp in range(2):
                                    nsp = st["nsp"]
                                    sp = sps[nsp % 2]
                                    em.wait_snap("tensor", "scalar", f"exp_{sw}",
                                                 nsp - 2)
                                    for tt in range(2):
                                        t = 2 * tp + tt
                                        em.op("tensor", lambda e, sp=sp, kb=kb,
                                              hl=hl, t=t, tt=tt, qoff=qoff, qw=qw,
                                              h=h: e.matmul(
                                                  sp[:, tt, 0:qw],
                                                  kb[0:65, hl * 512 + t * 128:
                                                     hl * 512 + (t + 1) * 128],
                                                  qbuf[0:65, h * TOK + qoff:
                                                       h * TOK + qoff + qw],
                                                  start=True, stop=True),
                                              tick=(tt == 1))
                                    if tp == 0:
                                        em.wait_snap("scalar", "tensor",
                                                     f"ozmm_{sw}", nt - 2)
                                    em.wait("scalar", "tensor")
                                    em.op("scalar", lambda e, sp=sp, tb=tb, tp=tp,
                                          qw=qw: e.activation(
                                              tb[:, 2 * tp:2 * tp + 2, 0:qw],
                                              sp[:, :, 0:qw], AF.Exp),
                                          tick=True)
                                    em.snap(f"exp_{sw}", "scalar")
                                    st["nsp"] += 1
                                    if ci == 0:
                                        em.wait("vector", "scalar")
                                        for tt in range(2):
                                            t = 2 * tp + tt
                                            em.op("vector", lambda e, tb=tb, t=t:
                                                  e.tensor_tensor(
                                                      tb[:, t, 0:128],
                                                      tb[:, t, 0:128],
                                                      m01[:, t * 128:(t + 1) * 128],
                                                      ALU.mult), tick=(tt == 1))
                                        if tp == 1:
                                            em.snap(f"dm_{sw}", "vector")
                                dmidx = -1
                                if ci == 0:
                                    dmidx = st["nd"]
                                    st["nd"] += 1
                                flush()
                                st["pend"] = (tb, ntb, hl, qoff, qw,
                                              st["nsp"] - 1, nt, dmidx)
                                st["nt"] += 1
                        flush()
                        em.snap(f"kb2f_{sw}", "tensor")

                for g in range(NG):
                    sweep(g, "uz", 2 * g)
                    for hl in range(GH):
                        h = 4 * g + hl
                        hc = hl * TOK
                        em.wait("vector", "tensor")
                        em.op("vector", lambda e, h=h, hc=hc: e.tensor_tensor(
                            prod[:], qbuf[0:64, h * TOK:(h + 1) * TOK],
                            uza[0:64, hc:hc + TOK], ALU.mult), tick=True)
                        for c in range(4):
                            ur = urp[c % 2]
                            em.wait("tensor", "vector")
                            em.wait_snap("tensor", "scalar", "ur_evac",
                                         state["nur"] - 2)
                            em.op("tensor", lambda e, ur=ur, c=c: e.matmul(
                                ur[:], ones_c_b[:], prod[0:64, c * 512:(c + 1) * 512],
                                start=True, stop=True), tick=True)
                            em.wait("scalar", "tensor")
                            em.op("scalar", lambda e, ur=ur, c=c: e.copy(
                                rowA[:, c * 512:(c + 1) * 512],
                                ur[:]), tick=True)
                            em.snap("ur_evac", "scalar")
                            state["nur"] += 1
                        em.wait("gpsimd", "scalar")
                        em.dma("gpsimd", "fin", lambda e, hl=hl: e.dma_start(
                            upart[hl * 32:(hl + 1) * 32, :], rowA[:]))
                    em.wait("gpsimd", "vector")
                    em.dma("gpsimd", "fin", lambda e: e.dma_start(
                        zpart[:], uza[64:65, :]))
                    em.wait("vector", "d:fin")
                    em.wait("scalar", "d:fin")
                    em.op("vector", lambda e: e.reciprocal(tA[:], zpart[:]),
                          tick=True)
                    em.op("scalar", lambda e: e.activation(
                        tC[:], zpart[:], AF.Ln), tick=True)
                    em.wait("vector", "scalar")
                    em.op("vector", lambda e: e.tensor_tensor(
                        tA[:], upart[:], tA[:], ALU.mult))
                    em.op("vector", lambda e: e.tensor_tensor(
                        tA[:], tC[:], tA[:], ALU.subtract))
                    em.op("vector", lambda e: e.tensor_tensor(
                        tB[:], tA[:], tA[:], ALU.mult))
                    em.op("vector", lambda e: e.tensor_scalar(
                        tC[:], tB[:], POLY[0], POLY[2], ALU.mult, ALU.add))
                    em.op("vector", lambda e: e.tensor_tensor(
                        tC[:], tC[:], tB[:], ALU.mult))
                    em.op("vector", lambda e: e.tensor_scalar(
                        tB[:], tB[:], POLY[1], POLY[3], ALU.mult, ALU.add))
                    em.op("vector", lambda e: e.tensor_tensor(
                        tB[:], tB[:], tA[:], ALU.mult))
                    em.op("vector", lambda e: e.tensor_tensor(
                        tC[:], tC[:], tB[:], ALU.add))
                    em.op("vector", lambda e: e.tensor_scalar(
                        tC[:], tC[:], POLY[4], 1.0, ALU.add, ALU.max))
                    em.op("vector", lambda e: e.tensor_scalar(
                        tB[:], tA[:], 0.5, None, ALU.is_gt, ALU.bypass))
                    em.op("vector", lambda e: e.tensor_scalar(
                        tC[:], tC[:], -1.0, None, ALU.add, ALU.bypass))
                    em.op("vector", lambda e: e.tensor_tensor(
                        tC[:], tB[:], tC[:], ALU.mult))
                    em.op("vector", lambda e: e.tensor_scalar(
                        tC[:], tC[:], 1.0, None, ALU.add, ALU.bypass), tick=True)
                    for hl in range(GH):
                        h = 4 * g + hl
                        em.wait("gpsimd", "vector")
                        em.wait("gpsimd", "tensor")
                        em.dma("gpsimd", "finb", lambda e, hl=hl: e.dma_start(
                            betaro[:], tC[hl * 32:(hl + 1) * 32, :]))
                        em.wait("tensor", "d:finb")
                        for c in range(4):
                            bc = bcp[c % 2]
                            em.wait_snap("tensor", "vector", "bcu", state["nbc"] - 2)
                            em.op("tensor", lambda e, bc=bc, c=c: e.matmul(
                                bc[:], ones_r_f[0:1, 0:64],
                                betaro[:, c * 512:(c + 1) * 512],
                                start=True, stop=True), tick=True)
                            em.wait("vector", "tensor")
                            em.op("vector", lambda e, bc=bc, h=h, c=c: e.tensor_tensor(
                                qbuf[0:64, h * TOK + c * 512:h * TOK + (c + 1) * 512],
                                qbuf[0:64, h * TOK + c * 512:h * TOK + (c + 1) * 512],
                                bc[:], ALU.mult), tick=True)
                            em.snap("bcu", "vector")
                            state["nbc"] += 1
                    em.wait("tensor", "vector")
                    sweep(g, "av", 2 * g + 1)
                    em.wait("gpsimd", "vector")
                    em.dma("gpsimd", "fin", lambda e: e.dma_start(
                        zpart[:], uza[64:65, :]))
                    em.wait("vector", "d:fin")
                    em.op("vector", lambda e: e.reciprocal(tA[:], zpart[:]),
                          tick=True)
                    for hl in range(GH):
                        h = 4 * g + hl
                        hc = hl * TOK
                        em.wait("gpsimd", "vector")
                        em.wait("gpsimd", "tensor")
                        em.dma("gpsimd", "finb", lambda e, hl=hl: e.dma_start(
                            betaro[:], tA[hl * 32:(hl + 1) * 32, :]))
                        em.wait("tensor", "d:finb")
                        for c in range(4):
                            bc = bcp[c % 2]
                            em.wait_snap("tensor", "vector", "bcu", state["nbc"] - 2)
                            em.op("tensor", lambda e, bc=bc, c=c: e.matmul(
                                bc[:], ones_r_f[0:1, 0:64],
                                betaro[:, c * 512:(c + 1) * 512],
                                start=True, stop=True), tick=True)
                            em.wait("vector", "tensor")
                            em.op("vector", lambda e, bc=bc, h=h, hc=hc, c=c: e.tensor_tensor(
                                ntbig[(h % 2) * 64:(h % 2) * 64 + 64,
                                      (h // 2) * TOK + c * 512:(h // 2) * TOK + (c + 1) * 512],
                                uza[0:64, hc + c * 512:hc + (c + 1) * 512],
                                bc[:], ALU.mult), tick=True)
                            em.snap("bcu", "vector")
                            state["nbc"] += 1
                while state["ck"] < len(conv_chunks):
                    conv_step(2)
                conv_last(em)

            em.run("attn_b", attn_b)
        stackQ.close()
        stackAcc = ExitStack()
        acctok = stackAcc.enter_context(
            nc.sbuf_tensor("acctok", [128, NSLOT, DM], BF16, side="right"))

        with ExitStack() as es:
            psb = lambda *a: es.enter_context(nc.psum_tensor(*a))
            lsb = lambda *a: es.enter_context(nc.sbuf_tensor(*a))
            wps = [psb(f"wps{i}", [128, 512], F32) for i in range(2)]
            atp = [psb(f"atp{i}", [128, 512], F32) for i in range(2)]
            afs = [lsb(f"afs{i}", [128, 512], F32) for i in range(2)]

            def wo_phase(em):
                for dc in range(8):
                    for tc in range(4):
                        it = dc * 4 + tc
                        wp = wps[it % 2]
                        em.wait_snap("tensor", "scalar", "wp_evac", it - 2)
                        for pc in range(8):
                            em.op("tensor", lambda e, wp=wp, pc=pc, dc=dc, tc=tc: e.matmul(
                                wp[:], wot[:, pc, dc * 128:(dc + 1) * 128],
                                ntbig[:, pc * TOK + tc * 512:pc * TOK + (tc + 1) * 512],
                                start=(pc == 0), stop=(pc == 7)), tick=(pc == 7))
                        af = afs[it % 2]
                        em.wait("scalar", "tensor")
                        em.wait_snap("scalar", "tensor", "afs_used", it - 2)
                        em.op("scalar", lambda e, wp=wp, af=af, dc=dc: e.activation(
                            af[:], wp[:], AF.Identity, bias=bot[:, dc:dc + 1]),
                            tick=True)
                        em.snap("wp_evac", "scalar")
                        at = atp[it % 2]
                        em.wait("tensor", "scalar")
                        em.wait_snap("tensor", "scalar", "atp_evac", it - 2)
                        for s in range(4):
                            em.op("tensor", lambda e, at=at, af=af, s=s: e.transpose(
                                at[:, s * 128:(s + 1) * 128],
                                af[:, s * 128:(s + 1) * 128], idf[:]), tick=(s == 3))
                        em.snap("afs_used", "tensor")
                        em.wait("scalar", "tensor")
                        em.op("scalar", lambda e, at=at, dc=dc, tc=tc: e.copy(
                            acctok[:, tc * 4:tc * 4 + 4, dc * 128:(dc + 1) * 128],
                            at[:].rearrange("p (t q) -> p t q", t=4)), tick=True)
                        em.snap("atp_evac", "scalar")

            em.run("wo", wo_phase)
        stackC.close()
        stackW.close()

        with ExitStack() as es:
            psb = lambda *a: es.enter_context(nc.psum_tensor(*a))
            lsb = lambda *a: es.enter_context(nc.sbuf_tensor(*a))
            w1g = [lsb(f"w1g{i}", [128, 8, 2048], BF16) for i in range(2)]
            w2g = [lsb(f"w2g{i}", [128, 8, 1024], BF16) for i in range(2)]
            xgr = [lsb(f"xgr{i}", [128, DM], BF16) for i in range(2)]
            xnt = [lsb(f"xnt{i}", [128, 8, 128], BF16) for i in range(2)]
            b1r = [lsb(f"b1raw{i}", [64, 128], F32) for i in range(2)]
            b1tA = lsb("b1tA", [128, NTILE, 64], F32)
            gmt = [lsb(f"gmt{i}", [128, 1], I32) for i in range(2)]
            ysacc = lsb("ysacc", [128, NTILE, DM], BF16)
            sgb = [lsb(f"sgb{i}", [128, 128], BF16) for i in range(2)]
            ubf = [lsb(f"ubf{i}", [128, 128], BF16) for i in range(2)]
            yps = [psb(f"yps{i}", [128, DM], F32) for i in range(2)]
            agp = [psb(f"agp{i}", [128, 256], F32) for i in range(2)]
            xtp = [psb(f"xtp{i}", [128, 128], BF16) for i in range(2)]

            regs = {}

            def ffn(em):
                if em.mode == "gpsimd" and "b" not in regs:
                    regs["b"] = em.eng.to_reg(32767)
                state = dict(nhc=0)
                em.dma("gpsimd", "gmt0", lambda e: e.dma_start(
                    gmt[0][:], gmap[0:128, :]))
                for qt in range(4):
                    for t in range(NTILE):
                        nq = qt * NTILE + t
                        par = t % 2
                        em.wait("gpsimd", f"d:gmt{par}")
                        em.wait_snap("gpsimd", "tensor", "xgr_used", nq - 2)
                        em.dma("gpsimd", f"xg{par}", lambda e, par=par: e.indirect_dma_start(
                            xgr[par][:], None, xtk,
                            bass.IndirectOffsetOnAxis(gmt[par][:], 0),
                            bounds_check=regs["b"], oob_is_err=False))
                        if qt == 0:
                            em.wait_snap("gpsimd", "tensor", "b1r_used", t - 2)
                            em.dma("gpsimd", f"b1g{par}", lambda e, par=par, t=t: e.indirect_dma_start(
                                b1r[par][:], None, b1flat,
                                bass.IndirectOffsetOnAxis(widxb1I[:, t:t + 1], 0)))
                        em.wait_snap("gpsimd", "tensor", "wg_free", nq - 2)
                        em.dma("gpsimd", f"wg{par}",
                               lambda e, par=par, t=t, qt=qt: e.indirect_dma_start(
                                   w1g[par][:].rearrange("p c j -> p (c j)"),
                                   None, w1b,
                                   bass.IndirectOffsetOnAxis(
                                       widxQ[:, qt, t:t + 1], 0),
                                   bounds_check=regs["b"], oob_is_err=False))
                        em.dma("gpsimd", f"wg{par}",
                               lambda e, par=par, t=t, qt=qt: e.indirect_dma_start(
                                   w2g[par][:].rearrange("p c j -> p (c j)"),
                                   None, w2b,
                                   bass.IndirectOffsetOnAxis(
                                       widxQ[:, qt, t:t + 1], 0),
                                   bounds_check=regs["b"], oob_is_err=False))
                        if nq + 1 < 4 * NTILE:
                            tn = (t + 1) % NTILE
                            em.dma("gpsimd", f"gmt{(t + 1) % 2}",
                                   lambda e, tn=tn: e.dma_start(
                                       gmt[tn % 2][:],
                                       gmap[tn * 128:(tn + 1) * 128, :]))
                        em.wait("tensor", f"d:xg{par}")
                        for c in range(8):
                            nxt = nq * 8 + c
                            em.wait_snap("tensor", "scalar", "xtp_evac", nxt - 2)
                            em.op("tensor", lambda e, c=c, par=par: e.transpose(
                                xtp[c % 2][:], xgr[par][:, c * 128:(c + 1) * 128],
                                idb[:]), tick=True)
                            em.wait("scalar", "tensor")
                            if c == 0:
                                em.wait_snap("scalar", "tensor", "xnt_used",
                                             nq - 2)
                            em.op("scalar", lambda e, c=c, par=par: e.copy(
                                xnt[par][:, c, :], xtp[c % 2][:]), tick=True)
                            em.snap("xtp_evac", "scalar")
                        em.snap("xgr_used", "tensor")
                        if qt == 0:
                            em.wait("tensor", f"d:b1g{par}")
                            em.wait_snap("tensor", "vector", "ag_free",
                                         state["nhc"] - 1)
                            em.op("tensor", lambda e, par=par: e.transpose(
                                agp[par][:, 0:64], b1r[par][:], idf[0:64, 0:64]),
                                tick=True)
                            em.snap("b1r_used", "tensor")
                            em.wait("scalar", "tensor")
                            em.op("scalar", lambda e, par=par, t=t: e.copy(
                                b1tA[:, t, :], agp[par][:, 0:64]), tick=True)
                        em.wait("tensor", f"d:wg{par}")
                        em.wait("tensor", "scalar")

                        def emit_y(hc):
                            ubp = ubf[hc % 2]
                            if hc == 0:
                                em.wait_snap("tensor", "vector", "yps_free",
                                             nq - 2)
                            em.wait("tensor", "vector")
                            for half in range(2):
                                em.op("tensor", lambda e, half=half, ubp=ubp,
                                      par=par, hc=hc: e.matmul(
                                          yps[par][:, half * 512:(half + 1) * 512],
                                          ubp[:], w2g[par][:, hc, half * 512:
                                                           (half + 1) * 512],
                                          start=(hc == 0), stop=(hc == 7)),
                                      tick=(half == 1))
                            em.snap("ub_used", "tensor")

                        for hc in range(8):
                            nhc = state["nhc"]
                            agq = agp[nhc % 2]
                            em.wait_snap("tensor", "vector", "ag_free", nhc - 2)
                            for c in range(8):
                                em.op("tensor", lambda e, agq=agq, par=par, c=c,
                                      hc=hc: e.matmul(
                                          agq[:, 0:128],
                                          w1g[par][:, c, hc * 128:(hc + 1) * 128],
                                          xnt[par][:, c, :],
                                          start=(c == 0), stop=(c == 7)),
                                      tick=(c == 7))
                            for c in range(8):
                                em.op("tensor", lambda e, agq=agq, par=par, c=c,
                                      hc=hc: e.matmul(
                                          agq[:, 128:256],
                                          w1g[par][:, c, 1024 + hc * 128:
                                                   1024 + (hc + 1) * 128],
                                          xnt[par][:, c, :],
                                          start=(c == 0), stop=(c == 7)),
                                      tick=(c == 7))
                            sg = sgb[nhc % 2]
                            ub = ubf[nhc % 2]
                            em.wait("scalar", "tensor")
                            em.wait_snap("scalar", "vector", "sg_used", nhc - 2)
                            em.op("scalar", lambda e, agq=agq, sg=sg, qt=qt,
                                  hc=hc, t=t: e.activation(
                                      sg[:], agq[:, 128:256], AF.Sigmoid,
                                      bias=b1tA[:, t, 32 + qt * 8 + hc:
                                                33 + qt * 8 + hc]),
                                  tick=True)
                            if hc > 0:
                                emit_y(hc - 1)
                            em.wait("vector", "scalar")
                            em.wait_snap("vector", "tensor", "ub_used", nhc - 2)
                            em.op("vector", lambda e, agq=agq, sg=sg, ub=ub,
                                  qt=qt, hc=hc, t=t: e.scalar_tensor_tensor(
                                      ub[:], agq[:, 0:128],
                                      b1tA[:, t, qt * 8 + hc:qt * 8 + hc + 1],
                                      sg[:], ALU.add, ALU.mult), tick=True)
                            em.snap("ag_free", "vector")
                            em.snap("sg_used", "vector")
                            state["nhc"] += 1
                        emit_y(7)
                        em.snap("wg_free", "tensor")
                        if qt == 3:
                            em.snap("xnt_used", "tensor")
                        else:
                            em.snap("xnt_used", "tensor")
                        em.wait("vector", "tensor")
                        if qt == 0:
                            em.op("vector", lambda e, par=par, t=t: e.tensor_copy(
                                ysacc[:, t, :], yps[par][:]), tick=True)
                        else:
                            em.op("vector", lambda e, par=par, t=t: e.tensor_tensor(
                                ysacc[:, t, :], ysacc[:, t, :], yps[par][:],
                                ALU.add), tick=True)
                        em.snap("yps_free", "vector")
                em.wait("gpsimd", "vector")
                for t in range(NTILE):
                    par = t % 2
                    em.dma("gpsimd", f"gmt{par}", lambda e, par=par, t=t: e.dma_start(
                        gmt[par][:], gmap[t * 128:(t + 1) * 128, :]))
                    em.wait("gpsimd", f"d:gmt{par}")
                    em.dma("gpsimd", f"ys{par}", lambda e, par=par, t=t: e.indirect_dma_start(
                        ytok, bass.IndirectOffsetOnAxis(gmt[par][:], 0),
                        ysacc[:, t, :], None,
                        bounds_check=regs["b"], oob_is_err=False))
                em.wait("gpsimd", "d:ys0")
                em.wait("gpsimd", "d:ys1")

            em.run("ffn", ffn)

        with ExitStack() as es:
            psb = lambda *a: es.enter_context(nc.psum_tensor(*a))
            lsb = lambda *a: es.enter_context(nc.sbuf_tensor(*a))
            xo = [lsb(f"xo{i}", [128, DM], F32) for i in range(2)]
            ytb = [lsb(f"ytb{i}", [128, DM], BF16) for i in range(2)]
            mTs = [lsb(f"mTs{i}", [8, 128], F32) for i in range(2)]
            ob = [lsb(f"ob{i}", [128, DM], F32) for i in range(2)]
            b2S = lsb("b2S", [8, DM], F32)
            b2ps = [psb(f"b2ps{i}", [128, DM], F32) for i in range(2)]
            mtp2 = psb("mtp2", [8, 128], F32)

            def out_phase(em):
                em.dma("sync", "b2l", lambda e: e.dma_start(b2S[:], b2))
                for i in range(NSLOT):
                    xb = xo[i % 2]
                    yb = ytb[i % 2]
                    em.wait_snap("sync", "vector", "xo_free", i - 2)
                    em.dma("sync", f"xo{i % 2}", lambda e, xb=xb, i=i: e.dma_start(
                        xb[:], xe[(4 * i) * 128:(4 * i + 1) * 128, :]))
                    em.dma("sync", f"xo{i % 2}", lambda e, yb=yb, i=i: e.dma_start(
                        yb[:], ytok[i * 128:(i + 1) * 128, :]))
                    em.wait_snap("tensor", "scalar", "mts_evac", i - 1)
                    em.op("tensor", lambda e, i=i: e.transpose(
                        mtp2[:], maskS[:].rearrange("p (e i) -> p i e", i=NSLOT)
                        [:, i, :], idf[:]), tick=True)
                    em.wait("scalar", "tensor")
                    em.op("scalar", lambda e, i=i: e.copy(
                        mTs[i % 2][:], mtp2[:]), tick=True)
                    em.snap("mts_evac", "scalar")
                    em.wait("tensor", "scalar")
                    if i == 0:
                        em.wait("tensor", "d:b2l")
                    em.wait_snap("tensor", "vector", "b2ps_used", i - 2)
                    for half in range(2):
                        em.op("tensor", lambda e, i=i, half=half: e.matmul(
                            b2ps[i % 2][:, half * 512:(half + 1) * 512],
                            mTs[i % 2][:], b2S[:, half * 512:(half + 1) * 512],
                            start=True, stop=True), tick=(half == 1))
                    o = ob[i % 2]
                    em.wait("vector", f"d:xo{i % 2}")
                    em.wait("vector", "tensor")
                    em.wait_snap("vector", f"d:yst{i % 2}", "ob_sent", i - 2)
                    em.op("vector", lambda e, xb=xb, o=o, i=i: e.tensor_tensor(
                        o[:], xb[:], acctok[:, i, :], ALU.add), tick=True)
                    em.op("vector", lambda e, o=o, yb=yb: e.tensor_tensor(
                        o[:], o[:], yb[:], ALU.add), tick=True)
                    em.op("vector", lambda e, o=o, i=i: e.tensor_tensor(
                        o[:], o[:], b2ps[i % 2][:], ALU.add), tick=True)
                    em.snap("xo_free", "vector")
                    em.snap("b2ps_used", "vector")
                    em.wait("sync", "vector")
                    em.dma("sync", f"yst{i % 2}", lambda e, o=o, i=i: e.dma_start(
                        y[i * 128:(i + 1) * 128, :], o[:]))
                    em.snap("ob_sent", f"d:yst{i % 2}")
                em.wait("sync", "d:yst0")
                em.wait("sync", "d:yst1")

            em.run("out", out_phase)
        stackAcc.close()

    return nc


def _perm_and_masks(j):
    sig = [j] + [m for m in range(4) if m != j]
    tile_order = []
    for u in range(16):
        for r in range(4):
            tile_order.append(4 * u + sig[r])
    m1 = np.zeros((128, 512), np.float32)
    qi = np.arange(128)[:, None]
    ki = np.arange(128)[None, :]
    for r in range(4):
        if r == 0:
            m1[:, 0:128] = np.where(ki > qi, NEG, 0.0)
        elif sig[r] > j:
            m1[:, r * 128:(r + 1) * 128] = NEG
    m2 = np.zeros((128, 512), np.float32)
    for t in range(4):
        m2[:, t * 128:(t + 1) * 128] = m1[:, t * 128:(t + 1) * 128].T
    return tile_order, m1, m2


def kernel(L0_attn_x, L0_attn_Wq, L0_attn_Wk, L0_attn_Wv, L0_attn_Wo,
           L0_attn_bo, L0_ffn_router_w, L0_ffn_router_b, L0_ffn_w1,
           L0_ffn_b1, L0_ffn_w2, L0_ffn_b2):
    x = np.ascontiguousarray(np.asarray(L0_attn_x, dtype=np.float32))
    args = dict(
        wq=np.ascontiguousarray(np.asarray(L0_attn_Wq, np.float32)),
        wk=np.ascontiguousarray(np.asarray(L0_attn_Wk, np.float32)),
        wv=np.ascontiguousarray(np.asarray(L0_attn_Wv, np.float32)),
        wo=np.ascontiguousarray(np.asarray(L0_attn_Wo, np.float32)),
        bo=np.ascontiguousarray(np.asarray(L0_attn_bo, np.float32)),
        rw=np.ascontiguousarray(np.asarray(L0_ffn_router_w, np.float32)),
        rb=np.ascontiguousarray(np.asarray(L0_ffn_router_b, np.float32)),
        w1=np.ascontiguousarray(
            np.asarray(L0_ffn_w1, np.float32).reshape(8, 8, 128, 2, 4, 1024)
            .transpose(0, 4, 2, 1, 3, 5).reshape(4096, 16384)),
        b1=np.ascontiguousarray(np.asarray(L0_ffn_b1, np.float32)),
        w2=np.ascontiguousarray(
            np.asarray(L0_ffn_w2, np.float32).reshape(8, 4, 8, 128, 1024)
            .transpose(0, 1, 3, 2, 4).reshape(4096, 8192)),
        b2=np.ascontiguousarray(np.asarray(L0_ffn_b2, np.float32)),
    )
    nc = build_nc()
    in_maps = []
    metas = []
    for c in range(8):
        b, j = c // 4, c % 4
        order, m1, m2 = _perm_and_masks(j)
        row_idx = np.concatenate(
            [np.arange(t * 128, (t + 1) * 128) for t in order])
        im = dict(args)
        im["xe"] = np.ascontiguousarray(x[b][row_idx])
        im["m1"] = m1
        im["m2"] = m2
        in_maps.append(im)
        metas.append((b, j))
    res = run_bass_kernel_spmd(nc, in_maps, core_ids=list(range(8)))
    out = np.empty((2, 8192, 1024), np.float32)
    for c in range(8):
        b, j = metas[c]
        yc = res.results[c]["y"]
        for i in range(NSLOT):
            t = 4 * i + j
            out[b, t * 128:(t + 1) * 128, :] = yc[i * 128:(i + 1) * 128, :]
    return out
